# version 06
# baseline (speedup 1.0000x reference)
import numpy as np
import ml_dtypes
from contextlib import ExitStack

import concourse.bass as bass
import concourse.mybir as mybir
from concourse.bass_utils import run_bass_kernel_spmd

F32 = mybir.dt.float32
BF16 = mybir.dt.bfloat16
I32 = mybir.dt.int32
AF = mybir.ActivationFunctionType
ALU = mybir.AluOpType
AX = mybir.AxisListType

DM = 4096
NDC = 32
TOK = 1024
NQT = 8
NTS = 16
SEQ = 2048
QL, NQC = 1536, 12
QF, NQF = 6144, 48
KVL = 512
NH, GH = 32, 4
DFF, NFB = 11008, 86
NHC = 32
SQ = 1.0 / np.sqrt(192.0)
EPS = 1e-5

ENGS = ("sync", "scalar", "vector", "tensor", "gpsimd")
ESEM = {"scalar": "cA", "vector": "cV", "tensor": "cP", "gpsimd": "cG"}


class Sched:
    def __init__(self, nc, ctx):
        self.nc = nc
        self.ctx = ctx
        self.q = {e: [] for e in ENGS}
        self.sem_h = {}
        self.sem_n = {}
        self.waited = {e: {} for e in ENGS}
        self.nbar = 0
        self.phase = "init"

    def sem(self, *names):
        for name in names:
            self.sem_h[name] = self.ctx.enter_context(self.nc.semaphore(name))
            self.sem_n[name] = 0

    def step(self, eng, fn, waits=(), inc=None, inc_by=1):
        own = ESEM.get(eng)
        wl = []
        for w in waits:
            if w is None:
                continue
            sn, v = w
            if sn == own:
                assert v <= self.sem_n[sn], f"self-wait ahead on {sn}: {v} > {self.sem_n[sn]}"
            prev = self.waited[eng].get(sn, 0)
            if v > prev:
                wl.append((sn, v))
                self.waited[eng][sn] = v
        tok = None
        if inc is not None:
            self.sem_n[inc] += inc_by
            tok = (inc, self.sem_n[inc])
        self.q[eng].append((fn, wl, inc, inc_by, self.phase))
        return tok

    def barrier(self):
        self.nbar += 1
        target = 5 * self.nbar
        for e in ENGS:
            self.sem_n["bar"] += 1
            self.q[e].append((lambda eng: eng.nop(), [], "bar", 1, self.phase))
        for e in ENGS:
            self.q[e].append((None, [("bar", target)], None, 0, self.phase))
            self.waited[e]["bar"] = target

    def build(self, block):
        for e in ENGS:
            ops = self.q[e]

            def mk(ops):
                def body(eng):
                    cur = None
                    sid = None
                    for fn, wl, inc, inc_by, ph in ops:
                        if ph != cur:
                            if cur is not None:
                                self.nc.leave_named_scope(cur, sid, False)
                            sid, _ = self.nc.enter_named_scope(ph, False)
                            cur = ph
                        for sn, v in wl:
                            eng.wait_ge(self.sem_h[sn], v)
                        if fn is None:
                            continue
                        inst = fn(eng)
                        if inc is not None:
                            inst.then_inc(self.sem_h[inc], inc_by)
                    if cur is not None:
                        self.nc.leave_named_scope(cur, sid, False)
                return body
            getattr(block, e)(mk(ops))


def build_nc(debug=False):
    nc = bass.Bass("TRN2", target_bir_lowering=False)
    x_seq = nc.dram_tensor("x_seq", [SEQ, DM], F32, kind="ExternalInput").ap()
    wqc = nc.dram_tensor("wqc", [DM, QL], F32, kind="ExternalInput").ap()
    wqup = nc.dram_tensor("wqup", [QL, QF], F32, kind="ExternalInput").ap()
    wkvc = nc.dram_tensor("wkvc", [DM, 576], F32, kind="ExternalInput").ap()
    wkvup = nc.dram_tensor("wkvup", [KVL, 8192], F32, kind="ExternalInput").ap()
    wo = nc.dram_tensor("wo", [128, DM], F32, kind="ExternalInput").ap()
    c_in = {}
    for name, sh in [("bqc", [128, 12]), ("bqup", [128, 48]), ("bkvc", [128, 5]),
                     ("bkvupk", [128, 32]), ("bkvupv", [1, 4096]), ("bo", [128, 32]),
                     ("gs", [128, 86]), ("us", [128, 86]), ("ds", [128, 32]),
                     ("identb", [128, 128]), ("identf", [128, 128])]:
        dt = BF16 if name == "identb" else F32
        c_in[name] = nc.dram_tensor(name, sh, dt, kind="ExternalInput").ap()
    wg = nc.dram_tensor("wg", [NFB, 128, NDC * 128], I32, kind="ExternalInput").ap()
    wu = nc.dram_tensor("wu", [NFB, 128, NDC * 128], I32, kind="ExternalInput").ap()
    wd = nc.dram_tensor("wd", [NHC, 128, NFB * 128], I32, kind="ExternalInput").ap()
    y = nc.dram_tensor("y", [TOK, DM], F32, kind="ExternalOutput").ap()
    x1_dram = nc.dram_tensor("x1_dram", [TOK, DM], F32, kind="Internal").ap()
    h_dram = nc.dram_tensor("h_dram", [NFB, 128, TOK], BF16, kind="Internal").ap()
    dbg = {}

    def dbg_out(name, shape, dtype):
        if debug and name not in dbg:
            dbg[name] = nc.dram_tensor("dbg_" + name, shape, dtype,
                                       kind="ExternalOutput").ap()
        return dbg.get(name)

    with ExitStack() as octx:
        S = Sched(nc, octx)
        S.sem("bar", "cA", "cV", "cP", "cG", "dC",
              "dX0", "dX1", "dW0", "dW1", "dKW0", "dKW1",
              "dXO", "dX1W", "dHW0", "dHW1", "dHH",
              "dWG0", "dWG1", "dWU0", "dWU1", "dWD0", "dWD1",
              "dX1R", "dY", "dDBG")
        st = S.step

        def sb(name, shape, dtype, ctx):
            return ctx.enter_context(nc.sbuf_tensor(name, shape, dtype))

        def ps(name, shape, dtype, ctx):
            return ctx.enter_context(nc.psum_tensor(name, shape, dtype))

        def dump(name, ap_src, shape, after, dtype):
            o = dbg_out(name, shape, dtype)
            if o is None:
                return
            st("sync", lambda e, o=o, a=ap_src: e.dma_start(out=o, in_=a),
               waits=[after], inc="dDBG", inc_by=16)

        wctx4 = ExitStack()
        kv_ctx = ExitStack()
        _attn_only = {"bqc", "bqup", "bkvc", "bkvupk", "identf"}
        epsT = sb("epsT", [128, 1], F32, octx)
        cst = {}
        for name in c_in:
            if name not in _attn_only and name != "bkvupv":
                dt = BF16 if name == "identb" else F32
                cst[name] = sb("c_" + name, list(c_in[name].tensor.shape), dt, octx)
        o_n = sb("o_n", [128, TOK], BF16, wctx4)
        for name in c_in:
            if name in _attn_only:
                cst[name] = sb("c_" + name, list(c_in[name].tensor.shape), F32,
                               kv_ctx)
        n_c = 0
        for name in c_in:
            if name == "bkvupv":
                continue
            n_c += 1
            st("sync", lambda e, name=name: e.dma_start(out=cst[name][:],
                                                        in_=c_in[name]),
               inc="dC", inc_by=16)
        tok_consts = ("dC", 16 * n_c)
        ones1 = sb("ones1", [1, 128], F32, kv_ctx)
        ones_b = sb("ones_b", [1, 128], BF16, kv_ctx)
        tok_ones = st("gpsimd", lambda e: e.memset(ones1[:], 1.0), inc="cG")
        tok_ones = st("gpsimd", lambda e: e.memset(ones_b[:], 1.0),
                      waits=[tok_ones], inc="cG")
        tok_eps = st("gpsimd", lambda e: e.memset(epsT[:], float(EPS)),
                     waits=[tok_ones], inc="cG")
        bqup_s = sb("bqup_s", [128, 48], F32, kv_ctx)
        tok_bqups = st("vector", lambda e: e.tensor_scalar(
            out=bqup_s[:], in0=cst["bqup"][:], scalar1=float(SQ), scalar2=None,
            op0=ALU.mult), waits=[tok_consts], inc="cV")
        kv_nope = [sb(f"kv_nope{i}", [128, SEQ], BF16, kv_ctx)
                   for i in range(4)]
        krope = sb("krope", [128, SEQ], BF16, kv_ctx)

        bctx = ExitStack()
        qc_all = sb("qc_all", [128, NQC * TOK], BF16, bctx)
        q_c = [qc_all[:, i * TOK:(i + 1) * TOK] for i in range(NQC)]
        actx = ExitStack()
        wp = [sb(f"wp{i}", [128, 640], BF16, actx) for i in range(2)]
        xt = [sb(f"xt{i}", [128, DM], BF16, actx) for i in range(2)]
        ln1_own = [sb(f"ln1o{i}", [128, TOK], BF16, actx) for i in range(NDC)]
        ln1_par = [sb(f"ln1p{i}", [128, 512], BF16, actx) for i in range(NDC)]
        ln_tok = [sb(f"lntok{i}", [128, DM], BF16, actx) for i in range(2)]
        stat6 = [sb(f"stat6{i}", [128, 48], F32, actx) for i in range(2)]
        mv = [sb(f"mv{i}", [128, 2], F32, actx) for i in range(2)]
        rstd = [sb(f"rstd{i}", [128, 1], F32, actx) for i in range(2)]
        lnv = [sb(f"lnv{i}", [128, 1], F32, actx) for i in range(2)]
        tpsA = [ps(f"tpsA{i}", [128, 512], BF16, actx) for i in range(2)]
        pj = [ps(f"pj{i}", [128, 1024], F32, actx) for i in range(3)]

        S.phase = "P0"
        lastx = {}
        tok_trp = {}
        tok_lncp = {}
        tok_kvpar = {}
        tok_kv_ev = {}

        for tau in range(NTS):
            sl = tau % 2
            w = [lastx.get(sl)]
            tok_x = st("gpsimd", lambda e, sl=sl, tau=tau: e.dma_start(
                out=xt[sl][:], in_=x_seq[tau * 128:(tau + 1) * 128, :]),
                waits=w, inc=f"dX{sl}", inc_by=16)

            def f_stats(e, sl=sl):
                last = None
                for i in range(8):
                    last = e.bn_stats(out=stat6[sl][:, 6 * i:6 * i + 6],
                                      in_=xt[sl][:, 512 * i:512 * (i + 1)])
                return last
            tok_st8 = st("vector", f_stats, waits=[tok_x], inc="cV")
            tok_stats = st("vector", lambda e, sl=sl: e.bn_aggr(
                out=mv[sl][:], in_=stat6[sl][:]), waits=[tok_st8], inc="cV")
            tok_ln = st("scalar", lambda e, sl=sl: e.activation(
                out=lnv[sl][:], in_=mv[sl][:, 1:2], func=AF.Ln, bias=epsT[:]),
                waits=[tok_stats, tok_eps], inc="cA")
            tok_rstd = st("scalar", lambda e, sl=sl: e.activation(
                out=rstd[sl][:], in_=lnv[sl][:], func=AF.Exp, scale=-0.5),
                waits=[tok_ln], inc="cA")
            w_lt = [tok_trp.get((tau - 2, 7))]
            tok_nrm = st("vector", lambda e, sl=sl: e.tensor_scalar(
                out=ln_tok[sl][:], in0=xt[sl][:], scalar1=mv[sl][:, 0:1],
                scalar2=rstd[sl][:], op0=ALU.subtract, op1=ALU.mult),
                waits=[tok_rstd] + w_lt, inc="cV")
            lastx[sl] = tok_nrm

            for grp in range(8):
                pp = (tau * 8 + grp) % 2
                prev_cp = tok_lncp.get(tau * 8 + grp - 2)

                def f_tr(e, sl=sl, grp=grp, pp=pp):
                    last = None
                    for i in range(4):
                        dc = grp * 4 + i
                        last = e.transpose(out=tpsA[pp][:, 128 * i:128 * (i + 1)],
                                           in_=ln_tok[sl][:, 128 * dc:128 * (dc + 1)],
                                           identity=cst["identb"][:])
                    return last
                tok_tr = st("tensor", f_tr, waits=[tok_nrm, prev_cp, tok_consts],
                            inc="cP")
                tok_trp[(tau, grp)] = tok_tr
                if tau < NQT:
                    def f_cp(e, tau=tau, grp=grp, pp=pp):
                        last = None
                        for i in range(4):
                            dc = grp * 4 + i
                            last = e.tensor_copy(
                                out=ln1_own[dc][:, 128 * tau:128 * (tau + 1)],
                                in_=tpsA[pp][:, 128 * i:128 * (i + 1)])
                        return last
                else:
                    def f_cp(e, tau=tau, grp=grp, pp=pp):
                        col = ((tau - NQT) % 4) * 128
                        last = None
                        for i in range(4):
                            dc = grp * 4 + i
                            last = e.tensor_copy(
                                out=ln1_par[dc][:, col:col + 128],
                                in_=tpsA[pp][:, 128 * i:128 * (i + 1)])
                        return last
                wkv = [tok_kvpar.get(0)] if tau >= 12 else []
                tok_lncp[tau * 8 + grp] = st("vector", f_cp,
                                             waits=[tok_tr] + wkv, inc="cV")

            if tau in (11, 15):
                g = 0 if tau == 11 else 1
                w_ln = [tok_lncp[tau * 8 + 7]]
                tok_mm_d = {}
                for dc in range(NDC):
                    wsl = dc % 2
                    tok_w = st("gpsimd", lambda e, dc=dc, wsl=wsl: e.dma_start(
                        out=wp[wsl][:, 0:576],
                        in_=wkvc[dc * 128:(dc + 1) * 128, :]),
                        waits=[tok_mm_d.get(dc - 2)], inc=f"dW{wsl}", inc_by=16)

                    def f_mm(e, dc=dc, wsl=wsl):
                        for r in range(4):
                            reg = pj[r // 2][:, (r % 2) * 512:(r % 2) * 512 + 512]
                            e.matmul(out=reg,
                                     lhsT=wp[wsl][:, r * 128:(r + 1) * 128],
                                     rhs=ln1_par[dc][:],
                                     start=(dc == 0), stop=(dc == NDC - 1))
                        return e.matmul(out=pj[2][0:64, 0:512],
                                        lhsT=wp[wsl][:, 512:576],
                                        rhs=ln1_par[dc][:],
                                        start=(dc == 0), stop=(dc == NDC - 1))
                    tok_mm_d[dc] = st("tensor", f_mm,
                                      waits=[tok_w] + w_ln +
                                            ([tok_kv_ev.get(g - 1)] if dc == 0 else []),
                                      inc="cP")
                tok_kvpar[g] = tok_mm_d[NDC - 1]
                col0 = TOK + 512 * g

                def f_ev(e, col0=col0):
                    for r in range(4):
                        reg = pj[r // 2][:, (r % 2) * 512:(r % 2) * 512 + 512]
                        e.activation(out=kv_nope[r][:, col0:col0 + 512],
                                     in_=reg, func=AF.Identity,
                                     bias=cst["bkvc"][:, r:r + 1])
                    e.activation(out=krope[0:64, col0:col0 + 512],
                                 in_=pj[2][0:64, 0:512], func=AF.Identity,
                                 bias=cst["bkvc"][0:64, 4:5])
                    return e.activation(out=krope[64:128, col0:col0 + 512],
                                        in_=pj[2][0:64, 0:512], func=AF.Identity,
                                        bias=cst["bkvc"][0:64, 4:5])
                tok_kv_ev[g] = st("scalar", f_ev,
                                  waits=[tok_kvpar[g], tok_consts], inc="cA")

        S.phase = "P1a"
        tok_qc_ev = {}
        w_ln_own = [tok_lncp[(NQT - 1) * 8 + 7]]
        last_mm = None
        mmt = {}
        for fcg in range(4):
            for dc in range(NDC):
                wsl = dc % 2
                i_lin = fcg * NDC + dc
                tok_w = st("gpsimd", lambda e, dc=dc, wsl=wsl, fcg=fcg: e.dma_start(
                    out=wp[wsl][:, 0:384],
                    in_=wqc[dc * 128:(dc + 1) * 128, fcg * 384:(fcg + 1) * 384]),
                    waits=[mmt.get(i_lin - 2,
                                   tok_kvpar[1] if i_lin < 2 else None)],
                    inc=f"dW{wsl}", inc_by=16)
                wev = [tok_qc_ev.get(fcg - 1), tok_kv_ev.get(1)] if dc == 0 else []

                def f_mm(e, dc=dc, wsl=wsl):
                    last = None
                    for fc3 in range(3):
                        for col in range(2):
                            last = e.matmul(
                                out=pj[fc3][:, col * 512:(col + 1) * 512],
                                lhsT=wp[wsl][:, fc3 * 128:(fc3 + 1) * 128],
                                rhs=ln1_own[dc][:, col * 512:(col + 1) * 512],
                                start=(dc == 0), stop=(dc == NDC - 1))
                    return last
                last_mm = st("tensor", f_mm, waits=[tok_w] + w_ln_own + wev,
                             inc="cP")
                mmt[i_lin] = last_mm

            def f_ev(e, fcg=fcg):
                last = None
                for fc3 in range(3):
                    fc = fcg * 3 + fc3
                    last = e.activation(out=q_c[fc], in_=pj[fc3][:],
                                        func=AF.Identity,
                                        bias=cst["bqc"][:, fc:fc + 1])
                return last
            tok_qc_ev[fcg] = st("scalar", f_ev, waits=[last_mm], inc="cA")

        tok_kvo_ev = {}
        mmt2 = {}
        for p in range(2):
            for dc in range(NDC):
                wsl = dc % 2
                i_lin = p * NDC + dc
                cols = (0, 384) if p == 0 else (384, 576)
                tok_w = st("gpsimd", lambda e, dc=dc, wsl=wsl, cols=cols: e.dma_start(
                    out=wp[wsl][:, 0:cols[1] - cols[0]],
                    in_=wkvc[dc * 128:(dc + 1) * 128, cols[0]:cols[1]]),
                    waits=[mmt2.get(i_lin - 2, last_mm if i_lin < 2 else None)],
                    inc=f"dW{wsl}", inc_by=16)
                wev = []
                if dc == 0:
                    wev = [tok_qc_ev[3]] if p == 0 else [tok_kvo_ev[0]]

                def f_mm(e, dc=dc, wsl=wsl, p=p):
                    last = None
                    if p == 0:
                        for c in range(3):
                            for col in range(2):
                                last = e.matmul(
                                    out=pj[c][:, col * 512:(col + 1) * 512],
                                    lhsT=wp[wsl][:, c * 128:(c + 1) * 128],
                                    rhs=ln1_own[dc][:, col * 512:(col + 1) * 512],
                                    start=(dc == 0), stop=(dc == NDC - 1))
                    else:
                        for col in range(2):
                            e.matmul(out=pj[0][:, col * 512:(col + 1) * 512],
                                     lhsT=wp[wsl][:, 0:128],
                                     rhs=ln1_own[dc][:, col * 512:(col + 1) * 512],
                                     start=(dc == 0), stop=(dc == NDC - 1))
                            last = e.matmul(
                                out=pj[1][0:64, col * 512:(col + 1) * 512],
                                lhsT=wp[wsl][:, 128:192],
                                rhs=ln1_own[dc][:, col * 512:(col + 1) * 512],
                                start=(dc == 0), stop=(dc == NDC - 1))
                    return last
                last_mm = st("tensor", f_mm, waits=[tok_w] + wev, inc="cP")
                mmt2[i_lin] = last_mm
            if p == 0:
                def f_ev(e):
                    last = None
                    for c in range(3):
                        last = e.activation(out=kv_nope[c][:, 0:TOK],
                                            in_=pj[c][:], func=AF.Identity,
                                            bias=cst["bkvc"][:, c:c + 1])
                    return last
            else:
                def f_ev(e):
                    e.activation(out=kv_nope[3][:, 0:TOK], in_=pj[0][:],
                                 func=AF.Identity, bias=cst["bkvc"][:, 3:4])
                    e.activation(out=krope[0:64, 0:TOK],
                                 in_=pj[1][0:64, 0:1024], func=AF.Identity,
                                 bias=cst["bkvc"][0:64, 4:5])
                    return e.activation(out=krope[64:128, 0:TOK],
                                        in_=pj[1][0:64, 0:1024], func=AF.Identity,
                                        bias=cst["bkvc"][0:64, 4:5])
            tok_kvo_ev[p] = st("scalar", f_ev, waits=[last_mm], inc="cA")
        tok_kv_done = tok_kvo_ev[1]

        if debug:
            for i in range(4):
                dump(f"kv_nope{i}", kv_nope[i][:], [128, SEQ], tok_kv_done, BF16)
            dump("krope", krope[:], [128, SEQ], tok_kv_done, BF16)
            dump("qc0", q_c[0], [128, TOK], tok_qc_ev[0], BF16)
            dump("ln1o0", ln1_own[0][:], [128, TOK], w_ln_own[0], BF16)

        S.barrier()
        actx.close()

        S.phase = "P1b"
        qctx = ExitStack()
        q_fm = [sb(f"qfm{i}", [128, TOK], BF16, qctx) for i in range(NQF)]
        p1b_ctx = ExitStack()
        wq2 = [sb(f"wq2_{i}", [128, 512], BF16, p1b_ctx) for i in range(2)]
        p1b_p = ExitStack()
        pu = [ps(f"pu{i}", [128, 1024], F32, p1b_p) for i in range(4)]
        tok_qfm_ev = {}
        last_mm = None
        mmt3 = {}
        for fcg in range(NQF // 4):
            for qc in range(NQC):
                wsl = qc % 2
                i_lin = fcg * NQC + qc
                tok_w = st("gpsimd", lambda e, qc=qc, wsl=wsl, fcg=fcg: e.dma_start(
                    out=wq2[wsl][:], in_=wqup[qc * 128:(qc + 1) * 128,
                                              fcg * 512:(fcg + 1) * 512]),
                    waits=[mmt3.get(i_lin - 2)], inc=f"dW{wsl}", inc_by=16)
                wev = [tok_qfm_ev.get(fcg - 1)] if qc == 0 else []

                def f_mm(e, qc=qc, wsl=wsl):
                    last = None
                    for fc4 in range(4):
                        for col in range(2):
                            last = e.matmul(
                                out=pu[fc4][:, col * 512:(col + 1) * 512],
                                lhsT=wq2[wsl][:, fc4 * 128:(fc4 + 1) * 128],
                                rhs=qc_all[:, qc * TOK + col * 512:
                                           qc * TOK + (col + 1) * 512],
                                start=(qc == 0), stop=(qc == NQC - 1))
                    return last
                last_mm = st("tensor", f_mm, waits=[tok_w] + wev, inc="cP")
                mmt3[i_lin] = last_mm

            def f_ev(e, fcg=fcg):
                last = None
                for fc4 in range(4):
                    fc = fcg * 4 + fc4
                    last = e.activation(out=q_fm[fc][:], in_=pu[fc4][:],
                                        func=AF.Identity, scale=float(SQ),
                                        bias=bqup_s[:, fc:fc + 1])
                return last
            tok_qfm_ev[fcg] = st("scalar", f_ev, waits=[last_mm, tok_bqups],
                                 inc="cA")
        if debug:
            dump("qfm0", q_fm[0][:], [128, TOK], tok_qfm_ev[0], BF16)
            dump("qfm47", q_fm[47][:], [128, TOK], tok_qfm_ev[11], BF16)

        S.barrier()
        p1b_p.close()
        p1b_ctx.close()

        attn_ctx = ExitStack()
        kf = [sb(f"kf{i}", [128, SEQ], BF16, attn_ctx) for i in range(2)]
        vtg = [sb(f"vtg{i}", [128, 16 * 512], BF16, attn_ctx) for i in range(2)]
        wkh = [sb(f"wkh{i}", [128, 512], BF16, attn_ctx) for i in range(2)]
        wkvv = [sb(f"wkvv{i}", [128, 2048], BF16, attn_ctx) for i in range(2)]
        bvg = [sb(f"bvg{i}", [1, 512], BF16, attn_ctx) for i in range(2)]
        ring = sb("ring", [128, 3 * TOK], BF16, attn_ctx)
        sums = sb("sums", [128, 8], F32, attn_ctx)
        sumsA = sb("sumsA", [128, 8], F32, attn_ctx)
        negm = sb("negm", [128, 8], F32, attn_ctx)
        lnS = sb("lnS", [128, 8], F32, attn_ctx)
        augc = sb("augc", [128, 8], F32, attn_ctx)
        augrow_h = sb("augrow_h", [1, TOK], BF16, attn_ctx)
        augrow_l = sb("augrow_l", [1, TOK], BF16, attn_ctx)
        negmA = sb("negmA", [128, 8], F32, attn_ctx)
        s1 = ps("s1", [128, SEQ], F32, attn_ctx)
        s2 = ps("s2", [128, TOK], F32, attn_ctx)
        osum = ps("osum", [128, TOK], F32, attn_ctx)

        S.phase = "ATTN"
        tok_kw = {}
        tok_kmm = {}
        tok_kfev = {}
        tok_vev = {}
        tok_exp1 = {}
        tok_exp2 = {}
        tok_avmm = {}
        tok_maxs = {}
        tok_stt = {}
        tok_augT = {}
        tok_augrow = {}
        disc = qc_all[:, 0:SEQ]

        def emit_augpre(hp):
            def f_augT(e, hp=hp):
                last = None
                for qt in range(8):
                    last = e.matmul(out=s1[0:1, qt * 128:(qt + 1) * 128],
                                    lhsT=augc[:, qt:qt + 1],
                                    rhs=cst["identf"][:],
                                    start=True, stop=True)
                return last
            tok_augT[hp] = st("tensor", f_augT,
                              waits=[tok_stt[(hp, qt)] for qt in range(8)] +
                                    [tok_exp1[(hp, 7)]], inc="cP")
            t_h = st("vector", lambda e: e.tensor_copy(
                out=augrow_h[:], in_=s1[0:1, 0:TOK]), waits=[tok_augT[hp]],
                inc="cV")
            tok_augrow[hp] = st("vector", lambda e: e.tensor_tensor(
                out=augrow_l[:], in0=s1[0:1, 0:TOK], in1=augrow_h[:],
                op=ALU.subtract), waits=[t_h], inc="cV")

        for i in range(NH + 1):
            ia = i
            ib = i - 1
            if ib >= 0:
                emit_augpre(ib)
            if ia < NH:
                g, hg = ia // GH, ia % GH
                hsl = ia % 2
                gsl = g % 2
                if hg == 0:
                    wpre = [tok_kmm.get(4 * (g - 2) + 3)] if g >= 2 else []
                    st("gpsimd", lambda e, g=g, gsl=gsl: e.dma_start(
                        out=bvg[gsl][:],
                        in_=c_in["bkvupv"][0:1, g * 512:(g + 1) * 512]),
                       waits=wpre, inc=f"dKW{gsl}", inc_by=16)
                    for kvc in range(4):
                        tok_kw[g] = st("gpsimd",
                                       lambda e, g=g, gsl=gsl, kvc=kvc: e.dma_start(
                                           out=wkvv[gsl][:, kvc * 512:(kvc + 1) * 512],
                                           in_=wkvup[kvc * 128:(kvc + 1) * 128,
                                                     4096 + g * 512:4096 + (g + 1) * 512]),
                                       waits=wpre, inc=f"dKW{gsl}", inc_by=16)
                    for rnd in range(4):
                        w_s1 = [tok_augrow.get(ib) if rnd == 0 else
                                tok_vev.get((g, rnd - 1))]
                        def f_vmm(e, g=g, gsl=gsl, rnd=rnd):
                            last = None
                            for t4 in range(4):
                                tc = rnd * 4 + t4
                                reg = s1[:, t4 * 512:(t4 + 1) * 512]
                                for kvc in range(4):
                                    e.matmul(out=reg,
                                             lhsT=kv_nope[kvc][:, tc * 128:(tc + 1) * 128],
                                             rhs=wkvv[gsl][:, kvc * 512:(kvc + 1) * 512],
                                             start=(kvc == 0), stop=False)
                                last = e.matmul(
                                    out=reg, lhsT=ones_b[:],
                                    rhs=bvg[gsl][:],
                                    start=False, stop=True)
                            return last
                        tok_vmm = st("tensor", f_vmm,
                                     waits=w_s1 + [tok_kw[g], tok_ones], inc="cP")
                        w_vtg = [tok_avmm.get((4 * (g - 2) + 3, 15))] if g >= 2 else []
                        def f_vev(e, gsl=gsl, rnd=rnd):
                            return e.activation(
                                out=vtg[gsl][:, rnd * 2048:(rnd + 1) * 2048],
                                in_=s1[:], func=AF.Copy)
                        tok_vev[(g, rnd)] = st("scalar", f_vev,
                                               waits=[tok_vmm] + w_vtg, inc="cA")
                    s1_free = tok_vev[(g, 3)]
                else:
                    s1_free = tok_augrow.get(ib)
                wkpre = [tok_kmm.get(ia - 2)]
                tok_wk = None
                for kvc in range(4):
                    tok_wk = st("gpsimd", lambda e, ia=ia, hsl=hsl, kvc=kvc: e.dma_start(
                        out=wkh[hsl][:, kvc * 128:(kvc + 1) * 128],
                        in_=wkvup[kvc * 128:(kvc + 1) * 128,
                                  ia * 128:(ia + 1) * 128]),
                        waits=wkpre, inc=f"dW{hsl}", inc_by=16)
                tok_kfh = {}
                for kh2 in range(2):
                    def f_kmm(e, hsl=hsl, kh2=kh2):
                        last = None
                        for col in range(2 * kh2, 2 * kh2 + 2):
                            for kvc in range(4):
                                last = e.matmul(
                                    out=s1[:, col * 512:(col + 1) * 512],
                                    lhsT=wkh[hsl][:, kvc * 128:(kvc + 1) * 128],
                                    rhs=kv_nope[kvc][:, col * 512:(col + 1) * 512],
                                    start=(kvc == 0), stop=(kvc == 3))
                        return last
                    tok_kmm[ia] = st("tensor", f_kmm,
                                     waits=[s1_free if kh2 == 0 else None,
                                            tok_wk], inc="cP")
                    def f_kev(e, hsl=hsl, ia=ia, kh2=kh2):
                        return e.activation(
                            out=kf[hsl][:, kh2 * TOK:(kh2 + 1) * TOK],
                            in_=s1[:, kh2 * TOK:(kh2 + 1) * TOK],
                            func=AF.Identity, bias=cst["bkvupk"][:, ia:ia + 1])
                    tok_kfh[kh2] = st("scalar", f_kev, waits=[tok_kmm[ia]],
                                      inc="cA")
                tok_kfev[ia] = tok_kfh[1]
                tok_kfev0 = tok_kfh[0]
                nope_t = q_fm[ia]
                rope_t = q_fm[32 + ia // 2]
                rb = (ia % 2) * 64
                tok_e1a_prev = None

            for kc in range(16):
                if ib >= 0:
                    gb, hgb = ib // GH, ib % GH
                    hslb = ib % 2
                    rbb = (ib % 2) * 64
                    rope_b = q_fm[32 + ib // 2]
                    tok_s2h = {}
                    for ch in range(2):
                        w_s2 = [tok_exp2.get((ib, kc - 1, ch)) if kc > 0 else
                                tok_exp2.get((ib - 1, 15, ch))]
                        def f_s2(e, ib=ib, kc=kc, hslb=hslb, rbb=rbb,
                                 rope_b=rope_b, col=ch):
                            reg = s2[:, col * 512:(col + 1) * 512]
                            e.matmul(out=reg,
                                     lhsT=kf[hslb][:, kc * 128:(kc + 1) * 128],
                                     rhs=q_fm[ib][:, col * 512:(col + 1) * 512],
                                     start=True, stop=False)
                            e.matmul(out=reg,
                                     lhsT=krope[rbb:rbb + 64, kc * 128:(kc + 1) * 128],
                                     rhs=rope_b[rbb:rbb + 64, col * 512:(col + 1) * 512],
                                     start=False, stop=False)
                            e.matmul(
                                out=reg, lhsT=ones_b[:],
                                rhs=augrow_h[0:1, col * 512:(col + 1) * 512],
                                start=False, stop=False)
                            return e.matmul(
                                out=reg, lhsT=ones_b[:],
                                rhs=augrow_l[0:1, col * 512:(col + 1) * 512],
                                start=False, stop=True)
                        tok_s2h[ch] = st("tensor", f_s2,
                                         waits=w_s2 + [tok_augrow[ib]], inc="cP")
                    if kc >= 1:
                        def f_av(e, ib=ib, kc=kc, hgb=hgb, gb=gb):
                            last = None
                            rsl = (kc - 1) % 3
                            for col in range(2):
                                last = e.matmul(
                                    out=osum[:, col * 512:(col + 1) * 512],
                                    lhsT=vtg[gb % 2][:, (kc - 1) * 512 + hgb * 128:
                                                     (kc - 1) * 512 + (hgb + 1) * 128],
                                    rhs=ring[:, rsl * TOK + col * 512:
                                             rsl * TOK + (col + 1) * 512],
                                    start=(ib == 0 and kc == 1),
                                    stop=False)
                            return last
                        tok_avmm[(ib, kc - 1)] = st(
                            "tensor", f_av,
                            waits=[tok_exp2[(ib, kc - 1, 1)], tok_vev[(gb, 3)]],
                            inc="cP")
                    rsl = kc % 3
                    prev_kc = kc - 3 if kc >= 3 else 15 - ((15 - kc) % 3)
                    w_ring = [tok_avmm.get((ib, prev_kc)) if kc >= 3 else
                              (tok_avmm.get((ib - 1, prev_kc)) if ib > 0 else None)]
                    for ch in range(2):
                        tok_exp2[(ib, kc, ch)] = st(
                            "scalar", lambda e, rsl=rsl, ch=ch: e.activation(
                                out=ring[:, rsl * TOK + ch * 512:
                                         rsl * TOK + (ch + 1) * 512],
                                in_=s2[:, ch * 512:(ch + 1) * 512],
                                func=AF.Exp),
                            waits=[tok_s2h[ch]] + w_ring, inc="cA")
                if ia < NH and kc % 2 == 0:
                    qt = kc // 2
                    w_s1 = [tok_kfev0 if qt == 0 else tok_e1a_prev]
                    def f_s1a(e, qt=qt, hsl=hsl, rb=rb, nope_t=nope_t, rope_t=rope_t):
                        last = None
                        for col in range(2):
                            e.matmul(out=s1[:, col * 512:(col + 1) * 512],
                                     lhsT=nope_t[:, qt * 128:(qt + 1) * 128],
                                     rhs=kf[hsl][:, col * 512:(col + 1) * 512],
                                     start=True, stop=False)
                            last = e.matmul(
                                out=s1[:, col * 512:(col + 1) * 512],
                                lhsT=rope_t[rb:rb + 64, qt * 128:(qt + 1) * 128],
                                rhs=krope[rb:rb + 64, col * 512:(col + 1) * 512],
                                start=False, stop=True)
                        return last
                    tok_s1a = st("tensor", f_s1a, waits=w_s1, inc="cP")
                    def f_s1b(e, qt=qt, hsl=hsl, rb=rb, nope_t=nope_t, rope_t=rope_t):
                        last = None
                        for col in range(2, 4):
                            e.matmul(out=s1[:, col * 512:(col + 1) * 512],
                                     lhsT=nope_t[:, qt * 128:(qt + 1) * 128],
                                     rhs=kf[hsl][:, col * 512:(col + 1) * 512],
                                     start=True, stop=False)
                            last = e.matmul(
                                out=s1[:, col * 512:(col + 1) * 512],
                                lhsT=rope_t[rb:rb + 64, qt * 128:(qt + 1) * 128],
                                rhs=krope[rb:rb + 64, col * 512:(col + 1) * 512],
                                start=False, stop=True)
                        return last
                    tok_s1b = st("tensor", f_s1b,
                                 waits=[tok_kfev[ia] if qt == 0 else
                                        tok_exp1[(ia, qt - 1)]], inc="cP")
                    w_negm = [tok_exp1.get((ia - 1, qt))] if ia > 0 else []
                    t_mA = st("vector", lambda e, qt=qt: e.reduce_max(
                        out=negmA[:, qt:qt + 1], in_=s1[:, 0:TOK], axis=AX.X,
                        negate=True), waits=[tok_s1a] + w_negm, inc="cV")
                    t_mB = st("vector", lambda e, qt=qt: e.reduce_max(
                        out=negm[:, qt:qt + 1], in_=s1[:, TOK:SEQ], axis=AX.X,
                        negate=True), waits=[tok_s1b] + w_negm, inc="cV")
                    tok_maxs[(ia, qt)] = st(
                        "vector", lambda e, qt=qt: e.tensor_tensor(
                            out=negm[:, qt:qt + 1], in0=negm[:, qt:qt + 1],
                            in1=negmA[:, qt:qt + 1], op=ALU.min),
                        waits=[t_mB], inc="cV")
                    tok_e1a = st(
                        "scalar", lambda e, qt=qt: e.activation(
                            out=disc[:, 0:TOK], in_=s1[:, 0:TOK], func=AF.Exp,
                            bias=negm[:, qt:qt + 1],
                            accum_out=sumsA[:, qt:qt + 1]),
                        waits=[tok_maxs[(ia, qt)]], inc="cA")
                    tok_exp1[(ia, qt)] = st(
                        "scalar", lambda e, qt=qt: e.activation(
                            out=disc[:, TOK:SEQ], in_=s1[:, TOK:SEQ],
                            func=AF.Exp, bias=negm[:, qt:qt + 1],
                            accum_out=sums[:, qt:qt + 1]),
                        waits=[tok_e1a], inc="cA")
                    tok_e1a_prev = tok_e1a
                    tok_sad = st("vector", lambda e, qt=qt: e.tensor_tensor(
                        out=sums[:, qt:qt + 1], in0=sums[:, qt:qt + 1],
                        in1=sumsA[:, qt:qt + 1], op=ALU.add),
                        waits=[tok_exp1[(ia, qt)]], inc="cV")
                    tok_lns = st("scalar", lambda e, qt=qt: e.activation(
                        out=lnS[:, qt:qt + 1], in_=sums[:, qt:qt + 1],
                        func=AF.Ln), waits=[tok_sad], inc="cA")
                    tok_stt[(ia, qt)] = st(
                        "vector", lambda e, qt=qt: e.scalar_tensor_tensor(
                            out=augc[:, qt:qt + 1], in0=lnS[:, qt:qt + 1],
                            scalar=-1.0, in1=negm[:, qt:qt + 1],
                            op0=ALU.mult, op1=ALU.add),
                        waits=[tok_lns, tok_maxs[(ia, qt)],
                               tok_augT.get(ia - 1)], inc="cV")
            if ib >= 0:
                gb, hgb = ib // GH, ib % GH
                def f_av15(e, ib=ib, hgb=hgb, gb=gb):
                    last = None
                    rsl = 15 % 3
                    for col in range(2):
                        last = e.matmul(
                            out=osum[:, col * 512:(col + 1) * 512],
                            lhsT=vtg[gb % 2][:, 15 * 512 + hgb * 128:
                                             15 * 512 + (hgb + 1) * 128],
                            rhs=ring[:, rsl * TOK + col * 512:
                                     rsl * TOK + (col + 1) * 512],
                            start=False,
                            stop=(ib == NH - 1))
                    return last
                tok_avmm[(ib, 15)] = st(
                    "tensor", f_av15,
                    waits=[tok_exp2[(ib, 15, 1)], tok_vev[(gb, 3)]], inc="cP")
        last_av = tok_avmm[(NH - 1, 15)]

        tok_on = st("vector", lambda e: e.tensor_copy(out=o_n[:], in_=osum[:]),
                    waits=[last_av], inc="cV")
        if debug:
            dump("o_n", o_n[:], [128, TOK], tok_on, BF16)
            dump("augrow", augrow_h[:], [1, TOK], tok_augrow[NH - 1], BF16)

        S.barrier()
        attn_ctx.close()
        qctx.close()
        bctx.close()
        kv_ctx.close()

        S.phase = "W"
        ln2_fm = sb("ln2_fm", [128, NDC * TOK], BF16, wctx4)
        wctx = ExitStack()
        wo_sb = sb("wo_sb", [128, DM], BF16, wctx)
        xa_fm = [sb(f"xafm{i}", [128, TOK], BF16, wctx) for i in range(2)]
        xa_tok = sb("xa_tok", [128, NQT * DM], BF16, wctx)
        xo = sb("xo", [128, DM], F32, wctx)
        x1t = sb("x1t", [128, DM], F32, wctx)
        lnA = sb("lnA", [128, DM], BF16, wctx)
        ln2_tok = sb("ln2_tok", [128, DM], BF16, wctx)
        stat6w = sb("stat6w", [128, 48], F32, wctx)
        mvw = sb("mvw", [128, 2], F32, wctx)
        rstdw = sb("rstdw", [128, 1], F32, wctx)
        lnvw = sb("lnvw", [128, 1], F32, wctx)
        wops = [ps(f"wops{i}", [128, TOK], F32, wctx) for i in range(2)]
        tpsW = [ps(f"tpsW{i}", [128, TOK], BF16, wctx) for i in range(2)]

        tok_wo = st("gpsimd", lambda e: e.dma_start(out=wo_sb[:], in_=wo),
                    inc="dKW0", inc_by=16)
        tok_wo_ev = {}
        tok_tcp = {}
        for Hc in range(NHC):
            wsl = Hc % 2
            w_ps = [tok_wo_ev.get(Hc - 2)]

            def f_wmm(e, Hc=Hc, wsl=wsl):
                last = None
                for col in range(2):
                    last = e.matmul(
                        out=wops[wsl][:, col * 512:(col + 1) * 512],
                        lhsT=wo_sb[:, Hc * 128:(Hc + 1) * 128],
                        rhs=o_n[:, col * 512:(col + 1) * 512],
                        start=True, stop=True)
                return last
            tok_wmm = st("tensor", f_wmm, waits=[tok_wo, tok_on] + w_ps,
                         inc="cP")
            w_fm = [tok_tcp.get(Hc - 2)]
            tok_ev = st("scalar", lambda e, Hc=Hc, wsl=wsl: e.activation(
                out=xa_fm[wsl][:], in_=wops[wsl][:], func=AF.Identity,
                bias=cst["bo"][:, Hc:Hc + 1]), waits=[tok_wmm] + w_fm, inc="cA")
            tok_wo_ev[Hc] = tok_ev

            def f_tw(e, wsl=wsl):
                last = None
                for qt in range(8):
                    last = e.transpose(out=tpsW[wsl][:, qt * 128:(qt + 1) * 128],
                                       in_=xa_fm[wsl][:, qt * 128:(qt + 1) * 128],
                                       identity=cst["identb"][:])
                return last
            tok_tw = st("tensor", f_tw, waits=[tok_ev] + [tok_tcp.get(Hc - 2)],
                        inc="cP")

            def f_tc(e, Hc=Hc, wsl=wsl):
                src = tpsW[wsl][:].rearrange("p (q c) -> p q c", q=8)
                dst = xa_tok[:].rearrange("p (q d) -> p q d", q=8)[
                    :, :, Hc * 128:(Hc + 1) * 128]
                return e.tensor_copy(out=dst, in_=src)
            tok_tcp[Hc] = st("vector", f_tc, waits=[tok_tw], inc="cV")

        tok_xa_done = tok_tcp[NHC - 1]
        if debug:
            dump("xa_tok", xa_tok[:, 0:DM], [128, DM], tok_xa_done, BF16)

        tok_ln2cp = {}
        tok_x1_lastread = {}
        for qt in range(NQT):
            xa = xa_tok[:, qt * DM:(qt + 1) * DM]
            tok_xod = st("sync", lambda e, qt=qt: e.dma_start(
                out=xo[:], in_=x_seq[qt * 128:(qt + 1) * 128, :]),
                waits=[tok_x1_lastread.get(qt - 1)], inc="dXO", inc_by=16)

            def f_st(e, xa=xa):
                last = None
                for i in range(8):
                    last = e.bn_stats(out=stat6w[:, 6 * i:6 * i + 6],
                                      in_=xa[:, 512 * i:512 * (i + 1)])
                return last
            tok_sw8 = st("vector", f_st, waits=[tok_xa_done], inc="cV")
            tok_sw = st("vector", lambda e: e.bn_aggr(
                out=mvw[:], in_=stat6w[:]), waits=[tok_sw8], inc="cV")
            tok_lnw = st("scalar", lambda e: e.activation(
                out=lnvw[:], in_=mvw[:, 1:2], func=AF.Ln, bias=epsT[:]),
                waits=[tok_sw], inc="cA")
            tok_rsw = st("scalar", lambda e: e.activation(
                out=rstdw[:], in_=lnvw[:], func=AF.Exp, scale=-0.5),
                waits=[tok_lnw], inc="cA")
            tok_nw = st("vector", lambda e, xa=xa: e.tensor_scalar(
                out=lnA[:], in0=xa, scalar1=mvw[:, 0:1], scalar2=rstdw[:],
                op0=ALU.subtract, op1=ALU.mult), waits=[tok_rsw], inc="cV")
            tok_x1 = st("vector", lambda e: e.tensor_tensor(
                out=x1t[:], in0=lnA[:], in1=xo[:], op=ALU.add),
                waits=[tok_nw, tok_xod, ("dX1W", 16 * qt)], inc="cV")
            st("sync", lambda e, qt=qt: e.dma_start(
                out=x1_dram[qt * 128:(qt + 1) * 128, :], in_=x1t[:]),
                waits=[tok_x1], inc="dX1W", inc_by=16)

            def f_st2(e):
                last = None
                for i in range(8):
                    last = e.bn_stats(out=stat6w[:, 6 * i:6 * i + 6],
                                      in_=x1t[:, 512 * i:512 * (i + 1)])
                return last
            tok_sw2a = st("vector", f_st2, waits=[tok_x1], inc="cV")
            tok_sw2 = st("vector", lambda e: e.bn_aggr(
                out=mvw[:], in_=stat6w[:]), waits=[tok_sw2a], inc="cV")
            tok_lnw2 = st("scalar", lambda e: e.activation(
                out=lnvw[:], in_=mvw[:, 1:2], func=AF.Ln, bias=epsT[:]),
                waits=[tok_sw2], inc="cA")
            tok_rsw2 = st("scalar", lambda e: e.activation(
                out=rstdw[:], in_=lnvw[:], func=AF.Exp, scale=-0.5),
                waits=[tok_lnw2], inc="cA")
            tok_nw2 = st("vector", lambda e: e.tensor_scalar(
                out=ln2_tok[:], in0=x1t[:], scalar1=mvw[:, 0:1],
                scalar2=rstdw[:], op0=ALU.subtract, op1=ALU.mult),
                waits=[tok_rsw2], inc="cV")
            tok_x1_lastread[qt] = tok_nw2
            for grp in range(8):
                pp = grp % 2
                w_tp = [tok_ln2cp.get((qt, grp - 2)) if grp >= 2 else
                        (tok_ln2cp.get((qt - 1, 6 + grp)) if qt > 0 else None)]

                def f_t2(e, pp=pp, grp=grp):
                    last = None
                    for i in range(4):
                        dc = grp * 4 + i
                        last = e.transpose(
                            out=tpsW[pp][:, i * 128:(i + 1) * 128],
                            in_=ln2_tok[:, dc * 128:(dc + 1) * 128],
                            identity=cst["identb"][:])
                    return last
                tok_t2 = st("tensor", f_t2, waits=[tok_nw2] + w_tp, inc="cP")

                def f_c2(e, pp=pp, grp=grp, qt=qt):
                    src = tpsW[pp][:, 0:512].rearrange("p (c f) -> p c f", c=4)
                    dst = ln2_fm[:].rearrange("p (d t) -> p d t", d=NDC)[
                        :, grp * 4:grp * 4 + 4, qt * 128:(qt + 1) * 128]
                    return e.tensor_copy(out=dst, in_=src)
                tok_ln2cp[(qt, grp)] = st("vector", f_c2, waits=[tok_t2],
                                          inc="cV")
        tok_ln2_done = tok_ln2cp[(NQT - 1, 7)]
        if debug:
            dump("x1_7", x1t[:], [128, DM], tok_x1_lastread[NQT - 1], F32)
            dump("ln2fm", ln2_fm[:, 0:TOK], [128, TOK], tok_ln2_done, BF16)

        S.barrier()
        wctx.close()

        S.phase = "F1"
        fctx = ExitStack()
        wgt = [sb(f"wgt{i}", [128, NDC * 128], BF16, fctx) for i in range(2)]
        wut = [sb(f"wut{i}", [128, NDC * 128], BF16, fctx) for i in range(2)]
        tmpg = [sb(f"tmpg{i}", [128, TOK], F32, fctx) for i in range(2)]
        tmpu = [sb(f"tmpu{i}", [128, TOK], F32, fctx) for i in range(2)]
        htile = [sb(f"htile{i}", [128, TOK], BF16, fctx) for i in range(2)]
        psg = [ps(f"psg{i}", [128, TOK], F32, fctx) for i in range(2)]
        psu2 = [ps(f"psu2{i}", [128, TOK], F32, fctx) for i in range(2)]
        tok_gev = {}
        tok_uev = {}
        tok_hmul = {}
        tok_f1mm = {}
        for fb in range(NFB):
            sl = fb % 2
            w_w = [tok_f1mm.get(fb - 2)]
            tok_wgd = st("gpsimd", lambda e, fb=fb, sl=sl: e.dma_start(
                out=wgt[sl][:], in_=wg[fb]), waits=w_w, inc=f"dWG{sl}",
                inc_by=16)
            tok_wud = st("gpsimd", lambda e, fb=fb, sl=sl: e.dma_start(
                out=wut[sl][:], in_=wu[fb]), waits=w_w, inc=f"dWU{sl}",
                inc_by=16)
            w_ps = [tok_gev.get(fb - 2), tok_uev.get(fb - 2)]

            def f_mm(e, sl=sl):
                last = None
                for dc in range(NDC):
                    for col in range(2):
                        e.matmul(out=psg[sl][:, col * 512:(col + 1) * 512],
                                 lhsT=wgt[sl][:, dc * 128:(dc + 1) * 128],
                                 rhs=ln2_fm[:, dc * TOK + col * 512:
                                            dc * TOK + (col + 1) * 512],
                                 start=(dc == 0), stop=(dc == NDC - 1))
                        last = e.matmul(
                            out=psu2[sl][:, col * 512:(col + 1) * 512],
                            lhsT=wut[sl][:, dc * 128:(dc + 1) * 128],
                            rhs=ln2_fm[:, dc * TOK + col * 512:
                                       dc * TOK + (col + 1) * 512],
                            start=(dc == 0), stop=(dc == NDC - 1))
                return last
            tok_f1mm[fb] = st("tensor", f_mm,
                              waits=[tok_wgd, tok_wud, tok_ln2_done] + w_ps,
                              inc="cP")
            w_tmp = [tok_hmul.get(fb - 2)]
            tok_g = st("scalar", lambda e, fb=fb, sl=sl: e.activation(
                out=tmpg[sl][:], in_=psg[sl][:], func=AF.Silu,
                scale=cst["gs"][:, fb:fb + 1]),
                waits=[tok_f1mm[fb]] + w_tmp, inc="cA")
            tok_gev[fb] = tok_g
            tok_u = st("vector", lambda e, fb=fb, sl=sl: e.tensor_scalar(
                out=tmpu[sl][:], in0=psu2[sl][:],
                scalar1=cst["us"][:, fb:fb + 1], scalar2=None, op0=ALU.mult),
                waits=[tok_f1mm[fb]] + w_tmp, inc="cV")
            tok_uev[fb] = tok_u
            hsem = f"dHW{sl}"
            w_h = [(hsem, 16 * (fb // 2))] if fb >= 2 else []
            tok_m = st("vector", lambda e, sl=sl: e.tensor_tensor(
                out=htile[sl][:], in0=tmpg[sl][:], in1=tmpu[sl][:],
                op=ALU.mult), waits=[tok_g, tok_u] + w_h, inc="cV")
            tok_hmul[fb] = tok_m
            st("sync", lambda e, fb=fb, sl=sl: e.dma_start(
                out=h_dram[fb], in_=htile[sl][:]), waits=[tok_m], inc=hsem,
                inc_by=16)
        tok_hw_all = [("dHW0", 16 * 43), ("dHW1", 16 * 43)]

        S.barrier()
        fctx.close()
        wctx4.close()

        S.phase = "F2"
        f2ctx = ExitStack()
        hh = sb("hh", [128, NFB * 512], BF16, f2ctx)
        wdt = [sb(f"wdt{i}", [128, NFB * 128], BF16, f2ctx) for i in range(2)]
        ff_fm = [sb(f"fffm{i}", [128, 512], BF16, f2ctx) for i in range(2)]
        ffn_tok = sb("ffn_tok", [128, 4 * DM], BF16, f2ctx)
        x1r = sb("x1r", [128, DM], F32, f2ctx)
        yt = sb("yt", [128, DM], F32, f2ctx)
        lnF = sb("lnF", [128, DM], BF16, f2ctx)
        stat6f = sb("stat6f", [128, 48], F32, f2ctx)
        mvf = sb("mvf", [128, 2], F32, f2ctx)
        rstdf = sb("rstdf", [128, 1], F32, f2ctx)
        lnvf = sb("lnvf", [128, 1], F32, f2ctx)
        dps = [ps(f"dps{i}", [128, 512], F32, f2ctx) for i in range(4)]
        tpsF = [ps(f"tpsF{i}", [128, 512], BF16, f2ctx) for i in range(2)]

        tok_y = {}
        tok_dmm = {}
        for half in range(2):
            w_hh = list(tok_hw_all)
            if half == 1:
                w_hh.append(tok_dmm[(0, NHC - 1)])

            fb0 = 0
            for n in [11] * 7 + [9]:
                def f_hh(e, half=half, fb0=fb0, n=n):
                    s_ = h_dram[fb0:fb0 + n, :, half * 512:(half + 1) * 512]
                    s_ = s_.rearrange("f p c -> p f c")
                    dst = hh[:].rearrange("p (f c) -> p f c", f=NFB)[
                        :, fb0:fb0 + n, :]
                    return e.dma_start(out=dst, in_=s_)
                tok_hh = st("sync", f_hh, waits=w_hh, inc="dHH", inc_by=16)
                fb0 += n
            tok_dev = {}
            tok_fcp = {}
            for j in range(NHC):
                sl = j % 2
                w_wd = [tok_dmm.get((half, j - 2)) if j >= 2 else
                        (tok_dmm.get((0, NHC - 2 + j)) if half == 1 else None)]
                tok_wdd = st("gpsimd", lambda e, j=j, sl=sl: e.dma_start(
                    out=wdt[sl][:], in_=wd[j]), waits=w_wd, inc=f"dWD{sl}",
                    inc_by=16)
                w_ps = [tok_dev.get(j - 4)]

                def f_dmm(e, j=j, sl=sl):
                    last = None
                    for fc in range(NFB):
                        last = e.matmul(
                            out=dps[j % 4][:],
                            lhsT=wdt[sl][:, fc * 128:(fc + 1) * 128],
                            rhs=hh[:, fc * 512:(fc + 1) * 512],
                            start=(fc == 0), stop=(fc == NFB - 1))
                    return last
                tok_dmm[(half, j)] = st("tensor", f_dmm,
                                        waits=[tok_wdd, ("dHH", 128 * (half + 1))] + w_ps,
                                        inc="cP")
                w_ff = [tok_fcp.get(j - 2)]
                tok_de = st("vector", lambda e, j=j, sl=sl: e.tensor_scalar(
                    out=ff_fm[sl][:], in0=dps[j % 4][:],
                    scalar1=cst["ds"][:, j:j + 1], scalar2=None, op0=ALU.mult),
                    waits=[tok_dmm[(half, j)]] + w_ff, inc="cV")
                tok_dev[j] = tok_de

                def f_tf(e, sl=sl):
                    last = None
                    for tt in range(4):
                        last = e.transpose(
                            out=tpsF[sl][:, tt * 128:(tt + 1) * 128],
                            in_=ff_fm[sl][:, tt * 128:(tt + 1) * 128],
                            identity=cst["identb"][:])
                    return last
                tok_tf = st("tensor", f_tf,
                            waits=[tok_de, tok_fcp.get(j - 2)], inc="cP")

                def f_cf(e, j=j, sl=sl):
                    src = tpsF[sl][:, 0:512].rearrange("p (t c) -> p t c", t=4)
                    dst = ffn_tok[:].rearrange("p (t d) -> p t d", t=4)[
                        :, :, j * 128:(j + 1) * 128]
                    return e.tensor_copy(out=dst, in_=src)
                tok_fcp[j] = st("vector", f_cf, waits=[tok_tf], inc="cV")
            tok_ffn_done = tok_fcp[NHC - 1]

            for tt in range(4):
                qt = half * 4 + tt
                ft = ffn_tok[:, tt * DM:(tt + 1) * DM]
                tok_x1r = st("sync", lambda e, qt=qt: e.dma_start(
                    out=x1r[:], in_=x1_dram[qt * 128:(qt + 1) * 128, :]),
                    waits=[tok_y.get(qt - 1), ("dX1W", 16 * NQT)],
                    inc="dX1R", inc_by=16)

                def f_sf(e, ft=ft):
                    last = None
                    for i in range(8):
                        last = e.bn_stats(out=stat6f[:, 6 * i:6 * i + 6],
                                          in_=ft[:, 512 * i:512 * (i + 1)])
                    return last
                tok_sf8 = st("vector", f_sf, waits=[tok_ffn_done], inc="cV")
                tok_sf = st("vector", lambda e: e.bn_aggr(
                    out=mvf[:], in_=stat6f[:]), waits=[tok_sf8], inc="cV")
                tok_lf = st("scalar", lambda e: e.activation(
                    out=lnvf[:], in_=mvf[:, 1:2], func=AF.Ln, bias=epsT[:]),
                    waits=[tok_sf], inc="cA")
                tok_rf = st("scalar", lambda e: e.activation(
                    out=rstdf[:], in_=lnvf[:], func=AF.Exp, scale=-0.5),
                    waits=[tok_lf], inc="cA")
                tok_nf = st("vector", lambda e, ft=ft: e.tensor_scalar(
                    out=lnF[:], in0=ft, scalar1=mvf[:, 0:1], scalar2=rstdf[:],
                    op0=ALU.subtract, op1=ALU.mult), waits=[tok_rf], inc="cV")
                tok_yt = st("vector", lambda e: e.tensor_tensor(
                    out=yt[:], in0=lnF[:], in1=x1r[:], op=ALU.add),
                    waits=[tok_nf, tok_x1r, ("dY", 16 * qt)], inc="cV")
                tok_y[qt] = st("sync", lambda e, qt=qt: e.dma_start(
                    out=y[qt * 128:(qt + 1) * 128, :], in_=yt[:]),
                    waits=[tok_yt], inc="dY", inc_by=16)

        S.q["sync"].append((None, [("dY", 16 * NQT)], None, 0, S.phase))
        if debug and dbg:
            S.q["sync"].append((None, [("dDBG", S.sem_n["dDBG"])], None, 0,
                                S.phase))

        block = octx.enter_context(nc.Block())
        S.build(block)
        f2ctx.close()
    return nc, dbg


_NC_CACHE = {}


def get_nc(debug=False):
    if debug not in _NC_CACHE:
        _NC_CACHE[debug] = build_nc(debug)
    return _NC_CACHE[debug]


def prep_inputs(L0_attn_hidden_size, L0_attn_Wq_c, L0_attn_bq_c, L0_attn_Wq_up,
                L0_attn_bq_up, L0_attn_Wkv_c, L0_attn_bkv_c, L0_attn_Wkv_up,
                L0_attn_bkv_up, L0_attn_Wo, L0_attn_bo, L0_ffn_gate_w,
                L0_ffn_gate_s, L0_ffn_up_w, L0_ffn_up_s, L0_ffn_down_w,
                L0_ffn_down_s):
    x = np.asarray(L0_attn_hidden_size, np.float32)

    def colsT(v, n):
        return np.ascontiguousarray(np.asarray(v, np.float32).reshape(n, 128).T)

    perm_q = np.concatenate(
        [(np.arange(32)[:, None] * 192 + np.arange(128)[None, :]).ravel(),
         (np.arange(32)[:, None] * 192 + 128 + np.arange(64)[None, :]).ravel()])
    perm_kv = np.concatenate(
        [(np.arange(32)[:, None] * 256 + np.arange(128)[None, :]).ravel(),
         (np.arange(32)[:, None] * 256 + 128 + np.arange(128)[None, :]).ravel()])

    wqc_t = np.ascontiguousarray(np.asarray(L0_attn_Wq_c, np.float32).T)
    wqup_t = np.ascontiguousarray(np.asarray(L0_attn_Wq_up, np.float32)[perm_q].T)
    wkvc_t = np.ascontiguousarray(np.asarray(L0_attn_Wkv_c, np.float32).T)
    wkvup_t = np.ascontiguousarray(
        np.asarray(L0_attn_Wkv_up, np.float32)[perm_kv].T)
    wo_t = np.ascontiguousarray(np.asarray(L0_attn_Wo, np.float32).T)

    bqc = colsT(L0_attn_bq_c, 12)
    bqup = colsT(np.asarray(L0_attn_bq_up, np.float32)[perm_q], 48)
    bkvc_pad = np.zeros(640, np.float32)
    bkvc_pad[:576] = np.asarray(L0_attn_bkv_c, np.float32)
    bkvc = colsT(bkvc_pad, 5)
    bkvup_p = np.asarray(L0_attn_bkv_up, np.float32)[perm_kv]
    bkvupk = colsT(bkvup_p[:4096], 32)
    bkvupv = np.ascontiguousarray(bkvup_p[4096:].reshape(1, 4096))
    bo = colsT(L0_attn_bo, 32)
    gs = colsT(L0_ffn_gate_s, 86)
    us = colsT(L0_ffn_up_s, 86)
    ds = colsT(L0_ffn_down_s, 32)

    gw = np.asarray(L0_ffn_gate_w, np.int32)
    uw = np.asarray(L0_ffn_up_w, np.int32)
    dw = np.asarray(L0_ffn_down_w, np.int32)

    def tile_gu(w):
        wt = np.ascontiguousarray(w.T)
        v = wt.reshape(NDC, 128, NFB, 128)
        return np.ascontiguousarray(
            v.transpose(2, 1, 0, 3).reshape(NFB, 128, NDC * 128))
    wg_t = tile_gu(gw)
    wu_t = tile_gu(uw)
    dwt = np.ascontiguousarray(dw.T)
    v = dwt.reshape(NFB, 128, NHC, 128)
    wd_t = np.ascontiguousarray(
        v.transpose(2, 1, 0, 3).reshape(NHC, 128, NFB * 128))

    identb = np.eye(128).astype(ml_dtypes.bfloat16)
    identf = np.eye(128, dtype=np.float32)

    shared = dict(wqc=wqc_t, wqup=wqup_t, wkvc=wkvc_t, wkvup=wkvup_t, wo=wo_t,
                  bqc=bqc, bqup=bqup, bkvc=bkvc, bkvupk=bkvupk, bkvupv=bkvupv,
                  bo=bo, gs=gs, us=us, ds=ds, identb=identb, identf=identf,
                  wg=wg_t, wu=wu_t, wd=wd_t)
    in_maps = []
    for c in range(8):
        b, hf = c // 2, c % 2
        xs = np.concatenate([x[b, hf * 1024:(hf + 1) * 1024],
                             x[b, (1 - hf) * 1024:(2 - hf) * 1024]], axis=0)
        m = dict(shared)
        m["x_seq"] = np.ascontiguousarray(xs)
        in_maps.append(m)
    return in_maps


def kernel(**inputs):
    in_maps = prep_inputs(**inputs)
    nc, _ = get_nc(debug=False)
    res = run_bass_kernel_spmd(nc, in_maps, core_ids=list(range(8)))
    out = np.empty((4, 2048, 4096), np.float32)
    for c in range(8):
        b, hf = c // 2, c % 2
        out[b, hf * 1024:(hf + 1) * 1024] = res.results[c]["y"]
    return out


# version 07
# speedup vs baseline: 1.0549x; 1.0549x over previous
import numpy as np
import ml_dtypes
from contextlib import ExitStack

import concourse.bass as bass
import concourse.mybir as mybir
from concourse.bass_utils import run_bass_kernel_spmd

F32 = mybir.dt.float32
BF16 = mybir.dt.bfloat16
I32 = mybir.dt.int32
AF = mybir.ActivationFunctionType
ALU = mybir.AluOpType
AX = mybir.AxisListType

DM = 4096
NDC = 32
TOK = 1024
NQT = 8
NTS = 16
SEQ = 2048
QL, NQC = 1536, 12
QF, NQF = 6144, 48
KVL = 512
NH, GH = 32, 4
DFF, NFB = 11008, 86
NHC = 32
SQ = 1.0 / np.sqrt(192.0)
EPS = 1e-5

ENGS = ("sync", "scalar", "vector", "tensor", "gpsimd")
ESEM = {"scalar": "cA", "vector": "cV", "tensor": "cP", "gpsimd": "cG"}


class Sched:
    def __init__(self, nc, ctx):
        self.nc = nc
        self.ctx = ctx
        self.q = {e: [] for e in ENGS}
        self.sem_h = {}
        self.sem_n = {}
        self.waited = {e: {} for e in ENGS}
        self.nbar = 0
        self.phase = "init"

    def sem(self, *names):
        for name in names:
            self.sem_h[name] = self.ctx.enter_context(self.nc.semaphore(name))
            self.sem_n[name] = 0

    def step(self, eng, fn, waits=(), inc=None, inc_by=1):
        own = ESEM.get(eng)
        wl = []
        for w in waits:
            if w is None:
                continue
            sn, v = w
            if sn == own:
                assert v <= self.sem_n[sn], f"self-wait ahead on {sn}: {v} > {self.sem_n[sn]}"
            prev = self.waited[eng].get(sn, 0)
            if v > prev:
                wl.append((sn, v))
                self.waited[eng][sn] = v
        tok = None
        if inc is not None:
            self.sem_n[inc] += inc_by
            tok = (inc, self.sem_n[inc])
        self.q[eng].append((fn, wl, inc, inc_by, self.phase))
        return tok

    def barrier(self):
        self.nbar += 1
        target = 5 * self.nbar
        for e in ENGS:
            self.sem_n["bar"] += 1
            self.q[e].append((lambda eng: eng.nop(), [], "bar", 1, self.phase))
        for e in ENGS:
            self.q[e].append((None, [("bar", target)], None, 0, self.phase))
            self.waited[e]["bar"] = target

    def build(self, block):
        for e in ENGS:
            ops = self.q[e]

            def mk(ops):
                def body(eng):
                    cur = None
                    sid = None
                    for fn, wl, inc, inc_by, ph in ops:
                        if ph != cur:
                            if cur is not None:
                                self.nc.leave_named_scope(cur, sid, False)
                            sid, _ = self.nc.enter_named_scope(ph, False)
                            cur = ph
                        for sn, v in wl:
                            eng.wait_ge(self.sem_h[sn], v)
                        if fn is None:
                            continue
                        inst = fn(eng)
                        if inc is not None:
                            inst.then_inc(self.sem_h[inc], inc_by)
                    if cur is not None:
                        self.nc.leave_named_scope(cur, sid, False)
                return body
            getattr(block, e)(mk(ops))


def build_nc(debug=False):
    nc = bass.Bass("TRN2", target_bir_lowering=False)
    x_seq = nc.dram_tensor("x_seq", [SEQ, DM], F32, kind="ExternalInput").ap()
    wqc = nc.dram_tensor("wqc", [DM, QL], F32, kind="ExternalInput").ap()
    wqup = nc.dram_tensor("wqup", [QL, QF], F32, kind="ExternalInput").ap()
    wkvc = nc.dram_tensor("wkvc", [DM, 576], F32, kind="ExternalInput").ap()
    wkvup = nc.dram_tensor("wkvup", [KVL, 8192], F32, kind="ExternalInput").ap()
    wo = nc.dram_tensor("wo", [128, DM], F32, kind="ExternalInput").ap()
    c_in = {}
    for name, sh in [("bqc", [128, 12]), ("bqup", [128, 48]), ("bkvc", [128, 5]),
                     ("bkvupk", [128, 32]), ("bkvupv", [1, 4096]), ("bo", [128, 32]),
                     ("gs", [128, 86]), ("us", [128, 86]), ("ds", [128, 32]),
                     ("identb", [128, 128]), ("identf", [128, 128])]:
        dt = BF16 if name == "identb" else F32
        c_in[name] = nc.dram_tensor(name, sh, dt, kind="ExternalInput").ap()
    wg = nc.dram_tensor("wg", [NFB, 128, NDC * 128], I32, kind="ExternalInput").ap()
    wu = nc.dram_tensor("wu", [NFB, 128, NDC * 128], I32, kind="ExternalInput").ap()
    wd = nc.dram_tensor("wd", [NHC, 128, NFB * 128], I32, kind="ExternalInput").ap()
    y = nc.dram_tensor("y", [TOK, DM], F32, kind="ExternalOutput").ap()
    x1_dram = nc.dram_tensor("x1_dram", [TOK, DM], F32, kind="Internal").ap()
    h_dram = nc.dram_tensor("h_dram", [NFB, 128, TOK], BF16, kind="Internal").ap()
    dbg = {}

    def dbg_out(name, shape, dtype):
        if debug and name not in dbg:
            dbg[name] = nc.dram_tensor("dbg_" + name, shape, dtype,
                                       kind="ExternalOutput").ap()
        return dbg.get(name)

    with ExitStack() as octx:
        S = Sched(nc, octx)
        S.sem("bar", "cA", "cV", "cP", "cG", "dC",
              "dX0", "dX1", "dW0", "dW1", "dKW0", "dKW1",
              "dXO", "dX1W", "dHW0", "dHW1", "dHH",
              "dWG0", "dWG1", "dWU0", "dWU1", "dWD0", "dWD1",
              "dX1R", "dY", "dDBG")
        st = S.step

        def sb(name, shape, dtype, ctx):
            return ctx.enter_context(nc.sbuf_tensor(name, shape, dtype))

        def ps(name, shape, dtype, ctx):
            return ctx.enter_context(nc.psum_tensor(name, shape, dtype))

        def dump(name, ap_src, shape, after, dtype):
            o = dbg_out(name, shape, dtype)
            if o is None:
                return
            st("sync", lambda e, o=o, a=ap_src: e.dma_start(out=o, in_=a),
               waits=[after], inc="dDBG", inc_by=16)

        wctx4 = ExitStack()
        kv_ctx = ExitStack()
        _attn_only = {"bqc", "bqup", "bkvc", "bkvupk", "identf"}
        epsT = sb("epsT", [128, 1], F32, octx)
        cst = {}
        for name in c_in:
            if name not in _attn_only and name != "bkvupv":
                dt = BF16 if name == "identb" else F32
                cst[name] = sb("c_" + name, list(c_in[name].tensor.shape), dt, octx)
        o_n = sb("o_n", [128, TOK], BF16, wctx4)
        for name in c_in:
            if name in _attn_only:
                cst[name] = sb("c_" + name, list(c_in[name].tensor.shape), F32,
                               kv_ctx)
        n_c = 0
        for name in c_in:
            if name == "bkvupv":
                continue
            n_c += 1
            st("sync", lambda e, name=name: e.dma_start(out=cst[name][:],
                                                        in_=c_in[name]),
               inc="dC", inc_by=16)
        tok_consts = ("dC", 16 * n_c)
        ones1 = sb("ones1", [1, 128], F32, kv_ctx)
        ones_b = sb("ones_b", [1, 128], BF16, kv_ctx)
        tok_ones = st("gpsimd", lambda e: e.memset(ones1[:], 1.0), inc="cG")
        tok_ones = st("gpsimd", lambda e: e.memset(ones_b[:], 1.0),
                      waits=[tok_ones], inc="cG")
        tok_eps = st("gpsimd", lambda e: e.memset(epsT[:], float(EPS)),
                     waits=[tok_ones], inc="cG")
        st("gpsimd", lambda e: e.memset(krz0[64:128, :], 0.0),
           waits=[tok_eps], inc="cG")
        tok_krz_z = st("gpsimd", lambda e: e.memset(krz1[0:64, :], 0.0),
                       inc="cG")
        bqup_s = sb("bqup_s", [128, 48], F32, kv_ctx)
        tok_bqups = st("vector", lambda e: e.tensor_scalar(
            out=bqup_s[:], in0=cst["bqup"][:], scalar1=float(SQ), scalar2=None,
            op0=ALU.mult), waits=[tok_consts], inc="cV")
        kv_nope = [sb(f"kv_nope{i}", [128, SEQ], BF16, kv_ctx)
                   for i in range(4)]
        krz0 = sb("krz0", [128, SEQ], BF16, kv_ctx)
        krz1 = sb("krz1", [128, SEQ], BF16, kv_ctx)

        bctx = ExitStack()
        qc_all = sb("qc_all", [128, NQC * TOK], BF16, bctx)
        q_c = [qc_all[:, i * TOK:(i + 1) * TOK] for i in range(NQC)]
        actx = ExitStack()
        wp = [sb(f"wp{i}", [128, 640], BF16, actx) for i in range(2)]
        xt = [sb(f"xt{i}", [128, DM], BF16, actx) for i in range(2)]
        ln1_own = [sb(f"ln1o{i}", [128, TOK], BF16, actx) for i in range(NDC)]
        ln1_par = [sb(f"ln1p{i}", [128, 512], BF16, actx) for i in range(NDC)]
        ln_tok = [sb(f"lntok{i}", [128, DM], BF16, actx) for i in range(2)]
        stat6 = [sb(f"stat6{i}", [128, 48], F32, actx) for i in range(2)]
        mv = [sb(f"mv{i}", [128, 2], F32, actx) for i in range(2)]
        rstd = [sb(f"rstd{i}", [128, 1], F32, actx) for i in range(2)]
        lnv = [sb(f"lnv{i}", [128, 1], F32, actx) for i in range(2)]
        tpsA = [ps(f"tpsA{i}", [128, 512], BF16, actx) for i in range(2)]
        pj = [ps(f"pj{i}", [128, 1024], F32, actx) for i in range(3)]

        S.phase = "P0"
        lastx = {}
        tok_trp = {}
        tok_lncp = {}
        tok_kvpar = {}
        tok_kv_ev = {}

        for tau in range(NTS):
            sl = tau % 2
            w = [lastx.get(sl)]
            tok_x = st("gpsimd", lambda e, sl=sl, tau=tau: e.dma_start(
                out=xt[sl][:], in_=x_seq[tau * 128:(tau + 1) * 128, :]),
                waits=w, inc=f"dX{sl}", inc_by=16)

            def f_stats(e, sl=sl):
                last = None
                for i in range(8):
                    last = e.bn_stats(out=stat6[sl][:, 6 * i:6 * i + 6],
                                      in_=xt[sl][:, 512 * i:512 * (i + 1)])
                return last
            tok_st8 = st("vector", f_stats, waits=[tok_x], inc="cV")
            tok_stats = st("vector", lambda e, sl=sl: e.bn_aggr(
                out=mv[sl][:], in_=stat6[sl][:]), waits=[tok_st8], inc="cV")
            tok_ln = st("scalar", lambda e, sl=sl: e.activation(
                out=lnv[sl][:], in_=mv[sl][:, 1:2], func=AF.Ln, bias=epsT[:]),
                waits=[tok_stats, tok_eps], inc="cA")
            tok_rstd = st("scalar", lambda e, sl=sl: e.activation(
                out=rstd[sl][:], in_=lnv[sl][:], func=AF.Exp, scale=-0.5),
                waits=[tok_ln], inc="cA")
            w_lt = [tok_trp.get((tau - 2, 7))]
            tok_nrm = st("vector", lambda e, sl=sl: e.tensor_scalar(
                out=ln_tok[sl][:], in0=xt[sl][:], scalar1=mv[sl][:, 0:1],
                scalar2=rstd[sl][:], op0=ALU.subtract, op1=ALU.mult),
                waits=[tok_rstd] + w_lt, inc="cV")
            lastx[sl] = tok_nrm

            for grp in range(8):
                pp = (tau * 8 + grp) % 2
                prev_cp = tok_lncp.get(tau * 8 + grp - 2)

                def f_tr(e, sl=sl, grp=grp, pp=pp):
                    last = None
                    for i in range(4):
                        dc = grp * 4 + i
                        last = e.transpose(out=tpsA[pp][:, 128 * i:128 * (i + 1)],
                                           in_=ln_tok[sl][:, 128 * dc:128 * (dc + 1)],
                                           identity=cst["identb"][:])
                    return last
                tok_tr = st("tensor", f_tr, waits=[tok_nrm, prev_cp, tok_consts],
                            inc="cP")
                tok_trp[(tau, grp)] = tok_tr
                if tau < NQT:
                    def f_cp(e, tau=tau, grp=grp, pp=pp):
                        last = None
                        for i in range(4):
                            dc = grp * 4 + i
                            last = e.tensor_copy(
                                out=ln1_own[dc][:, 128 * tau:128 * (tau + 1)],
                                in_=tpsA[pp][:, 128 * i:128 * (i + 1)])
                        return last
                else:
                    def f_cp(e, tau=tau, grp=grp, pp=pp):
                        col = ((tau - NQT) % 4) * 128
                        last = None
                        for i in range(4):
                            dc = grp * 4 + i
                            last = e.tensor_copy(
                                out=ln1_par[dc][:, col:col + 128],
                                in_=tpsA[pp][:, 128 * i:128 * (i + 1)])
                        return last
                wkv = [tok_kvpar.get(0)] if tau >= 12 else []
                tok_lncp[tau * 8 + grp] = st("vector", f_cp,
                                             waits=[tok_tr] + wkv, inc="cV")

            if tau in (11, 15):
                g = 0 if tau == 11 else 1
                w_ln = [tok_lncp[tau * 8 + 7]]
                tok_mm_d = {}
                for dc in range(NDC):
                    wsl = dc % 2
                    tok_w = st("gpsimd", lambda e, dc=dc, wsl=wsl: e.dma_start(
                        out=wp[wsl][:, 0:576],
                        in_=wkvc[dc * 128:(dc + 1) * 128, :]),
                        waits=[tok_mm_d.get(dc - 2)], inc=f"dW{wsl}", inc_by=16)

                    def f_mm(e, dc=dc, wsl=wsl):
                        for r in range(4):
                            reg = pj[r // 2][:, (r % 2) * 512:(r % 2) * 512 + 512]
                            e.matmul(out=reg,
                                     lhsT=wp[wsl][:, r * 128:(r + 1) * 128],
                                     rhs=ln1_par[dc][:],
                                     start=(dc == 0), stop=(dc == NDC - 1))
                        return e.matmul(out=pj[2][0:64, 0:512],
                                        lhsT=wp[wsl][:, 512:576],
                                        rhs=ln1_par[dc][:],
                                        start=(dc == 0), stop=(dc == NDC - 1))
                    tok_mm_d[dc] = st("tensor", f_mm,
                                      waits=[tok_w] + w_ln +
                                            ([tok_kv_ev.get(g - 1)] if dc == 0 else []),
                                      inc="cP")
                tok_kvpar[g] = tok_mm_d[NDC - 1]
                col0 = TOK + 512 * g

                def f_ev(e, col0=col0):
                    for r in range(4):
                        reg = pj[r // 2][:, (r % 2) * 512:(r % 2) * 512 + 512]
                        e.activation(out=kv_nope[r][:, col0:col0 + 512],
                                     in_=reg, func=AF.Identity,
                                     bias=cst["bkvc"][:, r:r + 1])
                    e.activation(out=krz0[0:64, col0:col0 + 512],
                                 in_=pj[2][0:64, 0:512], func=AF.Identity,
                                 bias=cst["bkvc"][0:64, 4:5])
                    return e.activation(out=krz1[64:128, col0:col0 + 512],
                                        in_=pj[2][0:64, 0:512], func=AF.Identity,
                                        bias=cst["bkvc"][0:64, 4:5])
                tok_kv_ev[g] = st("scalar", f_ev,
                                  waits=[tok_kvpar[g], tok_consts], inc="cA")

        S.phase = "P1a"
        tok_qc_ev = {}
        w_ln_own = [tok_lncp[(NQT - 1) * 8 + 7]]
        last_mm = None
        mmt = {}
        for fcg in range(4):
            for dc in range(NDC):
                wsl = dc % 2
                i_lin = fcg * NDC + dc
                tok_w = st("gpsimd", lambda e, dc=dc, wsl=wsl, fcg=fcg: e.dma_start(
                    out=wp[wsl][:, 0:384],
                    in_=wqc[dc * 128:(dc + 1) * 128, fcg * 384:(fcg + 1) * 384]),
                    waits=[mmt.get(i_lin - 2,
                                   tok_kvpar[1] if i_lin < 2 else None)],
                    inc=f"dW{wsl}", inc_by=16)
                wev = [tok_qc_ev.get(fcg - 1), tok_kv_ev.get(1)] if dc == 0 else []

                def f_mm(e, dc=dc, wsl=wsl):
                    last = None
                    for fc3 in range(3):
                        for col in range(2):
                            last = e.matmul(
                                out=pj[fc3][:, col * 512:(col + 1) * 512],
                                lhsT=wp[wsl][:, fc3 * 128:(fc3 + 1) * 128],
                                rhs=ln1_own[dc][:, col * 512:(col + 1) * 512],
                                start=(dc == 0), stop=(dc == NDC - 1))
                    return last
                last_mm = st("tensor", f_mm, waits=[tok_w] + w_ln_own + wev,
                             inc="cP")
                mmt[i_lin] = last_mm

            def f_ev(e, fcg=fcg):
                last = None
                for fc3 in range(3):
                    fc = fcg * 3 + fc3
                    last = e.activation(out=q_c[fc], in_=pj[fc3][:],
                                        func=AF.Identity,
                                        bias=cst["bqc"][:, fc:fc + 1])
                return last
            tok_qc_ev[fcg] = st("scalar", f_ev, waits=[last_mm], inc="cA")

        tok_kvo_ev = {}
        mmt2 = {}
        for p in range(2):
            for dc in range(NDC):
                wsl = dc % 2
                i_lin = p * NDC + dc
                cols = (0, 384) if p == 0 else (384, 576)
                tok_w = st("gpsimd", lambda e, dc=dc, wsl=wsl, cols=cols: e.dma_start(
                    out=wp[wsl][:, 0:cols[1] - cols[0]],
                    in_=wkvc[dc * 128:(dc + 1) * 128, cols[0]:cols[1]]),
                    waits=[mmt2.get(i_lin - 2, last_mm if i_lin < 2 else None)],
                    inc=f"dW{wsl}", inc_by=16)
                wev = []
                if dc == 0:
                    wev = [tok_qc_ev[3]] if p == 0 else [tok_kvo_ev[0]]

                def f_mm(e, dc=dc, wsl=wsl, p=p):
                    last = None
                    if p == 0:
                        for c in range(3):
                            for col in range(2):
                                last = e.matmul(
                                    out=pj[c][:, col * 512:(col + 1) * 512],
                                    lhsT=wp[wsl][:, c * 128:(c + 1) * 128],
                                    rhs=ln1_own[dc][:, col * 512:(col + 1) * 512],
                                    start=(dc == 0), stop=(dc == NDC - 1))
                    else:
                        for col in range(2):
                            e.matmul(out=pj[0][:, col * 512:(col + 1) * 512],
                                     lhsT=wp[wsl][:, 0:128],
                                     rhs=ln1_own[dc][:, col * 512:(col + 1) * 512],
                                     start=(dc == 0), stop=(dc == NDC - 1))
                            last = e.matmul(
                                out=pj[1][0:64, col * 512:(col + 1) * 512],
                                lhsT=wp[wsl][:, 128:192],
                                rhs=ln1_own[dc][:, col * 512:(col + 1) * 512],
                                start=(dc == 0), stop=(dc == NDC - 1))
                    return last
                last_mm = st("tensor", f_mm, waits=[tok_w] + wev, inc="cP")
                mmt2[i_lin] = last_mm
            if p == 0:
                def f_ev(e):
                    last = None
                    for c in range(3):
                        last = e.activation(out=kv_nope[c][:, 0:TOK],
                                            in_=pj[c][:], func=AF.Identity,
                                            bias=cst["bkvc"][:, c:c + 1])
                    return last
            else:
                def f_ev(e):
                    e.activation(out=kv_nope[3][:, 0:TOK], in_=pj[0][:],
                                 func=AF.Identity, bias=cst["bkvc"][:, 3:4])
                    e.activation(out=krz0[0:64, 0:TOK],
                                 in_=pj[1][0:64, 0:1024], func=AF.Identity,
                                 bias=cst["bkvc"][0:64, 4:5])
                    return e.activation(out=krz1[64:128, 0:TOK],
                                        in_=pj[1][0:64, 0:1024], func=AF.Identity,
                                        bias=cst["bkvc"][0:64, 4:5])
            tok_kvo_ev[p] = st("scalar", f_ev, waits=[last_mm], inc="cA")
        tok_kv_done = tok_kvo_ev[1]

        if debug:
            for i in range(4):
                dump(f"kv_nope{i}", kv_nope[i][:], [128, SEQ], tok_kv_done, BF16)
            dump("krope", krz0[:], [128, SEQ], tok_kv_done, BF16)
            dump("qc0", q_c[0], [128, TOK], tok_qc_ev[0], BF16)
            dump("ln1o0", ln1_own[0][:], [128, TOK], w_ln_own[0], BF16)

        S.barrier()
        actx.close()

        S.phase = "P1b"
        qctx = ExitStack()
        q_fm = [sb(f"qfm{i}", [128, TOK], BF16, qctx) for i in range(NQF)]
        p1b_ctx = ExitStack()
        wq2 = [sb(f"wq2_{i}", [128, 512], BF16, p1b_ctx) for i in range(2)]
        p1b_p = ExitStack()
        pu = [ps(f"pu{i}", [128, 1024], F32, p1b_p) for i in range(4)]
        tok_qfm_ev = {}
        last_mm = None
        mmt3 = {}
        for fcg in range(NQF // 4):
            for qc in range(NQC):
                wsl = qc % 2
                i_lin = fcg * NQC + qc
                tok_w = st("gpsimd", lambda e, qc=qc, wsl=wsl, fcg=fcg: e.dma_start(
                    out=wq2[wsl][:], in_=wqup[qc * 128:(qc + 1) * 128,
                                              fcg * 512:(fcg + 1) * 512]),
                    waits=[mmt3.get(i_lin - 2)], inc=f"dW{wsl}", inc_by=16)
                wev = [tok_qfm_ev.get(fcg - 1)] if qc == 0 else []

                def f_mm(e, qc=qc, wsl=wsl):
                    last = None
                    for fc4 in range(4):
                        for col in range(2):
                            last = e.matmul(
                                out=pu[fc4][:, col * 512:(col + 1) * 512],
                                lhsT=wq2[wsl][:, fc4 * 128:(fc4 + 1) * 128],
                                rhs=qc_all[:, qc * TOK + col * 512:
                                           qc * TOK + (col + 1) * 512],
                                start=(qc == 0), stop=(qc == NQC - 1))
                    return last
                last_mm = st("tensor", f_mm, waits=[tok_w] + wev, inc="cP")
                mmt3[i_lin] = last_mm

            def f_ev(e, fcg=fcg):
                last = None
                for fc4 in range(4):
                    fc = fcg * 4 + fc4
                    last = e.activation(out=q_fm[fc][:], in_=pu[fc4][:],
                                        func=AF.Identity, scale=float(SQ),
                                        bias=bqup_s[:, fc:fc + 1])
                return last
            tok_qfm_ev[fcg] = st("scalar", f_ev, waits=[last_mm, tok_bqups],
                                 inc="cA")
        if debug:
            dump("qfm0", q_fm[0][:], [128, TOK], tok_qfm_ev[0], BF16)
            dump("qfm47", q_fm[47][:], [128, TOK], tok_qfm_ev[11], BF16)

        S.barrier()
        p1b_p.close()
        p1b_ctx.close()

        attn_ctx = ExitStack()
        kf = [sb(f"kf{i}", [128, SEQ], BF16, attn_ctx) for i in range(2)]
        vtg = [sb(f"vtg{i}", [128, 16 * 512], BF16, attn_ctx) for i in range(2)]
        wkh = [sb(f"wkh{i}", [128, 512], BF16, attn_ctx) for i in range(2)]
        wkvv = [sb(f"wkvv{i}", [128, 2048], BF16, attn_ctx) for i in range(2)]
        bvg = [sb(f"bvg{i}", [1, 512], BF16, attn_ctx) for i in range(2)]
        ring = qc_all[:, 4096:4096 + 3 * TOK]
        sums = sb("sums", [128, 8], F32, attn_ctx)
        sumsA = sb("sumsA", [128, 8], F32, attn_ctx)
        negm = sb("negm", [128, 8], F32, attn_ctx)
        lnS = sb("lnS", [128, 8], F32, attn_ctx)
        augc = sb("augc", [128, 8], F32, attn_ctx)
        hic_b = sb("hic_b", [128, 8], BF16, attn_ctx)
        hilo_f = sb("hilo_f", [128, 16], F32, attn_ctx)
        augMIX = qc_all[0:2, 3072:3072 + TOK]
        ones2 = sb("ones2", [2, 128], BF16, attn_ctx)
        negmA = sb("negmA", [128, 8], F32, attn_ctx)
        s1 = ps("s1", [128, SEQ], F32, attn_ctx)
        s2 = ps("s2", [128, TOK], F32, attn_ctx)
        osum = ps("osum", [128, TOK], F32, attn_ctx)

        S.phase = "ATTN"
        tok_kw = {}
        tok_kmm = {}
        tok_kfev = {}
        tok_vev = {}
        tok_exp1 = {}
        tok_exp2 = {}
        tok_avmm = {}
        tok_maxs = {}
        tok_stt = {}
        tok_augT = {}
        tok_augrow = {}
        disc = qc_all[:, 0:SEQ]
        tok_o2 = st("gpsimd", lambda e: e.memset(ones2[:], 1.0),
                    waits=[tok_krz_z], inc="cG")

        def emit_augpre(hp):
            def f_augT(e, hp=hp):
                last = None
                for qt in range(8):
                    last = e.matmul(out=s1[0:2, qt * 128:(qt + 1) * 128],
                                    lhsT=hilo_f[:, 2 * qt:2 * qt + 2],
                                    rhs=cst["identf"][:],
                                    start=True, stop=True)
                return last
            tok_augT[hp] = st("tensor", f_augT,
                              waits=[tok_stt[(hp, qt)] for qt in range(8)] +
                                    [tok_exp1[(hp, 7)]], inc="cP")
            tok_augrow[hp] = st("vector", lambda e: e.tensor_copy(
                out=augMIX[:], in_=s1[0:2, 0:TOK]), waits=[tok_augT[hp]],
                inc="cV")

        for i in range(NH + 1):
            ia = i
            ib = i - 1
            if ib >= 0:
                emit_augpre(ib)
            if ia < NH:
                g, hg = ia // GH, ia % GH
                hsl = ia % 2
                gsl = g % 2
                if hg == 0:
                    wpre = [tok_kmm.get(4 * (g - 2) + 3)] if g >= 2 else []
                    st("gpsimd", lambda e, g=g, gsl=gsl: e.dma_start(
                        out=bvg[gsl][:],
                        in_=c_in["bkvupv"][0:1, g * 512:(g + 1) * 512]),
                       waits=wpre, inc=f"dKW{gsl}", inc_by=16)
                    for kvc in range(4):
                        tok_kw[g] = st("gpsimd",
                                       lambda e, g=g, gsl=gsl, kvc=kvc: e.dma_start(
                                           out=wkvv[gsl][:, kvc * 512:(kvc + 1) * 512],
                                           in_=wkvup[kvc * 128:(kvc + 1) * 128,
                                                     4096 + g * 512:4096 + (g + 1) * 512]),
                                       waits=wpre, inc=f"dKW{gsl}", inc_by=16)
                    for rnd in range(4):
                        w_s1 = [tok_augrow.get(ib) if rnd == 0 else
                                tok_vev.get((g, rnd - 1))]
                        def f_vmm(e, g=g, gsl=gsl, rnd=rnd):
                            last = None
                            for t4 in range(4):
                                tc = rnd * 4 + t4
                                reg = s1[:, t4 * 512:(t4 + 1) * 512]
                                for kvc in range(4):
                                    e.matmul(out=reg,
                                             lhsT=kv_nope[kvc][:, tc * 128:(tc + 1) * 128],
                                             rhs=wkvv[gsl][:, kvc * 512:(kvc + 1) * 512],
                                             start=(kvc == 0), stop=False)
                                last = e.matmul(
                                    out=reg, lhsT=ones_b[:],
                                    rhs=bvg[gsl][:],
                                    start=False, stop=True)
                            return last
                        tok_vmm = st("tensor", f_vmm,
                                     waits=w_s1 + [tok_kw[g], tok_ones], inc="cP")
                        w_vtg = [tok_avmm.get((4 * (g - 2) + 3, 15))] if g >= 2 else []
                        def f_vev(e, gsl=gsl, rnd=rnd):
                            return e.activation(
                                out=vtg[gsl][:, rnd * 2048:(rnd + 1) * 2048],
                                in_=s1[:], func=AF.Copy)
                        tok_vev[(g, rnd)] = st("scalar", f_vev,
                                               waits=[tok_vmm] + w_vtg, inc="cA")
                    s1_free = tok_vev[(g, 3)]
                else:
                    s1_free = tok_augrow.get(ib)
                wkpre = [tok_kmm.get(ia - 2)]
                tok_wk = None
                for kvc in range(4):
                    tok_wk = st("gpsimd", lambda e, ia=ia, hsl=hsl, kvc=kvc: e.dma_start(
                        out=wkh[hsl][:, kvc * 128:(kvc + 1) * 128],
                        in_=wkvup[kvc * 128:(kvc + 1) * 128,
                                  ia * 128:(ia + 1) * 128]),
                        waits=wkpre, inc=f"dW{hsl}", inc_by=16)
                tok_kfh = {}
                for kh2 in range(2):
                    def f_kmm(e, hsl=hsl, kh2=kh2):
                        last = None
                        for col in range(2 * kh2, 2 * kh2 + 2):
                            for kvc in range(4):
                                last = e.matmul(
                                    out=s1[:, col * 512:(col + 1) * 512],
                                    lhsT=wkh[hsl][:, kvc * 128:(kvc + 1) * 128],
                                    rhs=kv_nope[kvc][:, col * 512:(col + 1) * 512],
                                    start=(kvc == 0), stop=(kvc == 3))
                        return last
                    tok_kmm[ia] = st("tensor", f_kmm,
                                     waits=[s1_free if kh2 == 0 else None,
                                            tok_wk], inc="cP")
                    def f_kev(e, hsl=hsl, ia=ia, kh2=kh2):
                        return e.activation(
                            out=kf[hsl][:, kh2 * TOK:(kh2 + 1) * TOK],
                            in_=s1[:, kh2 * TOK:(kh2 + 1) * TOK],
                            func=AF.Identity, bias=cst["bkvupk"][:, ia:ia + 1])
                    tok_kfh[kh2] = st("scalar", f_kev, waits=[tok_kmm[ia]],
                                      inc="cA")
                tok_kfev[ia] = tok_kfh[1]
                tok_kfev0 = tok_kfh[0]
                nope_t = q_fm[ia]
                rope_t = q_fm[32 + ia // 2]
                rb = (ia % 2) * 64
                tok_e1a_prev = None

            for kc in range(16):
                if ib >= 0:
                    gb, hgb = ib // GH, ib % GH
                    hslb = ib % 2
                    rbb = (ib % 2) * 64
                    rope_b = q_fm[32 + ib // 2]
                    tok_s2h = {}
                    for ch in range(2):
                        w_s2 = [tok_exp2.get((ib, kc - 1, ch)) if kc > 0 else
                                tok_exp2.get((ib - 1, 15, ch))]
                        krzb = krz0 if ib % 2 == 0 else krz1
                        def f_s2(e, ib=ib, kc=kc, hslb=hslb,
                                 rope_b=rope_b, col=ch, krzb=krzb):
                            reg = s2[:, col * 512:(col + 1) * 512]
                            e.matmul(out=reg,
                                     lhsT=kf[hslb][:, kc * 128:(kc + 1) * 128],
                                     rhs=q_fm[ib][:, col * 512:(col + 1) * 512],
                                     start=True, stop=False)
                            e.matmul(out=reg,
                                     lhsT=krzb[:, kc * 128:(kc + 1) * 128],
                                     rhs=rope_b[:, col * 512:(col + 1) * 512],
                                     start=False, stop=False)
                            return e.matmul(
                                out=reg, lhsT=ones2[:],
                                rhs=augMIX[:, col * 512:(col + 1) * 512],
                                start=False, stop=True)
                        tok_s2h[ch] = st("tensor", f_s2,
                                         waits=w_s2 + [tok_augrow[ib]], inc="cP")
                    if kc >= 1:
                        def f_av(e, ib=ib, kc=kc, hgb=hgb, gb=gb):
                            last = None
                            rsl = (kc - 1) % 3
                            for col in range(2):
                                last = e.matmul(
                                    out=osum[:, col * 512:(col + 1) * 512],
                                    lhsT=vtg[gb % 2][:, (kc - 1) * 512 + hgb * 128:
                                                     (kc - 1) * 512 + (hgb + 1) * 128],
                                    rhs=ring[:, rsl * TOK + col * 512:
                                             rsl * TOK + (col + 1) * 512],
                                    start=(ib == 0 and kc == 1),
                                    stop=False)
                            return last
                        tok_avmm[(ib, kc - 1)] = st(
                            "tensor", f_av,
                            waits=[tok_exp2[(ib, kc - 1, 1)], tok_vev[(gb, 3)]],
                            inc="cP")
                    rsl = kc % 3
                    prev_kc = kc - 3 if kc >= 3 else 15 - ((15 - kc) % 3)
                    w_ring = [tok_avmm.get((ib, prev_kc)) if kc >= 3 else
                              (tok_avmm.get((ib - 1, prev_kc)) if ib > 0 else None)]
                    for ch in range(2):
                        tok_exp2[(ib, kc, ch)] = st(
                            "scalar", lambda e, rsl=rsl, ch=ch: e.activation(
                                out=ring[:, rsl * TOK + ch * 512:
                                         rsl * TOK + (ch + 1) * 512],
                                in_=s2[:, ch * 512:(ch + 1) * 512],
                                func=AF.Exp),
                            waits=[tok_s2h[ch]] + w_ring, inc="cA")
                if ia < NH and kc % 2 == 0:
                    qt = kc // 2
                    w_s1 = [tok_kfev0 if qt == 0 else tok_e1a_prev]
                    krz = krz0 if ia % 2 == 0 else krz1
                    def f_s1a(e, qt=qt, hsl=hsl, nope_t=nope_t,
                              rope_t=rope_t, krz=krz):
                        last = None
                        for col in range(2):
                            e.matmul(out=s1[:, col * 512:(col + 1) * 512],
                                     lhsT=nope_t[:, qt * 128:(qt + 1) * 128],
                                     rhs=kf[hsl][:, col * 512:(col + 1) * 512],
                                     start=True, stop=False)
                            last = e.matmul(
                                out=s1[:, col * 512:(col + 1) * 512],
                                lhsT=rope_t[:, qt * 128:(qt + 1) * 128],
                                rhs=krz[:, col * 512:(col + 1) * 512],
                                start=False, stop=True)
                        return last
                    tok_s1a = st("tensor", f_s1a, waits=w_s1, inc="cP")
                    def f_s1b(e, qt=qt, hsl=hsl, nope_t=nope_t,
                              rope_t=rope_t, krz=krz):
                        last = None
                        for col in range(2, 4):
                            e.matmul(out=s1[:, col * 512:(col + 1) * 512],
                                     lhsT=nope_t[:, qt * 128:(qt + 1) * 128],
                                     rhs=kf[hsl][:, col * 512:(col + 1) * 512],
                                     start=True, stop=False)
                            last = e.matmul(
                                out=s1[:, col * 512:(col + 1) * 512],
                                lhsT=rope_t[:, qt * 128:(qt + 1) * 128],
                                rhs=krz[:, col * 512:(col + 1) * 512],
                                start=False, stop=True)
                        return last
                    tok_s1b = st("tensor", f_s1b,
                                 waits=[tok_kfev[ia] if qt == 0 else
                                        tok_exp1[(ia, qt - 1)]], inc="cP")
                    w_negm = [tok_exp1.get((ia - 1, qt))] if ia > 0 else []
                    t_mA = st("vector", lambda e, qt=qt: e.reduce_max(
                        out=negmA[:, qt:qt + 1], in_=s1[:, 0:TOK], axis=AX.X,
                        negate=True), waits=[tok_s1a] + w_negm, inc="cV")
                    t_mB = st("vector", lambda e, qt=qt: e.reduce_max(
                        out=negm[:, qt:qt + 1], in_=s1[:, TOK:SEQ], axis=AX.X,
                        negate=True), waits=[tok_s1b] + w_negm, inc="cV")
                    tok_maxs[(ia, qt)] = st(
                        "vector", lambda e, qt=qt: e.tensor_tensor(
                            out=negm[:, qt:qt + 1], in0=negm[:, qt:qt + 1],
                            in1=negmA[:, qt:qt + 1], op=ALU.min),
                        waits=[t_mB], inc="cV")
                    tok_e1a = st(
                        "scalar", lambda e, qt=qt: e.activation(
                            out=disc[:, 0:TOK], in_=s1[:, 0:TOK], func=AF.Exp,
                            bias=negm[:, qt:qt + 1],
                            accum_out=sumsA[:, qt:qt + 1]),
                        waits=[tok_maxs[(ia, qt)]], inc="cA")
                    tok_exp1[(ia, qt)] = st(
                        "scalar", lambda e, qt=qt: e.activation(
                            out=disc[:, TOK:SEQ], in_=s1[:, TOK:SEQ],
                            func=AF.Exp, bias=negm[:, qt:qt + 1],
                            accum_out=sums[:, qt:qt + 1]),
                        waits=[tok_e1a], inc="cA")
                    tok_e1a_prev = tok_e1a
                    tok_sad = st("vector", lambda e, qt=qt: e.tensor_tensor(
                        out=sums[:, qt:qt + 1], in0=sums[:, qt:qt + 1],
                        in1=sumsA[:, qt:qt + 1], op=ALU.add),
                        waits=[tok_exp1[(ia, qt)]], inc="cV")
                    tok_lns = st("scalar", lambda e, qt=qt: e.activation(
                        out=lnS[:, qt:qt + 1], in_=sums[:, qt:qt + 1],
                        func=AF.Ln), waits=[tok_sad], inc="cA")
                    t_st = st("vector", lambda e, qt=qt: e.scalar_tensor_tensor(
                        out=augc[:, qt:qt + 1], in0=lnS[:, qt:qt + 1],
                        scalar=-1.0, in1=negm[:, qt:qt + 1],
                        op0=ALU.mult, op1=ALU.add),
                        waits=[tok_lns, tok_maxs[(ia, qt)],
                               tok_augT.get(ia - 1)], inc="cV")
                    t_hb = st("vector", lambda e, qt=qt: e.tensor_copy(
                        out=hic_b[:, qt:qt + 1], in_=augc[:, qt:qt + 1]),
                        waits=[t_st], inc="cV")
                    t_hf = st("vector", lambda e, qt=qt: e.tensor_copy(
                        out=hilo_f[:, 2 * qt:2 * qt + 1],
                        in_=hic_b[:, qt:qt + 1]), waits=[t_hb], inc="cV")
                    tok_stt[(ia, qt)] = st(
                        "vector", lambda e, qt=qt: e.tensor_tensor(
                            out=hilo_f[:, 2 * qt + 1:2 * qt + 2],
                            in0=augc[:, qt:qt + 1],
                            in1=hilo_f[:, 2 * qt:2 * qt + 1],
                            op=ALU.subtract), waits=[t_hf], inc="cV")
            if ib >= 0:
                gb, hgb = ib // GH, ib % GH
                def f_av15(e, ib=ib, hgb=hgb, gb=gb):
                    last = None
                    rsl = 15 % 3
                    for col in range(2):
                        last = e.matmul(
                            out=osum[:, col * 512:(col + 1) * 512],
                            lhsT=vtg[gb % 2][:, 15 * 512 + hgb * 128:
                                             15 * 512 + (hgb + 1) * 128],
                            rhs=ring[:, rsl * TOK + col * 512:
                                     rsl * TOK + (col + 1) * 512],
                            start=False,
                            stop=(ib == NH - 1))
                    return last
                tok_avmm[(ib, 15)] = st(
                    "tensor", f_av15,
                    waits=[tok_exp2[(ib, 15, 1)], tok_vev[(gb, 3)]], inc="cP")
        last_av = tok_avmm[(NH - 1, 15)]

        tok_on = st("vector", lambda e: e.tensor_copy(out=o_n[:], in_=osum[:]),
                    waits=[last_av], inc="cV")
        if debug:
            dump("o_n", o_n[:], [128, TOK], tok_on, BF16)
            dump("augrow", augMIX[0:1, :], [1, TOK], tok_augrow[NH - 1], BF16)

        S.barrier()
        attn_ctx.close()
        qctx.close()
        bctx.close()
        kv_ctx.close()

        S.phase = "W"
        ln2_fm = sb("ln2_fm", [128, NDC * TOK], BF16, wctx4)
        wctx = ExitStack()
        wo_sb = sb("wo_sb", [128, DM], BF16, wctx)
        xa_fm = [sb(f"xafm{i}", [128, TOK], BF16, wctx) for i in range(2)]
        xa_tok = sb("xa_tok", [128, NQT * DM], BF16, wctx)
        xo = sb("xo", [128, DM], F32, wctx)
        x1t = sb("x1t", [128, DM], F32, wctx)
        lnA = sb("lnA", [128, DM], BF16, wctx)
        ln2_tok = sb("ln2_tok", [128, DM], BF16, wctx)
        stat6w = sb("stat6w", [128, 48], F32, wctx)
        mvw = sb("mvw", [128, 2], F32, wctx)
        rstdw = sb("rstdw", [128, 1], F32, wctx)
        lnvw = sb("lnvw", [128, 1], F32, wctx)
        wops = [ps(f"wops{i}", [128, TOK], F32, wctx) for i in range(2)]
        tpsW = [ps(f"tpsW{i}", [128, TOK], BF16, wctx) for i in range(2)]

        tok_wo = st("gpsimd", lambda e: e.dma_start(out=wo_sb[:], in_=wo),
                    inc="dKW0", inc_by=16)
        tok_wo_ev = {}
        tok_tcp = {}
        for Hc in range(NHC):
            wsl = Hc % 2
            w_ps = [tok_wo_ev.get(Hc - 2)]

            def f_wmm(e, Hc=Hc, wsl=wsl):
                last = None
                for col in range(2):
                    last = e.matmul(
                        out=wops[wsl][:, col * 512:(col + 1) * 512],
                        lhsT=wo_sb[:, Hc * 128:(Hc + 1) * 128],
                        rhs=o_n[:, col * 512:(col + 1) * 512],
                        start=True, stop=True)
                return last
            tok_wmm = st("tensor", f_wmm, waits=[tok_wo, tok_on] + w_ps,
                         inc="cP")
            w_fm = [tok_tcp.get(Hc - 2)]
            tok_ev = st("scalar", lambda e, Hc=Hc, wsl=wsl: e.activation(
                out=xa_fm[wsl][:], in_=wops[wsl][:], func=AF.Identity,
                bias=cst["bo"][:, Hc:Hc + 1]), waits=[tok_wmm] + w_fm, inc="cA")
            tok_wo_ev[Hc] = tok_ev

            def f_tw(e, wsl=wsl):
                last = None
                for qt in range(8):
                    last = e.transpose(out=tpsW[wsl][:, qt * 128:(qt + 1) * 128],
                                       in_=xa_fm[wsl][:, qt * 128:(qt + 1) * 128],
                                       identity=cst["identb"][:])
                return last
            tok_tw = st("tensor", f_tw, waits=[tok_ev] + [tok_tcp.get(Hc - 2)],
                        inc="cP")

            def f_tc(e, Hc=Hc, wsl=wsl):
                src = tpsW[wsl][:].rearrange("p (q c) -> p q c", q=8)
                dst = xa_tok[:].rearrange("p (q d) -> p q d", q=8)[
                    :, :, Hc * 128:(Hc + 1) * 128]
                return e.tensor_copy(out=dst, in_=src)
            tok_tcp[Hc] = st("vector", f_tc, waits=[tok_tw], inc="cV")

        tok_xa_done = tok_tcp[NHC - 1]
        if debug:
            dump("xa_tok", xa_tok[:, 0:DM], [128, DM], tok_xa_done, BF16)

        tok_ln2cp = {}
        tok_x1_lastread = {}
        for qt in range(NQT):
            xa = xa_tok[:, qt * DM:(qt + 1) * DM]
            tok_xod = st("sync", lambda e, qt=qt: e.dma_start(
                out=xo[:], in_=x_seq[qt * 128:(qt + 1) * 128, :]),
                waits=[tok_x1_lastread.get(qt - 1)], inc="dXO", inc_by=16)

            def f_st(e, xa=xa):
                last = None
                for i in range(8):
                    last = e.bn_stats(out=stat6w[:, 6 * i:6 * i + 6],
                                      in_=xa[:, 512 * i:512 * (i + 1)])
                return last
            tok_sw8 = st("vector", f_st, waits=[tok_xa_done], inc="cV")
            tok_sw = st("vector", lambda e: e.bn_aggr(
                out=mvw[:], in_=stat6w[:]), waits=[tok_sw8], inc="cV")
            tok_lnw = st("scalar", lambda e: e.activation(
                out=lnvw[:], in_=mvw[:, 1:2], func=AF.Ln, bias=epsT[:]),
                waits=[tok_sw], inc="cA")
            tok_rsw = st("scalar", lambda e: e.activation(
                out=rstdw[:], in_=lnvw[:], func=AF.Exp, scale=-0.5),
                waits=[tok_lnw], inc="cA")
            tok_nw = st("vector", lambda e, xa=xa: e.tensor_scalar(
                out=lnA[:], in0=xa, scalar1=mvw[:, 0:1], scalar2=rstdw[:],
                op0=ALU.subtract, op1=ALU.mult), waits=[tok_rsw], inc="cV")
            tok_x1 = st("vector", lambda e: e.tensor_tensor(
                out=x1t[:], in0=lnA[:], in1=xo[:], op=ALU.add),
                waits=[tok_nw, tok_xod, ("dX1W", 16 * qt)], inc="cV")
            st("sync", lambda e, qt=qt: e.dma_start(
                out=x1_dram[qt * 128:(qt + 1) * 128, :], in_=x1t[:]),
                waits=[tok_x1], inc="dX1W", inc_by=16)

            def f_st2(e):
                last = None
                for i in range(8):
                    last = e.bn_stats(out=stat6w[:, 6 * i:6 * i + 6],
                                      in_=x1t[:, 512 * i:512 * (i + 1)])
                return last
            tok_sw2a = st("vector", f_st2, waits=[tok_x1], inc="cV")
            tok_sw2 = st("vector", lambda e: e.bn_aggr(
                out=mvw[:], in_=stat6w[:]), waits=[tok_sw2a], inc="cV")
            tok_lnw2 = st("scalar", lambda e: e.activation(
                out=lnvw[:], in_=mvw[:, 1:2], func=AF.Ln, bias=epsT[:]),
                waits=[tok_sw2], inc="cA")
            tok_rsw2 = st("scalar", lambda e: e.activation(
                out=rstdw[:], in_=lnvw[:], func=AF.Exp, scale=-0.5),
                waits=[tok_lnw2], inc="cA")
            tok_nw2 = st("vector", lambda e: e.tensor_scalar(
                out=ln2_tok[:], in0=x1t[:], scalar1=mvw[:, 0:1],
                scalar2=rstdw[:], op0=ALU.subtract, op1=ALU.mult),
                waits=[tok_rsw2], inc="cV")
            tok_x1_lastread[qt] = tok_nw2
            for grp in range(8):
                pp = grp % 2
                w_tp = [tok_ln2cp.get((qt, grp - 2)) if grp >= 2 else
                        (tok_ln2cp.get((qt - 1, 6 + grp)) if qt > 0 else None)]

                def f_t2(e, pp=pp, grp=grp):
                    last = None
                    for i in range(4):
                        dc = grp * 4 + i
                        last = e.transpose(
                            out=tpsW[pp][:, i * 128:(i + 1) * 128],
                            in_=ln2_tok[:, dc * 128:(dc + 1) * 128],
                            identity=cst["identb"][:])
                    return last
                tok_t2 = st("tensor", f_t2, waits=[tok_nw2] + w_tp, inc="cP")

                def f_c2(e, pp=pp, grp=grp, qt=qt):
                    src = tpsW[pp][:, 0:512].rearrange("p (c f) -> p c f", c=4)
                    dst = ln2_fm[:].rearrange("p (d t) -> p d t", d=NDC)[
                        :, grp * 4:grp * 4 + 4, qt * 128:(qt + 1) * 128]
                    return e.tensor_copy(out=dst, in_=src)
                tok_ln2cp[(qt, grp)] = st("vector", f_c2, waits=[tok_t2],
                                          inc="cV")
        tok_ln2_done = tok_ln2cp[(NQT - 1, 7)]
        if debug:
            dump("x1_7", x1t[:], [128, DM], tok_x1_lastread[NQT - 1], F32)
            dump("ln2fm", ln2_fm[:, 0:TOK], [128, TOK], tok_ln2_done, BF16)

        S.barrier()
        wctx.close()

        S.phase = "F1"
        fctx = ExitStack()
        wgt = [sb(f"wgt{i}", [128, NDC * 128], BF16, fctx) for i in range(2)]
        wut = [sb(f"wut{i}", [128, NDC * 128], BF16, fctx) for i in range(2)]
        tmpg = [sb(f"tmpg{i}", [128, TOK], F32, fctx) for i in range(2)]
        tmpu = [sb(f"tmpu{i}", [128, TOK], F32, fctx) for i in range(2)]
        htile = [sb(f"htile{i}", [128, TOK], BF16, fctx) for i in range(2)]
        psg = [ps(f"psg{i}", [128, TOK], F32, fctx) for i in range(2)]
        psu2 = [ps(f"psu2{i}", [128, TOK], F32, fctx) for i in range(2)]
        tok_gev = {}
        tok_uev = {}
        tok_hmul = {}
        tok_f1mm = {}
        for fb in range(NFB):
            sl = fb % 2
            w_w = [tok_f1mm.get(fb - 2)]
            tok_wgd = st("gpsimd", lambda e, fb=fb, sl=sl: e.dma_start(
                out=wgt[sl][:], in_=wg[fb]), waits=w_w, inc=f"dWG{sl}",
                inc_by=16)
            tok_wud = st("gpsimd", lambda e, fb=fb, sl=sl: e.dma_start(
                out=wut[sl][:], in_=wu[fb]), waits=w_w, inc=f"dWU{sl}",
                inc_by=16)
            w_ps = [tok_gev.get(fb - 2), tok_uev.get(fb - 2)]

            def f_mm(e, sl=sl):
                last = None
                for dc in range(NDC):
                    for col in range(2):
                        e.matmul(out=psg[sl][:, col * 512:(col + 1) * 512],
                                 lhsT=wgt[sl][:, dc * 128:(dc + 1) * 128],
                                 rhs=ln2_fm[:, dc * TOK + col * 512:
                                            dc * TOK + (col + 1) * 512],
                                 start=(dc == 0), stop=(dc == NDC - 1))
                        last = e.matmul(
                            out=psu2[sl][:, col * 512:(col + 1) * 512],
                            lhsT=wut[sl][:, dc * 128:(dc + 1) * 128],
                            rhs=ln2_fm[:, dc * TOK + col * 512:
                                       dc * TOK + (col + 1) * 512],
                            start=(dc == 0), stop=(dc == NDC - 1))
                return last
            tok_f1mm[fb] = st("tensor", f_mm,
                              waits=[tok_wgd, tok_wud, tok_ln2_done] + w_ps,
                              inc="cP")
            w_tmp = [tok_hmul.get(fb - 2)]
            tok_g = st("scalar", lambda e, fb=fb, sl=sl: e.activation(
                out=tmpg[sl][:], in_=psg[sl][:], func=AF.Silu,
                scale=cst["gs"][:, fb:fb + 1]),
                waits=[tok_f1mm[fb]] + w_tmp, inc="cA")
            tok_gev[fb] = tok_g
            tok_u = st("vector", lambda e, fb=fb, sl=sl: e.tensor_scalar(
                out=tmpu[sl][:], in0=psu2[sl][:],
                scalar1=cst["us"][:, fb:fb + 1], scalar2=None, op0=ALU.mult),
                waits=[tok_f1mm[fb]] + w_tmp, inc="cV")
            tok_uev[fb] = tok_u
            hsem = f"dHW{sl}"
            w_h = [(hsem, 16 * (fb // 2))] if fb >= 2 else []
            tok_m = st("vector", lambda e, sl=sl: e.tensor_tensor(
                out=htile[sl][:], in0=tmpg[sl][:], in1=tmpu[sl][:],
                op=ALU.mult), waits=[tok_g, tok_u] + w_h, inc="cV")
            tok_hmul[fb] = tok_m
            st("sync", lambda e, fb=fb, sl=sl: e.dma_start(
                out=h_dram[fb], in_=htile[sl][:]), waits=[tok_m], inc=hsem,
                inc_by=16)
        tok_hw_all = [("dHW0", 16 * 43), ("dHW1", 16 * 43)]

        S.barrier()
        fctx.close()
        wctx4.close()

        S.phase = "F2"
        f2ctx = ExitStack()
        hh = sb("hh", [128, NFB * 512], BF16, f2ctx)
        wdt = [sb(f"wdt{i}", [128, NFB * 128], BF16, f2ctx) for i in range(2)]
        ff_fm = [sb(f"fffm{i}", [128, 512], BF16, f2ctx) for i in range(2)]
        ffn_tok = sb("ffn_tok", [128, 4 * DM], BF16, f2ctx)
        x1r = sb("x1r", [128, DM], F32, f2ctx)
        yt = sb("yt", [128, DM], F32, f2ctx)
        lnF = sb("lnF", [128, DM], BF16, f2ctx)
        stat6f = sb("stat6f", [128, 48], F32, f2ctx)
        mvf = sb("mvf", [128, 2], F32, f2ctx)
        rstdf = sb("rstdf", [128, 1], F32, f2ctx)
        lnvf = sb("lnvf", [128, 1], F32, f2ctx)
        dps = [ps(f"dps{i}", [128, 512], F32, f2ctx) for i in range(4)]
        tpsF = [ps(f"tpsF{i}", [128, 512], BF16, f2ctx) for i in range(2)]

        tok_y = {}
        tok_dmm = {}
        for half in range(2):
            w_hh = list(tok_hw_all)
            if half == 1:
                w_hh.append(tok_dmm[(0, NHC - 1)])

            fb0 = 0
            for n in [11] * 7 + [9]:
                def f_hh(e, half=half, fb0=fb0, n=n):
                    s_ = h_dram[fb0:fb0 + n, :, half * 512:(half + 1) * 512]
                    s_ = s_.rearrange("f p c -> p f c")
                    dst = hh[:].rearrange("p (f c) -> p f c", f=NFB)[
                        :, fb0:fb0 + n, :]
                    return e.dma_start(out=dst, in_=s_)
                tok_hh = st("sync", f_hh, waits=w_hh, inc="dHH", inc_by=16)
                fb0 += n
            tok_dev = {}
            tok_fcp = {}
            for j in range(NHC):
                sl = j % 2
                w_wd = [tok_dmm.get((half, j - 2)) if j >= 2 else
                        (tok_dmm.get((0, NHC - 2 + j)) if half == 1 else None)]
                tok_wdd = st("gpsimd", lambda e, j=j, sl=sl: e.dma_start(
                    out=wdt[sl][:], in_=wd[j]), waits=w_wd, inc=f"dWD{sl}",
                    inc_by=16)
                w_ps = [tok_dev.get(j - 4)]

                def f_dmm(e, j=j, sl=sl):
                    last = None
                    for fc in range(NFB):
                        last = e.matmul(
                            out=dps[j % 4][:],
                            lhsT=wdt[sl][:, fc * 128:(fc + 1) * 128],
                            rhs=hh[:, fc * 512:(fc + 1) * 512],
                            start=(fc == 0), stop=(fc == NFB - 1))
                    return last
                tok_dmm[(half, j)] = st("tensor", f_dmm,
                                        waits=[tok_wdd, ("dHH", 128 * (half + 1))] + w_ps,
                                        inc="cP")
                w_ff = [tok_fcp.get(j - 2)]
                tok_de = st("vector", lambda e, j=j, sl=sl: e.tensor_scalar(
                    out=ff_fm[sl][:], in0=dps[j % 4][:],
                    scalar1=cst["ds"][:, j:j + 1], scalar2=None, op0=ALU.mult),
                    waits=[tok_dmm[(half, j)]] + w_ff, inc="cV")
                tok_dev[j] = tok_de

                def f_tf(e, sl=sl):
                    last = None
                    for tt in range(4):
                        last = e.transpose(
                            out=tpsF[sl][:, tt * 128:(tt + 1) * 128],
                            in_=ff_fm[sl][:, tt * 128:(tt + 1) * 128],
                            identity=cst["identb"][:])
                    return last
                tok_tf = st("tensor", f_tf,
                            waits=[tok_de, tok_fcp.get(j - 2)], inc="cP")

                def f_cf(e, j=j, sl=sl):
                    src = tpsF[sl][:, 0:512].rearrange("p (t c) -> p t c", t=4)
                    dst = ffn_tok[:].rearrange("p (t d) -> p t d", t=4)[
                        :, :, j * 128:(j + 1) * 128]
                    return e.tensor_copy(out=dst, in_=src)
                tok_fcp[j] = st("vector", f_cf, waits=[tok_tf], inc="cV")
            tok_ffn_done = tok_fcp[NHC - 1]

            for tt in range(4):
                qt = half * 4 + tt
                ft = ffn_tok[:, tt * DM:(tt + 1) * DM]
                tok_x1r = st("sync", lambda e, qt=qt: e.dma_start(
                    out=x1r[:], in_=x1_dram[qt * 128:(qt + 1) * 128, :]),
                    waits=[tok_y.get(qt - 1), ("dX1W", 16 * NQT)],
                    inc="dX1R", inc_by=16)

                def f_sf(e, ft=ft):
                    last = None
                    for i in range(8):
                        last = e.bn_stats(out=stat6f[:, 6 * i:6 * i + 6],
                                          in_=ft[:, 512 * i:512 * (i + 1)])
                    return last
                tok_sf8 = st("vector", f_sf, waits=[tok_ffn_done], inc="cV")
                tok_sf = st("vector", lambda e: e.bn_aggr(
                    out=mvf[:], in_=stat6f[:]), waits=[tok_sf8], inc="cV")
                tok_lf = st("scalar", lambda e: e.activation(
                    out=lnvf[:], in_=mvf[:, 1:2], func=AF.Ln, bias=epsT[:]),
                    waits=[tok_sf], inc="cA")
                tok_rf = st("scalar", lambda e: e.activation(
                    out=rstdf[:], in_=lnvf[:], func=AF.Exp, scale=-0.5),
                    waits=[tok_lf], inc="cA")
                tok_nf = st("vector", lambda e, ft=ft: e.tensor_scalar(
                    out=lnF[:], in0=ft, scalar1=mvf[:, 0:1], scalar2=rstdf[:],
                    op0=ALU.subtract, op1=ALU.mult), waits=[tok_rf], inc="cV")
                tok_yt = st("vector", lambda e: e.tensor_tensor(
                    out=yt[:], in0=lnF[:], in1=x1r[:], op=ALU.add),
                    waits=[tok_nf, tok_x1r, ("dY", 16 * qt)], inc="cV")
                tok_y[qt] = st("sync", lambda e, qt=qt: e.dma_start(
                    out=y[qt * 128:(qt + 1) * 128, :], in_=yt[:]),
                    waits=[tok_yt], inc="dY", inc_by=16)

        S.q["sync"].append((None, [("dY", 16 * NQT)], None, 0, S.phase))
        if debug and dbg:
            S.q["sync"].append((None, [("dDBG", S.sem_n["dDBG"])], None, 0,
                                S.phase))

        block = octx.enter_context(nc.Block())
        S.build(block)
        f2ctx.close()
    return nc, dbg


_NC_CACHE = {}


def get_nc(debug=False):
    if debug not in _NC_CACHE:
        _NC_CACHE[debug] = build_nc(debug)
    return _NC_CACHE[debug]


def prep_inputs(L0_attn_hidden_size, L0_attn_Wq_c, L0_attn_bq_c, L0_attn_Wq_up,
                L0_attn_bq_up, L0_attn_Wkv_c, L0_attn_bkv_c, L0_attn_Wkv_up,
                L0_attn_bkv_up, L0_attn_Wo, L0_attn_bo, L0_ffn_gate_w,
                L0_ffn_gate_s, L0_ffn_up_w, L0_ffn_up_s, L0_ffn_down_w,
                L0_ffn_down_s):
    x = np.asarray(L0_attn_hidden_size, np.float32)

    def colsT(v, n):
        return np.ascontiguousarray(np.asarray(v, np.float32).reshape(n, 128).T)

    perm_q = np.concatenate(
        [(np.arange(32)[:, None] * 192 + np.arange(128)[None, :]).ravel(),
         (np.arange(32)[:, None] * 192 + 128 + np.arange(64)[None, :]).ravel()])
    perm_kv = np.concatenate(
        [(np.arange(32)[:, None] * 256 + np.arange(128)[None, :]).ravel(),
         (np.arange(32)[:, None] * 256 + 128 + np.arange(128)[None, :]).ravel()])

    wqc_t = np.ascontiguousarray(np.asarray(L0_attn_Wq_c, np.float32).T)
    wqup_t = np.ascontiguousarray(np.asarray(L0_attn_Wq_up, np.float32)[perm_q].T)
    wkvc_t = np.ascontiguousarray(np.asarray(L0_attn_Wkv_c, np.float32).T)
    wkvup_t = np.ascontiguousarray(
        np.asarray(L0_attn_Wkv_up, np.float32)[perm_kv].T)
    wo_t = np.ascontiguousarray(np.asarray(L0_attn_Wo, np.float32).T)

    bqc = colsT(L0_attn_bq_c, 12)
    bqup = colsT(np.asarray(L0_attn_bq_up, np.float32)[perm_q], 48)
    bkvc_pad = np.zeros(640, np.float32)
    bkvc_pad[:576] = np.asarray(L0_attn_bkv_c, np.float32)
    bkvc = colsT(bkvc_pad, 5)
    bkvup_p = np.asarray(L0_attn_bkv_up, np.float32)[perm_kv]
    bkvupk = colsT(bkvup_p[:4096], 32)
    bkvupv = np.ascontiguousarray(bkvup_p[4096:].reshape(1, 4096))
    bo = colsT(L0_attn_bo, 32)
    gs = colsT(L0_ffn_gate_s, 86)
    us = colsT(L0_ffn_up_s, 86)
    ds = colsT(L0_ffn_down_s, 32)

    gw = np.asarray(L0_ffn_gate_w, np.int32)
    uw = np.asarray(L0_ffn_up_w, np.int32)
    dw = np.asarray(L0_ffn_down_w, np.int32)

    def tile_gu(w):
        wt = np.ascontiguousarray(w.T)
        v = wt.reshape(NDC, 128, NFB, 128)
        return np.ascontiguousarray(
            v.transpose(2, 1, 0, 3).reshape(NFB, 128, NDC * 128))
    wg_t = tile_gu(gw)
    wu_t = tile_gu(uw)
    dwt = np.ascontiguousarray(dw.T)
    v = dwt.reshape(NFB, 128, NHC, 128)
    wd_t = np.ascontiguousarray(
        v.transpose(2, 1, 0, 3).reshape(NHC, 128, NFB * 128))

    identb = np.eye(128).astype(ml_dtypes.bfloat16)
    identf = np.eye(128, dtype=np.float32)

    shared = dict(wqc=wqc_t, wqup=wqup_t, wkvc=wkvc_t, wkvup=wkvup_t, wo=wo_t,
                  bqc=bqc, bqup=bqup, bkvc=bkvc, bkvupk=bkvupk, bkvupv=bkvupv,
                  bo=bo, gs=gs, us=us, ds=ds, identb=identb, identf=identf,
                  wg=wg_t, wu=wu_t, wd=wd_t)
    in_maps = []
    for c in range(8):
        b, hf = c // 2, c % 2
        xs = np.concatenate([x[b, hf * 1024:(hf + 1) * 1024],
                             x[b, (1 - hf) * 1024:(2 - hf) * 1024]], axis=0)
        m = dict(shared)
        m["x_seq"] = np.ascontiguousarray(xs)
        in_maps.append(m)
    return in_maps


def kernel(**inputs):
    in_maps = prep_inputs(**inputs)
    nc, _ = get_nc(debug=False)
    res = run_bass_kernel_spmd(nc, in_maps, core_ids=list(range(8)))
    out = np.empty((4, 2048, 4096), np.float32)
    for c in range(8):
        b, hf = c // 2, c % 2
        out[b, hf * 1024:(hf + 1) * 1024] = res.results[c]["y"]
    return out


# version 08
# speedup vs baseline: 1.0769x; 1.0208x over previous
import numpy as np
import ml_dtypes
from contextlib import ExitStack

import concourse.bass as bass
import concourse.mybir as mybir
from concourse.bass_utils import run_bass_kernel_spmd

F32 = mybir.dt.float32
BF16 = mybir.dt.bfloat16
I32 = mybir.dt.int32
AF = mybir.ActivationFunctionType
ALU = mybir.AluOpType
AX = mybir.AxisListType

DM = 4096
NDC = 32
TOK = 1024
NQT = 8
NTS = 16
SEQ = 2048
QL, NQC = 1536, 12
QF, NQF = 6144, 48
KVL = 512
NH, GH = 32, 4
DFF, NFB = 11008, 86
NHC = 32
SQ = 1.0 / np.sqrt(192.0)
TRUNC = 0
EPS = 1e-5

ENGS = ("sync", "scalar", "vector", "tensor", "gpsimd")
ESEM = {"scalar": "cA", "vector": "cV", "tensor": "cP", "gpsimd": "cG"}


class Sched:
    def __init__(self, nc, ctx):
        self.nc = nc
        self.ctx = ctx
        self.q = {e: [] for e in ENGS}
        self.sem_h = {}
        self.sem_n = {}
        self.waited = {e: {} for e in ENGS}
        self.nbar = 0
        self.phase = "init"

    def sem(self, *names):
        for name in names:
            self.sem_h[name] = self.ctx.enter_context(self.nc.semaphore(name))
            self.sem_n[name] = 0

    def step(self, eng, fn, waits=(), inc=None, inc_by=1):
        own = ESEM.get(eng)
        wl = []
        for w in waits:
            if w is None:
                continue
            sn, v = w
            if sn == own:
                assert v <= self.sem_n[sn], f"self-wait ahead on {sn}: {v} > {self.sem_n[sn]}"
            prev = self.waited[eng].get(sn, 0)
            if v > prev:
                wl.append((sn, v))
                self.waited[eng][sn] = v
        tok = None
        if inc is not None:
            self.sem_n[inc] += inc_by
            tok = (inc, self.sem_n[inc])
        self.q[eng].append((fn, wl, inc, inc_by, self.phase))
        return tok

    def barrier(self):
        self.nbar += 1
        target = 5 * self.nbar
        for e in ENGS:
            self.sem_n["bar"] += 1
            self.q[e].append((lambda eng: eng.nop(), [], "bar", 1, self.phase))
        for e in ENGS:
            self.q[e].append((None, [("bar", target)], None, 0, self.phase))
            self.waited[e]["bar"] = target

    def build(self, block):
        for e in ENGS:
            ops = self.q[e]

            def mk(ops):
                def body(eng):
                    cur = None
                    sid = None
                    for fn, wl, inc, inc_by, ph in ops:
                        if ph != cur:
                            if cur is not None:
                                self.nc.leave_named_scope(cur, sid, False)
                            sid, _ = self.nc.enter_named_scope(ph, False)
                            cur = ph
                        for sn, v in wl:
                            eng.wait_ge(self.sem_h[sn], v)
                        if fn is None:
                            continue
                        inst = fn(eng)
                        if inc is not None:
                            inst.then_inc(self.sem_h[inc], inc_by)
                    if cur is not None:
                        self.nc.leave_named_scope(cur, sid, False)
                return body
            getattr(block, e)(mk(ops))


def build_nc(debug=False):
    nc = bass.Bass("TRN2", target_bir_lowering=False)
    x_seq = nc.dram_tensor("x_seq", [SEQ, DM], F32, kind="ExternalInput").ap()
    wqc = nc.dram_tensor("wqc", [4, NDC * 128, 384], F32,
                         kind="ExternalInput").ap()
    wqup = nc.dram_tensor("wqup", [12, NQC * 128, 512], F32,
                          kind="ExternalInput").ap()
    wkvc = nc.dram_tensor("wkvc", [NDC * 128, 576], F32,
                          kind="ExternalInput").ap()
    wkvup = nc.dram_tensor("wkvup", [KVL, 8192], F32, kind="ExternalInput").ap()
    wo = nc.dram_tensor("wo", [128, DM], F32, kind="ExternalInput").ap()
    c_in = {}
    for name, sh in [("bqc", [128, 12]), ("bqup", [128, 48]), ("bkvc", [128, 5]),
                     ("bkvupk", [128, 32]), ("bkvupv", [1, 4096]), ("bo", [128, 32]),
                     ("gs", [128, 86]), ("us", [128, 86]), ("ds", [128, 32]),
                     ("identb", [128, 128]), ("identf", [128, 128])]:
        dt = BF16 if name == "identb" else F32
        c_in[name] = nc.dram_tensor(name, sh, dt, kind="ExternalInput").ap()
    wg = nc.dram_tensor("wg", [NFB, 128, NDC * 128], I32, kind="ExternalInput").ap()
    wu = nc.dram_tensor("wu", [NFB, 128, NDC * 128], I32, kind="ExternalInput").ap()
    wd = nc.dram_tensor("wd", [NHC, 128, NFB * 128], I32, kind="ExternalInput").ap()
    y = nc.dram_tensor("y", [TOK, DM], F32, kind="ExternalOutput").ap()
    x1_dram = nc.dram_tensor("x1_dram", [TOK, DM], F32, kind="Internal").ap()
    h_dram = nc.dram_tensor("h_dram", [NFB, 128, TOK], BF16, kind="Internal").ap()
    dbg = {}

    def dbg_out(name, shape, dtype):
        if debug and name not in dbg:
            dbg[name] = nc.dram_tensor("dbg_" + name, shape, dtype,
                                       kind="ExternalOutput").ap()
        return dbg.get(name)

    with ExitStack() as octx:
        S = Sched(nc, octx)
        S.sem("bar", "cA", "cV", "cP", "cG", "dC",
              "dX0", "dX1", "dW0", "dW1", "dKW0", "dKW1",
              "dXO", "dX1W", "dHW0", "dHW1", "dHH",
              "dWG0", "dWG1", "dWU0", "dWU1", "dWD0", "dWD1",
              "dX1R", "dY", "dDBG")
        st = S.step

        def sb(name, shape, dtype, ctx):
            return ctx.enter_context(nc.sbuf_tensor(name, shape, dtype))

        def ps(name, shape, dtype, ctx):
            return ctx.enter_context(nc.psum_tensor(name, shape, dtype))

        def dump(name, ap_src, shape, after, dtype):
            o = dbg_out(name, shape, dtype)
            if o is None:
                return
            st("sync", lambda e, o=o, a=ap_src: e.dma_start(out=o, in_=a),
               waits=[after], inc="dDBG", inc_by=16)

        wctx4 = ExitStack()
        kv_ctx = ExitStack()
        _attn_only = {"bqc", "bqup", "bkvc", "bkvupk", "identf"}
        epsT = sb("epsT", [128, 1], F32, octx)
        cst = {}
        for name in c_in:
            if name not in _attn_only and name != "bkvupv":
                dt = BF16 if name == "identb" else F32
                cst[name] = sb("c_" + name, list(c_in[name].tensor.shape), dt, octx)
        o_n = sb("o_n", [128, TOK], BF16, wctx4)
        for name in c_in:
            if name in _attn_only:
                cst[name] = sb("c_" + name, list(c_in[name].tensor.shape), F32,
                               kv_ctx)
        n_c = 0
        for name in c_in:
            if name == "bkvupv":
                continue
            n_c += 1
            st("sync", lambda e, name=name: e.dma_start(out=cst[name][:],
                                                        in_=c_in[name]),
               inc="dC", inc_by=16)
        tok_consts = ("dC", 16 * n_c)
        ones1 = sb("ones1", [1, 128], F32, kv_ctx)
        ones_b = sb("ones_b", [1, 128], BF16, kv_ctx)
        tok_ones = st("gpsimd", lambda e: e.memset(ones1[:], 1.0), inc="cG")
        tok_ones = st("gpsimd", lambda e: e.memset(ones_b[:], 1.0),
                      waits=[tok_ones], inc="cG")
        tok_eps = st("gpsimd", lambda e: e.memset(epsT[:], float(EPS)),
                     waits=[tok_ones], inc="cG")
        st("gpsimd", lambda e: e.memset(krz0[64:128, :], 0.0),
           waits=[tok_eps], inc="cG")
        tok_krz_z = st("gpsimd", lambda e: e.memset(krz1[0:64, :], 0.0),
                       inc="cG")
        bqup_s = sb("bqup_s", [128, 48], F32, kv_ctx)
        tok_bqups = st("vector", lambda e: e.tensor_scalar(
            out=bqup_s[:], in0=cst["bqup"][:], scalar1=float(SQ), scalar2=None,
            op0=ALU.mult), waits=[tok_consts], inc="cV")
        kv_nope = [sb(f"kv_nope{i}", [128, SEQ], BF16, kv_ctx)
                   for i in range(4)]
        krz0 = sb("krz0", [128, SEQ], BF16, kv_ctx)
        krz1 = sb("krz1", [128, SEQ], BF16, kv_ctx)

        bctx = ExitStack()
        qc_all = sb("qc_all", [128, NQC * TOK], BF16, bctx)
        q_c = [qc_all[:, i * TOK:(i + 1) * TOK] for i in range(NQC)]
        actx = ExitStack()
        wkvc_sb = sb("wkvc_sb", [128, NDC * 576], BF16, actx)
        wqc_sb = [sb(f"wqc_sb{i}", [128, 8 * 384], BF16, actx)
                  for i in range(2)]
        xt = [sb("xt0", [128, DM], BF16, actx)] * 2
        ln1_own = [sb(f"ln1o{i}", [128, TOK], BF16, actx) for i in range(NDC)]
        ln1_par = [sb(f"ln1p{i}", [128, 256], BF16, actx) for i in range(NDC)]
        ln_tok = [sb("lntok0", [128, DM], BF16, actx)] * 2
        stat6 = [sb(f"stat6{i}", [128, 48], F32, actx) for i in range(2)]
        mv = [sb(f"mv{i}", [128, 2], F32, actx) for i in range(2)]
        rstd = [sb(f"rstd{i}", [128, 1], F32, actx) for i in range(2)]
        lnv = [sb(f"lnv{i}", [128, 1], F32, actx) for i in range(2)]
        tpsA = [ps(f"tpsA{i}", [128, 512], BF16, actx) for i in range(2)]
        pj = [ps(f"pj{i}", [128, 1024], F32, actx) for i in range(3)]

        S.phase = "P0"
        for kq in range(4):
            def f_wkvc(e, kq=kq):
                s_ = wkvc[kq * 1024:(kq + 1) * 1024, :]
                s_ = s_.rearrange("(d p) c -> p d c", p=128)
                d_ = wkvc_sb[:, kq * 8 * 576:(kq + 1) * 8 * 576]
                d_ = d_.rearrange("p (d c) -> p d c", d=8)
                return e.dma_start(out=d_, in_=s_)
            tok_wkvc = st("gpsimd", f_wkvc, inc="dW0", inc_by=16)
        lastx = {}
        tok_trp = {}
        tok_lncp = {}
        tok_kvpar = {}
        tok_kv_ev = {}

        for tau in range(NTS):
            sl = tau % 2
            w = [lastx.get(0), lastx.get(1)]
            tok_x = st("gpsimd", lambda e, sl=sl, tau=tau: e.dma_start(
                out=xt[sl][:], in_=x_seq[tau * 128:(tau + 1) * 128, :]),
                waits=w, inc=f"dX{sl}", inc_by=16)

            def f_stats(e, sl=sl):
                last = None
                for i in range(8):
                    last = e.bn_stats(out=stat6[sl][:, 6 * i:6 * i + 6],
                                      in_=xt[sl][:, 512 * i:512 * (i + 1)])
                return last
            tok_st8 = st("vector", f_stats, waits=[tok_x], inc="cV")
            tok_stats = st("vector", lambda e, sl=sl: e.bn_aggr(
                out=mv[sl][:], in_=stat6[sl][:]), waits=[tok_st8], inc="cV")
            tok_ln = st("scalar", lambda e, sl=sl: e.activation(
                out=lnv[sl][:], in_=mv[sl][:, 1:2], func=AF.Ln, bias=epsT[:]),
                waits=[tok_stats, tok_eps], inc="cA")
            tok_rstd = st("scalar", lambda e, sl=sl: e.activation(
                out=rstd[sl][:], in_=lnv[sl][:], func=AF.Exp, scale=-0.5),
                waits=[tok_ln], inc="cA")
            w_lt = [tok_trp.get((tau - 1, 7))]
            tok_nrm = st("vector", lambda e, sl=sl: e.tensor_scalar(
                out=ln_tok[sl][:], in0=xt[sl][:], scalar1=mv[sl][:, 0:1],
                scalar2=rstd[sl][:], op0=ALU.subtract, op1=ALU.mult),
                waits=[tok_rstd] + w_lt, inc="cV")
            lastx[sl] = tok_nrm

            for grp in range(8):
                pp = (tau * 8 + grp) % 2
                prev_cp = tok_lncp.get(tau * 8 + grp - 2)

                def f_tr(e, sl=sl, grp=grp, pp=pp):
                    last = None
                    for i in range(4):
                        dc = grp * 4 + i
                        last = e.transpose(out=tpsA[pp][:, 128 * i:128 * (i + 1)],
                                           in_=ln_tok[sl][:, 128 * dc:128 * (dc + 1)],
                                           identity=cst["identb"][:])
                    return last
                tok_tr = st("tensor", f_tr, waits=[tok_nrm, prev_cp, tok_consts],
                            inc="cP")
                tok_trp[(tau, grp)] = tok_tr
                if tau < NQT:
                    def f_cp(e, tau=tau, grp=grp, pp=pp):
                        last = None
                        for i in range(4):
                            dc = grp * 4 + i
                            last = e.tensor_copy(
                                out=ln1_own[dc][:, 128 * tau:128 * (tau + 1)],
                                in_=tpsA[pp][:, 128 * i:128 * (i + 1)])
                        return last
                else:
                    def f_cp(e, tau=tau, grp=grp, pp=pp):
                        col = ((tau - NQT) % 2) * 128
                        last = None
                        for i in range(4):
                            dc = grp * 4 + i
                            last = e.tensor_copy(
                                out=ln1_par[dc][:, col:col + 128],
                                in_=tpsA[pp][:, 128 * i:128 * (i + 1)])
                        return last
                wkv = [tok_kvpar.get((tau - NQT) // 2 - 1)] if tau >= 10 else []
                tok_lncp[tau * 8 + grp] = st("vector", f_cp,
                                             waits=[tok_tr] + wkv, inc="cV")

            if tau in (9, 11, 13, 15):
                g = (tau - 9) // 2
                w_ln = [tok_lncp[tau * 8 + 7]]
                tok_mm_d = {}
                for dc in range(NDC):
                    def f_mm(e, dc=dc):
                        w0 = wkvc_sb[:, dc * 576:(dc + 1) * 576]
                        for r in range(4):
                            reg = pj[r // 2][:, (r % 2) * 512:(r % 2) * 512 + 256]
                            e.matmul(out=reg,
                                     lhsT=w0[:, r * 128:(r + 1) * 128],
                                     rhs=ln1_par[dc][:],
                                     start=(dc == 0), stop=(dc == NDC - 1))
                        return e.matmul(out=pj[2][0:64, 0:256],
                                        lhsT=w0[:, 512:576],
                                        rhs=ln1_par[dc][:],
                                        start=(dc == 0), stop=(dc == NDC - 1))
                    tok_mm_d[dc] = st("tensor", f_mm,
                                      waits=[tok_wkvc] + w_ln +
                                            ([tok_kv_ev.get(g - 1)] if dc == 0 else []),
                                      inc="cP")
                tok_kvpar[g] = tok_mm_d[NDC - 1]
                col0 = TOK + 256 * g

                def f_ev(e, col0=col0):
                    for r in range(4):
                        reg = pj[r // 2][:, (r % 2) * 512:(r % 2) * 512 + 256]
                        e.activation(out=kv_nope[r][:, col0:col0 + 256],
                                     in_=reg, func=AF.Identity,
                                     bias=cst["bkvc"][:, r:r + 1])
                    e.activation(out=krz0[0:64, col0:col0 + 256],
                                 in_=pj[2][0:64, 0:256], func=AF.Identity,
                                 bias=cst["bkvc"][0:64, 4:5])
                    return e.activation(out=krz1[64:128, col0:col0 + 256],
                                        in_=pj[2][0:64, 0:256], func=AF.Identity,
                                        bias=cst["bkvc"][0:64, 4:5])
                tok_kv_ev[g] = st("scalar", f_ev,
                                  waits=[tok_kvpar[g], tok_consts], inc="cA")

        if TRUNC == 3:
            S.q["sync"].append((None, [("cA", S.sem_n["cA"]),
                                       ("cV", S.sem_n["cV"]),
                                       ("cP", S.sem_n["cP"])], None, 0, S.phase))
            block = octx.enter_context(nc.Block())
            S.build(block)
            actx.close(); qctx_dummy = None
            bctx.close(); kv_ctx.close(); wctx4.close()
            return nc, dbg
        S.phase = "P1a"
        tok_qc_ev = {}
        w_ln_own = [tok_lncp[(NQT - 1) * 8 + 7]]
        last_mm = None
        tok_wqcg = {}
        tok_qcmm_blk = {}
        for blk in range(16):
            fcg, qr = blk // 4, blk % 4
            wsl = blk % 2
            wg_pre = [tok_qcmm_blk.get(blk - 2)]
            def f_wqc(e, fcg=fcg, qr=qr, wsl=wsl):
                s_ = wqc[fcg][qr * 1024:(qr + 1) * 1024, :]
                s_ = s_.rearrange("(d p) c -> p d c", p=128)
                d_ = wqc_sb[wsl][:].rearrange("p (d c) -> p d c", d=8)
                return e.dma_start(out=d_, in_=s_)
            tok_wqcg[blk] = st("gpsimd", f_wqc, waits=wg_pre,
                               inc=f"dW{wsl}", inc_by=16)
            for dc8 in range(8):
                dc = qr * 8 + dc8
                wev = ([tok_qc_ev.get(fcg - 1), tok_kv_ev.get(3)]
                       if dc == 0 else [])
                def f_mm(e, dc=dc, dc8=dc8, wsl=wsl):
                    last = None
                    w0 = wqc_sb[wsl][:, dc8 * 384:(dc8 + 1) * 384]
                    for fc3 in range(3):
                        for col in range(2):
                            last = e.matmul(
                                out=pj[fc3][:, col * 512:(col + 1) * 512],
                                lhsT=w0[:, fc3 * 128:(fc3 + 1) * 128],
                                rhs=ln1_own[dc][:, col * 512:(col + 1) * 512],
                                start=(dc == 0), stop=(dc == NDC - 1))
                    return last
                last_mm = st("tensor", f_mm,
                             waits=[tok_wqcg[blk]] + w_ln_own + wev, inc="cP")
            tok_qcmm_blk[blk] = last_mm
            if qr == 3:
                def f_ev(e, fcg=fcg):
                    last = None
                    for fc3 in range(3):
                        fc = fcg * 3 + fc3
                        last = e.activation(out=q_c[fc], in_=pj[fc3][:],
                                            func=AF.Identity,
                                            bias=cst["bqc"][:, fc:fc + 1])
                    return last
                tok_qc_ev[fcg] = st("scalar", f_ev, waits=[last_mm], inc="cA")

        if TRUNC == 4:
            S.q["sync"].append((None, [("cA", S.sem_n["cA"]),
                                       ("cV", S.sem_n["cV"]),
                                       ("cP", S.sem_n["cP"])], None, 0, S.phase))
            block = octx.enter_context(nc.Block())
            S.build(block)
            actx.close()
            bctx.close(); kv_ctx.close(); wctx4.close()
            return nc, dbg
        tok_kvo_ev = {}
        for p in range(2):
            for dc in range(NDC):
                wev = []
                if dc == 0:
                    wev = [tok_qc_ev[3]] if p == 0 else [tok_kvo_ev[0]]
                def f_mm(e, dc=dc, p=p):
                    w0 = wkvc_sb[:, dc * 576:(dc + 1) * 576]
                    last = None
                    if p == 0:
                        for c in range(3):
                            for col in range(2):
                                last = e.matmul(
                                    out=pj[c][:, col * 512:(col + 1) * 512],
                                    lhsT=w0[:, c * 128:(c + 1) * 128],
                                    rhs=ln1_own[dc][:, col * 512:(col + 1) * 512],
                                    start=(dc == 0), stop=(dc == NDC - 1))
                    else:
                        for col in range(2):
                            e.matmul(out=pj[0][:, col * 512:(col + 1) * 512],
                                     lhsT=w0[:, 384:512],
                                     rhs=ln1_own[dc][:, col * 512:(col + 1) * 512],
                                     start=(dc == 0), stop=(dc == NDC - 1))
                            last = e.matmul(
                                out=pj[1][0:64, col * 512:(col + 1) * 512],
                                lhsT=w0[:, 512:576],
                                rhs=ln1_own[dc][:, col * 512:(col + 1) * 512],
                                start=(dc == 0), stop=(dc == NDC - 1))
                    return last
                last_mm = st("tensor", f_mm, waits=[tok_wkvc] + wev, inc="cP")
            if p == 0:
                def f_ev(e):
                    last = None
                    for c in range(3):
                        last = e.activation(out=kv_nope[c][:, 0:TOK],
                                            in_=pj[c][:], func=AF.Identity,
                                            bias=cst["bkvc"][:, c:c + 1])
                    return last
            else:
                def f_ev(e):
                    e.activation(out=kv_nope[3][:, 0:TOK], in_=pj[0][:],
                                 func=AF.Identity, bias=cst["bkvc"][:, 3:4])
                    e.activation(out=krz0[0:64, 0:TOK],
                                 in_=pj[1][0:64, 0:1024], func=AF.Identity,
                                 bias=cst["bkvc"][0:64, 4:5])
                    return e.activation(out=krz1[64:128, 0:TOK],
                                        in_=pj[1][0:64, 0:1024], func=AF.Identity,
                                        bias=cst["bkvc"][0:64, 4:5])
            tok_kvo_ev[p] = st("scalar", f_ev, waits=[last_mm], inc="cA")
        if TRUNC == 5:
            S.q["sync"].append((None, [("cA", S.sem_n["cA"]),
                                       ("cV", S.sem_n["cV"]),
                                       ("cP", S.sem_n["cP"])], None, 0, S.phase))
            block = octx.enter_context(nc.Block())
            S.build(block)
            actx.close()
            bctx.close(); kv_ctx.close(); wctx4.close()
            return nc, dbg
        tok_kv_done = tok_kvo_ev[1]

        if debug:
            for i in range(4):
                dump(f"kv_nope{i}", kv_nope[i][:], [128, SEQ], tok_kv_done, BF16)
            dump("krope", krz0[:], [128, SEQ], tok_kv_done, BF16)
            dump("qc0", q_c[0], [128, TOK], tok_qc_ev[0], BF16)
            dump("ln1o0", ln1_own[0][:], [128, TOK], w_ln_own[0], BF16)

        S.barrier()
        actx.close()

        S.phase = "P1b"
        qctx = ExitStack()
        q_fm = [sb(f"qfm{i}", [128, TOK], BF16, qctx) for i in range(NQF)]
        p1b_ctx = ExitStack()
        wq2 = [sb(f"wq2_{i}", [128, NQC * 512], BF16, p1b_ctx)
               for i in range(2)]
        p1b_p = ExitStack()
        pu = [ps(f"pu{i}", [128, 1024], F32, p1b_p) for i in range(4)]
        tok_qfm_ev = {}
        last_mm = None
        tok_wq2 = {}
        for fcg in range(NQF // 4):
            wsl = fcg % 2
            wg_pre = [tok_qfm_ev.get(fcg - 2)]
            def f_w(e, fcg=fcg, wsl=wsl):
                s_ = wqup[fcg].rearrange("(q p) c -> p q c", p=128)
                d_ = wq2[wsl][:].rearrange("p (q c) -> p q c", q=NQC)
                return e.dma_start(out=d_, in_=s_)
            tok_wq2[fcg] = st("gpsimd", f_w, waits=wg_pre,
                              inc=f"dW{wsl}", inc_by=16)
            for qc in range(NQC):
                wev = [tok_qfm_ev.get(fcg - 1)] if qc == 0 else []
                def f_mm(e, qc=qc, wsl=wsl):
                    last = None
                    w0 = wq2[wsl][:, qc * 512:(qc + 1) * 512]
                    for fc4 in range(4):
                        for col in range(2):
                            last = e.matmul(
                                out=pu[fc4][:, col * 512:(col + 1) * 512],
                                lhsT=w0[:, fc4 * 128:(fc4 + 1) * 128],
                                rhs=qc_all[:, qc * TOK + col * 512:
                                           qc * TOK + (col + 1) * 512],
                                start=(qc == 0), stop=(qc == NQC - 1))
                    return last
                last_mm = st("tensor", f_mm, waits=[tok_wq2[fcg]] + wev,
                             inc="cP")

            def f_ev(e, fcg=fcg):
                last = None
                for fc4 in range(4):
                    fc = fcg * 4 + fc4
                    last = e.activation(out=q_fm[fc][:], in_=pu[fc4][:],
                                        func=AF.Identity, scale=float(SQ),
                                        bias=bqup_s[:, fc:fc + 1])
                return last
            tok_qfm_ev[fcg] = st("scalar", f_ev, waits=[last_mm, tok_bqups],
                                 inc="cA")
        if debug:
            dump("qfm0", q_fm[0][:], [128, TOK], tok_qfm_ev[0], BF16)
            dump("qfm47", q_fm[47][:], [128, TOK], tok_qfm_ev[11], BF16)

        S.barrier()
        p1b_p.close()
        p1b_ctx.close()

        if TRUNC == 1:
            block = octx.enter_context(nc.Block())
            S.build(block)
            qctx.close(); bctx.close(); kv_ctx.close(); wctx4.close()
            return nc, dbg
        attn_ctx = ExitStack()
        kf = [sb(f"kf{i}", [128, SEQ], BF16, attn_ctx) for i in range(2)]
        vtg = [sb(f"vtg{i}", [128, 16 * 512], BF16, attn_ctx) for i in range(2)]
        wkh = [sb(f"wkh{i}", [128, 512], BF16, attn_ctx) for i in range(2)]
        wkvv = [sb(f"wkvv{i}", [128, 2048], BF16, attn_ctx) for i in range(2)]
        bvg = [sb(f"bvg{i}", [1, 512], BF16, attn_ctx) for i in range(2)]
        ring = qc_all[:, 4096:4096 + 3 * TOK]
        sums = sb("sums", [128, 8], F32, attn_ctx)
        sumsA = sb("sumsA", [128, 8], F32, attn_ctx)
        negm = sb("negm", [128, 8], F32, attn_ctx)
        lnS = sb("lnS", [128, 8], F32, attn_ctx)
        augc = sb("augc", [128, 8], F32, attn_ctx)
        hic_b = sb("hic_b", [128, 8], BF16, attn_ctx)
        hilo_f = sb("hilo_f", [128, 16], F32, attn_ctx)
        augMIX = qc_all[0:2, 3072:3072 + TOK]
        ones2 = sb("ones2", [2, 128], BF16, attn_ctx)
        negmA = sb("negmA", [128, 8], F32, attn_ctx)
        s1 = ps("s1", [128, SEQ], F32, attn_ctx)
        s2 = ps("s2", [128, TOK], F32, attn_ctx)
        osum = ps("osum", [128, TOK], F32, attn_ctx)

        S.phase = "ATTN"
        tok_kw = {}
        tok_kmm = {}
        tok_kfev = {}
        tok_vev = {}
        tok_exp1 = {}
        tok_exp2 = {}
        tok_avmm = {}
        tok_maxs = {}
        tok_stt = {}
        tok_augT = {}
        tok_augrow = {}
        disc = qc_all[:, 0:SEQ]
        tok_o2 = st("gpsimd", lambda e: e.memset(ones2[:], 1.0),
                    waits=[tok_krz_z], inc="cG")

        def emit_augpre(hp):
            def f_augT(e, hp=hp):
                last = None
                for qt in range(8):
                    last = e.matmul(out=s1[0:2, qt * 128:(qt + 1) * 128],
                                    lhsT=hilo_f[:, 2 * qt:2 * qt + 2],
                                    rhs=cst["identf"][:],
                                    start=True, stop=True)
                return last
            tok_augT[hp] = st("tensor", f_augT,
                              waits=[tok_stt[(hp, qt)] for qt in range(8)] +
                                    [tok_exp1[(hp, 7)]], inc="cP")
            tok_augrow[hp] = st("vector", lambda e: e.tensor_copy(
                out=augMIX[:], in_=s1[0:2, 0:TOK]), waits=[tok_augT[hp]],
                inc="cV")

        for i in range(NH + 1):
            ia = i
            ib = i - 1
            if ib >= 0:
                emit_augpre(ib)
            if ia < NH:
                g, hg = ia // GH, ia % GH
                hsl = ia % 2
                gsl = g % 2
                if hg == 0:
                    wpre = [tok_kmm.get(4 * (g - 2) + 3)] if g >= 2 else []
                    st("gpsimd", lambda e, g=g, gsl=gsl: e.dma_start(
                        out=bvg[gsl][:],
                        in_=c_in["bkvupv"][0:1, g * 512:(g + 1) * 512]),
                       waits=wpre, inc=f"dKW{gsl}", inc_by=16)
                    for kvc in range(4):
                        tok_kw[g] = st("gpsimd",
                                       lambda e, g=g, gsl=gsl, kvc=kvc: e.dma_start(
                                           out=wkvv[gsl][:, kvc * 512:(kvc + 1) * 512],
                                           in_=wkvup[kvc * 128:(kvc + 1) * 128,
                                                     4096 + g * 512:4096 + (g + 1) * 512]),
                                       waits=wpre, inc=f"dKW{gsl}", inc_by=16)
                    for rnd in range(4):
                        w_s1 = [tok_augrow.get(ib) if rnd == 0 else
                                tok_vev.get((g, rnd - 1))]
                        def f_vmm(e, g=g, gsl=gsl, rnd=rnd):
                            last = None
                            for t4 in range(4):
                                tc = rnd * 4 + t4
                                reg = s1[:, t4 * 512:(t4 + 1) * 512]
                                for kvc in range(4):
                                    e.matmul(out=reg,
                                             lhsT=kv_nope[kvc][:, tc * 128:(tc + 1) * 128],
                                             rhs=wkvv[gsl][:, kvc * 512:(kvc + 1) * 512],
                                             start=(kvc == 0), stop=False)
                                last = e.matmul(
                                    out=reg, lhsT=ones_b[:],
                                    rhs=bvg[gsl][:],
                                    start=False, stop=True)
                            return last
                        tok_vmm = st("tensor", f_vmm,
                                     waits=w_s1 + [tok_kw[g], tok_ones], inc="cP")
                        w_vtg = [tok_avmm.get((4 * (g - 2) + 3, 15))] if g >= 2 else []
                        def f_vev(e, gsl=gsl, rnd=rnd):
                            return e.activation(
                                out=vtg[gsl][:, rnd * 2048:(rnd + 1) * 2048],
                                in_=s1[:], func=AF.Copy)
                        tok_vev[(g, rnd)] = st("scalar", f_vev,
                                               waits=[tok_vmm] + w_vtg, inc="cA")
                    s1_free = tok_vev[(g, 3)]
                else:
                    s1_free = tok_augrow.get(ib)
                wkpre = [tok_kmm.get(ia - 2)]
                tok_wk = None
                for kvc in range(4):
                    tok_wk = st("gpsimd", lambda e, ia=ia, hsl=hsl, kvc=kvc: e.dma_start(
                        out=wkh[hsl][:, kvc * 128:(kvc + 1) * 128],
                        in_=wkvup[kvc * 128:(kvc + 1) * 128,
                                  ia * 128:(ia + 1) * 128]),
                        waits=wkpre, inc=f"dW{hsl}", inc_by=16)
                tok_kfh = {}
                for kh2 in range(2):
                    def f_kmm(e, hsl=hsl, kh2=kh2):
                        last = None
                        for col in range(2 * kh2, 2 * kh2 + 2):
                            for kvc in range(4):
                                last = e.matmul(
                                    out=s1[:, col * 512:(col + 1) * 512],
                                    lhsT=wkh[hsl][:, kvc * 128:(kvc + 1) * 128],
                                    rhs=kv_nope[kvc][:, col * 512:(col + 1) * 512],
                                    start=(kvc == 0), stop=(kvc == 3))
                        return last
                    tok_kmm[ia] = st("tensor", f_kmm,
                                     waits=[s1_free if kh2 == 0 else None,
                                            tok_wk], inc="cP")
                    def f_kev(e, hsl=hsl, ia=ia, kh2=kh2):
                        return e.activation(
                            out=kf[hsl][:, kh2 * TOK:(kh2 + 1) * TOK],
                            in_=s1[:, kh2 * TOK:(kh2 + 1) * TOK],
                            func=AF.Identity, bias=cst["bkvupk"][:, ia:ia + 1])
                    tok_kfh[kh2] = st("scalar", f_kev, waits=[tok_kmm[ia]],
                                      inc="cA")
                tok_kfev[ia] = tok_kfh[1]
                tok_kfev0 = tok_kfh[0]
                nope_t = q_fm[ia]
                rope_t = q_fm[32 + ia // 2]
                rb = (ia % 2) * 64
                tok_e1a_prev = None

            for kc in range(16):
                if ib >= 0:
                    gb, hgb = ib // GH, ib % GH
                    hslb = ib % 2
                    rbb = (ib % 2) * 64
                    rope_b = q_fm[32 + ib // 2]
                    tok_s2h = {}
                    for ch in range(2):
                        w_s2 = [tok_exp2.get((ib, kc - 1, ch)) if kc > 0 else
                                tok_exp2.get((ib - 1, 15, ch))]
                        krzb = krz0 if ib % 2 == 0 else krz1
                        def f_s2(e, ib=ib, kc=kc, hslb=hslb,
                                 rope_b=rope_b, col=ch, krzb=krzb):
                            reg = s2[:, col * 512:(col + 1) * 512]
                            e.matmul(out=reg,
                                     lhsT=kf[hslb][:, kc * 128:(kc + 1) * 128],
                                     rhs=q_fm[ib][:, col * 512:(col + 1) * 512],
                                     start=True, stop=False)
                            e.matmul(out=reg,
                                     lhsT=krzb[:, kc * 128:(kc + 1) * 128],
                                     rhs=rope_b[:, col * 512:(col + 1) * 512],
                                     start=False, stop=False)
                            return e.matmul(
                                out=reg, lhsT=ones2[:],
                                rhs=augMIX[:, col * 512:(col + 1) * 512],
                                start=False, stop=True)
                        tok_s2h[ch] = st("tensor", f_s2,
                                         waits=w_s2 + [tok_augrow[ib]], inc="cP")
                    if kc >= 1:
                        def f_av(e, ib=ib, kc=kc, hgb=hgb, gb=gb):
                            last = None
                            rsl = (kc - 1) % 3
                            for col in range(2):
                                last = e.matmul(
                                    out=osum[:, col * 512:(col + 1) * 512],
                                    lhsT=vtg[gb % 2][:, (kc - 1) * 512 + hgb * 128:
                                                     (kc - 1) * 512 + (hgb + 1) * 128],
                                    rhs=ring[:, rsl * TOK + col * 512:
                                             rsl * TOK + (col + 1) * 512],
                                    start=(ib == 0 and kc == 1),
                                    stop=False)
                            return last
                        tok_avmm[(ib, kc - 1)] = st(
                            "tensor", f_av,
                            waits=[tok_exp2[(ib, kc - 1, 1)], tok_vev[(gb, 3)]],
                            inc="cP")
                    rsl = kc % 3
                    prev_kc = kc - 3 if kc >= 3 else 15 - ((15 - kc) % 3)
                    w_ring = [tok_avmm.get((ib, prev_kc)) if kc >= 3 else
                              (tok_avmm.get((ib - 1, prev_kc)) if ib > 0 else None)]
                    for ch in range(2):
                        tok_exp2[(ib, kc, ch)] = st(
                            "scalar", lambda e, rsl=rsl, ch=ch: e.activation(
                                out=ring[:, rsl * TOK + ch * 512:
                                         rsl * TOK + (ch + 1) * 512],
                                in_=s2[:, ch * 512:(ch + 1) * 512],
                                func=AF.Exp),
                            waits=[tok_s2h[ch]] + w_ring, inc="cA")
                if ia < NH and kc % 2 == 0:
                    qt = kc // 2
                    w_s1 = [tok_kfev0 if qt == 0 else tok_e1a_prev]
                    krz = krz0 if ia % 2 == 0 else krz1
                    def f_s1a(e, qt=qt, hsl=hsl, nope_t=nope_t,
                              rope_t=rope_t, krz=krz):
                        last = None
                        for col in range(2):
                            e.matmul(out=s1[:, col * 512:(col + 1) * 512],
                                     lhsT=nope_t[:, qt * 128:(qt + 1) * 128],
                                     rhs=kf[hsl][:, col * 512:(col + 1) * 512],
                                     start=True, stop=False)
                            last = e.matmul(
                                out=s1[:, col * 512:(col + 1) * 512],
                                lhsT=rope_t[:, qt * 128:(qt + 1) * 128],
                                rhs=krz[:, col * 512:(col + 1) * 512],
                                start=False, stop=True)
                        return last
                    tok_s1a = st("tensor", f_s1a, waits=w_s1, inc="cP")
                    def f_s1b(e, qt=qt, hsl=hsl, nope_t=nope_t,
                              rope_t=rope_t, krz=krz):
                        last = None
                        for col in range(2, 4):
                            e.matmul(out=s1[:, col * 512:(col + 1) * 512],
                                     lhsT=nope_t[:, qt * 128:(qt + 1) * 128],
                                     rhs=kf[hsl][:, col * 512:(col + 1) * 512],
                                     start=True, stop=False)
                            last = e.matmul(
                                out=s1[:, col * 512:(col + 1) * 512],
                                lhsT=rope_t[:, qt * 128:(qt + 1) * 128],
                                rhs=krz[:, col * 512:(col + 1) * 512],
                                start=False, stop=True)
                        return last
                    tok_s1b = st("tensor", f_s1b,
                                 waits=[tok_kfev[ia] if qt == 0 else
                                        tok_exp1[(ia, qt - 1)]], inc="cP")
                    w_negm = [tok_exp1.get((ia - 1, qt))] if ia > 0 else []
                    t_mA = st("vector", lambda e, qt=qt: e.reduce_max(
                        out=negmA[:, qt:qt + 1], in_=s1[:, 0:TOK], axis=AX.X,
                        negate=True), waits=[tok_s1a] + w_negm, inc="cV")
                    t_mB = st("vector", lambda e, qt=qt: e.reduce_max(
                        out=negm[:, qt:qt + 1], in_=s1[:, TOK:SEQ], axis=AX.X,
                        negate=True), waits=[tok_s1b] + w_negm, inc="cV")
                    tok_maxs[(ia, qt)] = st(
                        "vector", lambda e, qt=qt: e.tensor_tensor(
                            out=negm[:, qt:qt + 1], in0=negm[:, qt:qt + 1],
                            in1=negmA[:, qt:qt + 1], op=ALU.min),
                        waits=[t_mB], inc="cV")
                    tok_e1a = st(
                        "scalar", lambda e, qt=qt: e.activation(
                            out=disc[:, 0:TOK], in_=s1[:, 0:TOK], func=AF.Exp,
                            bias=negm[:, qt:qt + 1],
                            accum_out=sumsA[:, qt:qt + 1]),
                        waits=[tok_maxs[(ia, qt)]], inc="cA")
                    tok_exp1[(ia, qt)] = st(
                        "scalar", lambda e, qt=qt: e.activation(
                            out=disc[:, TOK:SEQ], in_=s1[:, TOK:SEQ],
                            func=AF.Exp, bias=negm[:, qt:qt + 1],
                            accum_out=sums[:, qt:qt + 1]),
                        waits=[tok_e1a], inc="cA")
                    tok_e1a_prev = tok_e1a
                    tok_sad = st("vector", lambda e, qt=qt: e.tensor_tensor(
                        out=sums[:, qt:qt + 1], in0=sums[:, qt:qt + 1],
                        in1=sumsA[:, qt:qt + 1], op=ALU.add),
                        waits=[tok_exp1[(ia, qt)]], inc="cV")
                    tok_lns = st("scalar", lambda e, qt=qt: e.activation(
                        out=lnS[:, qt:qt + 1], in_=sums[:, qt:qt + 1],
                        func=AF.Ln), waits=[tok_sad], inc="cA")
                    t_st = st("vector", lambda e, qt=qt: e.scalar_tensor_tensor(
                        out=augc[:, qt:qt + 1], in0=lnS[:, qt:qt + 1],
                        scalar=-1.0, in1=negm[:, qt:qt + 1],
                        op0=ALU.mult, op1=ALU.add),
                        waits=[tok_lns, tok_maxs[(ia, qt)],
                               tok_augT.get(ia - 1)], inc="cV")
                    t_hb = st("vector", lambda e, qt=qt: e.tensor_copy(
                        out=hic_b[:, qt:qt + 1], in_=augc[:, qt:qt + 1]),
                        waits=[t_st], inc="cV")
                    t_hf = st("vector", lambda e, qt=qt: e.tensor_copy(
                        out=hilo_f[:, 2 * qt:2 * qt + 1],
                        in_=hic_b[:, qt:qt + 1]), waits=[t_hb], inc="cV")
                    tok_stt[(ia, qt)] = st(
                        "vector", lambda e, qt=qt: e.tensor_tensor(
                            out=hilo_f[:, 2 * qt + 1:2 * qt + 2],
                            in0=augc[:, qt:qt + 1],
                            in1=hilo_f[:, 2 * qt:2 * qt + 1],
                            op=ALU.subtract), waits=[t_hf], inc="cV")
            if ib >= 0:
                gb, hgb = ib // GH, ib % GH
                def f_av15(e, ib=ib, hgb=hgb, gb=gb):
                    last = None
                    rsl = 15 % 3
                    for col in range(2):
                        last = e.matmul(
                            out=osum[:, col * 512:(col + 1) * 512],
                            lhsT=vtg[gb % 2][:, 15 * 512 + hgb * 128:
                                             15 * 512 + (hgb + 1) * 128],
                            rhs=ring[:, rsl * TOK + col * 512:
                                     rsl * TOK + (col + 1) * 512],
                            start=False,
                            stop=(ib == NH - 1))
                    return last
                tok_avmm[(ib, 15)] = st(
                    "tensor", f_av15,
                    waits=[tok_exp2[(ib, 15, 1)], tok_vev[(gb, 3)]], inc="cP")
        last_av = tok_avmm[(NH - 1, 15)]

        tok_on = st("vector", lambda e: e.tensor_copy(out=o_n[:], in_=osum[:]),
                    waits=[last_av], inc="cV")
        if debug:
            dump("o_n", o_n[:], [128, TOK], tok_on, BF16)
            dump("augrow", augMIX[0:1, :], [1, TOK], tok_augrow[NH - 1], BF16)

        S.barrier()
        attn_ctx.close()
        qctx.close()
        bctx.close()
        kv_ctx.close()
        if TRUNC == 2:
            block = octx.enter_context(nc.Block())
            S.build(block)
            wctx4.close()
            return nc, dbg

        S.phase = "W"
        ln2_fm = sb("ln2_fm", [128, NDC * TOK], BF16, wctx4)
        wctx = ExitStack()
        wo_sb = sb("wo_sb", [128, DM], BF16, wctx)
        xa_fm = [sb(f"xafm{i}", [128, TOK], BF16, wctx) for i in range(2)]
        xa_tok = sb("xa_tok", [128, NQT * DM], BF16, wctx)
        xo = sb("xo", [128, DM], F32, wctx)
        x1t = sb("x1t", [128, DM], F32, wctx)
        lnA = sb("lnA", [128, DM], BF16, wctx)
        ln2_tok = sb("ln2_tok", [128, DM], BF16, wctx)
        stat6w = sb("stat6w", [128, 48], F32, wctx)
        mvw = sb("mvw", [128, 2], F32, wctx)
        rstdw = sb("rstdw", [128, 1], F32, wctx)
        lnvw = sb("lnvw", [128, 1], F32, wctx)
        wops = [ps(f"wops{i}", [128, TOK], F32, wctx) for i in range(2)]
        tpsW = [ps(f"tpsW{i}", [128, TOK], BF16, wctx) for i in range(2)]

        tok_wo = st("gpsimd", lambda e: e.dma_start(out=wo_sb[:], in_=wo),
                    inc="dKW0", inc_by=16)
        tok_wo_ev = {}
        tok_tcp = {}
        for Hc in range(NHC):
            wsl = Hc % 2
            w_ps = [tok_wo_ev.get(Hc - 2)]

            def f_wmm(e, Hc=Hc, wsl=wsl):
                last = None
                for col in range(2):
                    last = e.matmul(
                        out=wops[wsl][:, col * 512:(col + 1) * 512],
                        lhsT=wo_sb[:, Hc * 128:(Hc + 1) * 128],
                        rhs=o_n[:, col * 512:(col + 1) * 512],
                        start=True, stop=True)
                return last
            tok_wmm = st("tensor", f_wmm, waits=[tok_wo, tok_on] + w_ps,
                         inc="cP")
            w_fm = [tok_tcp.get(Hc - 2)]
            tok_ev = st("scalar", lambda e, Hc=Hc, wsl=wsl: e.activation(
                out=xa_fm[wsl][:], in_=wops[wsl][:], func=AF.Identity,
                bias=cst["bo"][:, Hc:Hc + 1]), waits=[tok_wmm] + w_fm, inc="cA")
            tok_wo_ev[Hc] = tok_ev

            def f_tw(e, wsl=wsl):
                last = None
                for qt in range(8):
                    last = e.transpose(out=tpsW[wsl][:, qt * 128:(qt + 1) * 128],
                                       in_=xa_fm[wsl][:, qt * 128:(qt + 1) * 128],
                                       identity=cst["identb"][:])
                return last
            tok_tw = st("tensor", f_tw, waits=[tok_ev] + [tok_tcp.get(Hc - 2)],
                        inc="cP")

            def f_tc(e, Hc=Hc, wsl=wsl):
                src = tpsW[wsl][:].rearrange("p (q c) -> p q c", q=8)
                dst = xa_tok[:].rearrange("p (q d) -> p q d", q=8)[
                    :, :, Hc * 128:(Hc + 1) * 128]
                return e.tensor_copy(out=dst, in_=src)
            tok_tcp[Hc] = st("vector", f_tc, waits=[tok_tw], inc="cV")

        tok_xa_done = tok_tcp[NHC - 1]
        if debug:
            dump("xa_tok", xa_tok[:, 0:DM], [128, DM], tok_xa_done, BF16)

        tok_ln2cp = {}
        tok_x1_lastread = {}
        for qt in range(NQT):
            xa = xa_tok[:, qt * DM:(qt + 1) * DM]
            tok_xod = st("sync", lambda e, qt=qt: e.dma_start(
                out=xo[:], in_=x_seq[qt * 128:(qt + 1) * 128, :]),
                waits=[tok_x1_lastread.get(qt - 1)], inc="dXO", inc_by=16)

            def f_st(e, xa=xa):
                last = None
                for i in range(8):
                    last = e.bn_stats(out=stat6w[:, 6 * i:6 * i + 6],
                                      in_=xa[:, 512 * i:512 * (i + 1)])
                return last
            tok_sw8 = st("vector", f_st, waits=[tok_xa_done], inc="cV")
            tok_sw = st("vector", lambda e: e.bn_aggr(
                out=mvw[:], in_=stat6w[:]), waits=[tok_sw8], inc="cV")
            tok_lnw = st("scalar", lambda e: e.activation(
                out=lnvw[:], in_=mvw[:, 1:2], func=AF.Ln, bias=epsT[:]),
                waits=[tok_sw], inc="cA")
            tok_rsw = st("scalar", lambda e: e.activation(
                out=rstdw[:], in_=lnvw[:], func=AF.Exp, scale=-0.5),
                waits=[tok_lnw], inc="cA")
            tok_nw = st("vector", lambda e, xa=xa: e.tensor_scalar(
                out=lnA[:], in0=xa, scalar1=mvw[:, 0:1], scalar2=rstdw[:],
                op0=ALU.subtract, op1=ALU.mult), waits=[tok_rsw], inc="cV")
            tok_x1 = st("vector", lambda e: e.tensor_tensor(
                out=x1t[:], in0=lnA[:], in1=xo[:], op=ALU.add),
                waits=[tok_nw, tok_xod, ("dX1W", 16 * qt)], inc="cV")
            st("sync", lambda e, qt=qt: e.dma_start(
                out=x1_dram[qt * 128:(qt + 1) * 128, :], in_=x1t[:]),
                waits=[tok_x1], inc="dX1W", inc_by=16)

            def f_st2(e):
                last = None
                for i in range(8):
                    last = e.bn_stats(out=stat6w[:, 6 * i:6 * i + 6],
                                      in_=x1t[:, 512 * i:512 * (i + 1)])
                return last
            tok_sw2a = st("vector", f_st2, waits=[tok_x1], inc="cV")
            tok_sw2 = st("vector", lambda e: e.bn_aggr(
                out=mvw[:], in_=stat6w[:]), waits=[tok_sw2a], inc="cV")
            tok_lnw2 = st("scalar", lambda e: e.activation(
                out=lnvw[:], in_=mvw[:, 1:2], func=AF.Ln, bias=epsT[:]),
                waits=[tok_sw2], inc="cA")
            tok_rsw2 = st("scalar", lambda e: e.activation(
                out=rstdw[:], in_=lnvw[:], func=AF.Exp, scale=-0.5),
                waits=[tok_lnw2], inc="cA")
            tok_nw2 = st("vector", lambda e: e.tensor_scalar(
                out=ln2_tok[:], in0=x1t[:], scalar1=mvw[:, 0:1],
                scalar2=rstdw[:], op0=ALU.subtract, op1=ALU.mult),
                waits=[tok_rsw2], inc="cV")
            tok_x1_lastread[qt] = tok_nw2
            for grp in range(8):
                pp = grp % 2
                w_tp = [tok_ln2cp.get((qt, grp - 2)) if grp >= 2 else
                        (tok_ln2cp.get((qt - 1, 6 + grp)) if qt > 0 else None)]

                def f_t2(e, pp=pp, grp=grp):
                    last = None
                    for i in range(4):
                        dc = grp * 4 + i
                        last = e.transpose(
                            out=tpsW[pp][:, i * 128:(i + 1) * 128],
                            in_=ln2_tok[:, dc * 128:(dc + 1) * 128],
                            identity=cst["identb"][:])
                    return last
                tok_t2 = st("tensor", f_t2, waits=[tok_nw2] + w_tp, inc="cP")

                def f_c2(e, pp=pp, grp=grp, qt=qt):
                    src = tpsW[pp][:, 0:512].rearrange("p (c f) -> p c f", c=4)
                    dst = ln2_fm[:].rearrange("p (d t) -> p d t", d=NDC)[
                        :, grp * 4:grp * 4 + 4, qt * 128:(qt + 1) * 128]
                    return e.tensor_copy(out=dst, in_=src)
                tok_ln2cp[(qt, grp)] = st("vector", f_c2, waits=[tok_t2],
                                          inc="cV")
        tok_ln2_done = tok_ln2cp[(NQT - 1, 7)]
        if debug:
            dump("x1_7", x1t[:], [128, DM], tok_x1_lastread[NQT - 1], F32)
            dump("ln2fm", ln2_fm[:, 0:TOK], [128, TOK], tok_ln2_done, BF16)

        S.barrier()
        wctx.close()

        S.phase = "F1"
        fctx = ExitStack()
        wgt = [sb(f"wgt{i}", [128, NDC * 128], BF16, fctx) for i in range(2)]
        wut = [sb(f"wut{i}", [128, NDC * 128], BF16, fctx) for i in range(2)]
        tmpg = [sb(f"tmpg{i}", [128, TOK], F32, fctx) for i in range(2)]
        tmpu = [sb(f"tmpu{i}", [128, TOK], F32, fctx) for i in range(2)]
        htile = [sb(f"htile{i}", [128, TOK], BF16, fctx) for i in range(2)]
        psg = [ps(f"psg{i}", [128, TOK], F32, fctx) for i in range(2)]
        psu2 = [ps(f"psu2{i}", [128, TOK], F32, fctx) for i in range(2)]
        tok_gev = {}
        tok_uev = {}
        tok_hmul = {}
        tok_f1mm = {}
        for fb in range(NFB):
            sl = fb % 2
            w_w = [tok_f1mm.get(fb - 2)]
            tok_wgd = st("gpsimd", lambda e, fb=fb, sl=sl: e.dma_start(
                out=wgt[sl][:], in_=wg[fb]), waits=w_w, inc=f"dWG{sl}",
                inc_by=16)
            tok_wud = st("gpsimd", lambda e, fb=fb, sl=sl: e.dma_start(
                out=wut[sl][:], in_=wu[fb]), waits=w_w, inc=f"dWU{sl}",
                inc_by=16)
            w_ps = [tok_gev.get(fb - 2), tok_uev.get(fb - 2)]

            def f_mm(e, sl=sl):
                last = None
                for dc in range(NDC):
                    for col in range(2):
                        e.matmul(out=psg[sl][:, col * 512:(col + 1) * 512],
                                 lhsT=wgt[sl][:, dc * 128:(dc + 1) * 128],
                                 rhs=ln2_fm[:, dc * TOK + col * 512:
                                            dc * TOK + (col + 1) * 512],
                                 start=(dc == 0), stop=(dc == NDC - 1))
                        last = e.matmul(
                            out=psu2[sl][:, col * 512:(col + 1) * 512],
                            lhsT=wut[sl][:, dc * 128:(dc + 1) * 128],
                            rhs=ln2_fm[:, dc * TOK + col * 512:
                                       dc * TOK + (col + 1) * 512],
                            start=(dc == 0), stop=(dc == NDC - 1))
                return last
            tok_f1mm[fb] = st("tensor", f_mm,
                              waits=[tok_wgd, tok_wud, tok_ln2_done] + w_ps,
                              inc="cP")
            w_tmp = [tok_hmul.get(fb - 2)]
            tok_g = st("scalar", lambda e, fb=fb, sl=sl: e.activation(
                out=tmpg[sl][:], in_=psg[sl][:], func=AF.Silu,
                scale=cst["gs"][:, fb:fb + 1]),
                waits=[tok_f1mm[fb]] + w_tmp, inc="cA")
            tok_gev[fb] = tok_g
            tok_u = st("vector", lambda e, fb=fb, sl=sl: e.tensor_scalar(
                out=tmpu[sl][:], in0=psu2[sl][:],
                scalar1=cst["us"][:, fb:fb + 1], scalar2=None, op0=ALU.mult),
                waits=[tok_f1mm[fb]] + w_tmp, inc="cV")
            tok_uev[fb] = tok_u
            hsem = f"dHW{sl}"
            w_h = [(hsem, 16 * (fb // 2))] if fb >= 2 else []
            tok_m = st("vector", lambda e, sl=sl: e.tensor_tensor(
                out=htile[sl][:], in0=tmpg[sl][:], in1=tmpu[sl][:],
                op=ALU.mult), waits=[tok_g, tok_u] + w_h, inc="cV")
            tok_hmul[fb] = tok_m
            st("sync", lambda e, fb=fb, sl=sl: e.dma_start(
                out=h_dram[fb], in_=htile[sl][:]), waits=[tok_m], inc=hsem,
                inc_by=16)
        tok_hw_all = [("dHW0", 16 * 43), ("dHW1", 16 * 43)]

        S.barrier()
        fctx.close()
        wctx4.close()

        S.phase = "F2"
        f2ctx = ExitStack()
        hh = sb("hh", [128, NFB * 512], BF16, f2ctx)
        wdt = [sb(f"wdt{i}", [128, NFB * 128], BF16, f2ctx) for i in range(2)]
        ff_fm = [sb(f"fffm{i}", [128, 512], BF16, f2ctx) for i in range(2)]
        ffn_tok = sb("ffn_tok", [128, 4 * DM], BF16, f2ctx)
        x1r = sb("x1r", [128, DM], F32, f2ctx)
        yt = sb("yt", [128, DM], F32, f2ctx)
        lnF = sb("lnF", [128, DM], BF16, f2ctx)
        stat6f = sb("stat6f", [128, 48], F32, f2ctx)
        mvf = sb("mvf", [128, 2], F32, f2ctx)
        rstdf = sb("rstdf", [128, 1], F32, f2ctx)
        lnvf = sb("lnvf", [128, 1], F32, f2ctx)
        dps = [ps(f"dps{i}", [128, 512], F32, f2ctx) for i in range(4)]
        tpsF = [ps(f"tpsF{i}", [128, 512], BF16, f2ctx) for i in range(2)]

        tok_y = {}
        tok_dmm = {}
        for half in range(2):
            w_hh = list(tok_hw_all)
            if half == 1:
                w_hh.append(tok_dmm[(0, NHC - 1)])

            fb0 = 0
            for n in [11] * 7 + [9]:
                def f_hh(e, half=half, fb0=fb0, n=n):
                    s_ = h_dram[fb0:fb0 + n, :, half * 512:(half + 1) * 512]
                    s_ = s_.rearrange("f p c -> p f c")
                    dst = hh[:].rearrange("p (f c) -> p f c", f=NFB)[
                        :, fb0:fb0 + n, :]
                    return e.dma_start(out=dst, in_=s_)
                tok_hh = st("sync", f_hh, waits=w_hh, inc="dHH", inc_by=16)
                fb0 += n
            tok_dev = {}
            tok_fcp = {}
            for j in range(NHC):
                sl = j % 2
                w_wd = [tok_dmm.get((half, j - 2)) if j >= 2 else
                        (tok_dmm.get((0, NHC - 2 + j)) if half == 1 else None)]
                tok_wdd = st("gpsimd", lambda e, j=j, sl=sl: e.dma_start(
                    out=wdt[sl][:], in_=wd[j]), waits=w_wd, inc=f"dWD{sl}",
                    inc_by=16)
                w_ps = [tok_dev.get(j - 4)]

                def f_dmm(e, j=j, sl=sl):
                    last = None
                    for fc in range(NFB):
                        last = e.matmul(
                            out=dps[j % 4][:],
                            lhsT=wdt[sl][:, fc * 128:(fc + 1) * 128],
                            rhs=hh[:, fc * 512:(fc + 1) * 512],
                            start=(fc == 0), stop=(fc == NFB - 1))
                    return last
                tok_dmm[(half, j)] = st("tensor", f_dmm,
                                        waits=[tok_wdd, ("dHH", 128 * (half + 1))] + w_ps,
                                        inc="cP")
                w_ff = [tok_fcp.get(j - 2)]
                tok_de = st("vector", lambda e, j=j, sl=sl: e.tensor_scalar(
                    out=ff_fm[sl][:], in0=dps[j % 4][:],
                    scalar1=cst["ds"][:, j:j + 1], scalar2=None, op0=ALU.mult),
                    waits=[tok_dmm[(half, j)]] + w_ff, inc="cV")
                tok_dev[j] = tok_de

                def f_tf(e, sl=sl):
                    last = None
                    for tt in range(4):
                        last = e.transpose(
                            out=tpsF[sl][:, tt * 128:(tt + 1) * 128],
                            in_=ff_fm[sl][:, tt * 128:(tt + 1) * 128],
                            identity=cst["identb"][:])
                    return last
                tok_tf = st("tensor", f_tf,
                            waits=[tok_de, tok_fcp.get(j - 2)], inc="cP")

                def f_cf(e, j=j, sl=sl):
                    src = tpsF[sl][:, 0:512].rearrange("p (t c) -> p t c", t=4)
                    dst = ffn_tok[:].rearrange("p (t d) -> p t d", t=4)[
                        :, :, j * 128:(j + 1) * 128]
                    return e.tensor_copy(out=dst, in_=src)
                tok_fcp[j] = st("vector", f_cf, waits=[tok_tf], inc="cV")
            tok_ffn_done = tok_fcp[NHC - 1]

            for tt in range(4):
                qt = half * 4 + tt
                ft = ffn_tok[:, tt * DM:(tt + 1) * DM]
                tok_x1r = st("sync", lambda e, qt=qt: e.dma_start(
                    out=x1r[:], in_=x1_dram[qt * 128:(qt + 1) * 128, :]),
                    waits=[tok_y.get(qt - 1), ("dX1W", 16 * NQT)],
                    inc="dX1R", inc_by=16)

                def f_sf(e, ft=ft):
                    last = None
                    for i in range(8):
                        last = e.bn_stats(out=stat6f[:, 6 * i:6 * i + 6],
                                          in_=ft[:, 512 * i:512 * (i + 1)])
                    return last
                tok_sf8 = st("vector", f_sf, waits=[tok_ffn_done], inc="cV")
                tok_sf = st("vector", lambda e: e.bn_aggr(
                    out=mvf[:], in_=stat6f[:]), waits=[tok_sf8], inc="cV")
                tok_lf = st("scalar", lambda e: e.activation(
                    out=lnvf[:], in_=mvf[:, 1:2], func=AF.Ln, bias=epsT[:]),
                    waits=[tok_sf], inc="cA")
                tok_rf = st("scalar", lambda e: e.activation(
                    out=rstdf[:], in_=lnvf[:], func=AF.Exp, scale=-0.5),
                    waits=[tok_lf], inc="cA")
                tok_nf = st("vector", lambda e, ft=ft: e.tensor_scalar(
                    out=lnF[:], in0=ft, scalar1=mvf[:, 0:1], scalar2=rstdf[:],
                    op0=ALU.subtract, op1=ALU.mult), waits=[tok_rf], inc="cV")
                tok_yt = st("vector", lambda e: e.tensor_tensor(
                    out=yt[:], in0=lnF[:], in1=x1r[:], op=ALU.add),
                    waits=[tok_nf, tok_x1r, ("dY", 16 * qt)], inc="cV")
                tok_y[qt] = st("sync", lambda e, qt=qt: e.dma_start(
                    out=y[qt * 128:(qt + 1) * 128, :], in_=yt[:]),
                    waits=[tok_yt], inc="dY", inc_by=16)

        S.q["sync"].append((None, [("dY", 16 * NQT)], None, 0, S.phase))
        if debug and dbg:
            S.q["sync"].append((None, [("dDBG", S.sem_n["dDBG"])], None, 0,
                                S.phase))

        block = octx.enter_context(nc.Block())
        S.build(block)
        f2ctx.close()
    return nc, dbg


_NC_CACHE = {}


def get_nc(debug=False):
    if debug not in _NC_CACHE:
        _NC_CACHE[debug] = build_nc(debug)
    return _NC_CACHE[debug]


def prep_inputs(L0_attn_hidden_size, L0_attn_Wq_c, L0_attn_bq_c, L0_attn_Wq_up,
                L0_attn_bq_up, L0_attn_Wkv_c, L0_attn_bkv_c, L0_attn_Wkv_up,
                L0_attn_bkv_up, L0_attn_Wo, L0_attn_bo, L0_ffn_gate_w,
                L0_ffn_gate_s, L0_ffn_up_w, L0_ffn_up_s, L0_ffn_down_w,
                L0_ffn_down_s):
    x = np.asarray(L0_attn_hidden_size, np.float32)

    def colsT(v, n):
        return np.ascontiguousarray(np.asarray(v, np.float32).reshape(n, 128).T)

    perm_q = np.concatenate(
        [(np.arange(32)[:, None] * 192 + np.arange(128)[None, :]).ravel(),
         (np.arange(32)[:, None] * 192 + 128 + np.arange(64)[None, :]).ravel()])
    perm_kv = np.concatenate(
        [(np.arange(32)[:, None] * 256 + np.arange(128)[None, :]).ravel(),
         (np.arange(32)[:, None] * 256 + 128 + np.arange(128)[None, :]).ravel()])

    wqc_T = np.asarray(L0_attn_Wq_c, np.float32).T
    wqc_t = np.ascontiguousarray(
        wqc_T.reshape(DM, 4, 384).transpose(1, 0, 2))
    wqup_T = np.asarray(L0_attn_Wq_up, np.float32)[perm_q].T
    wqup_t = np.ascontiguousarray(
        wqup_T.reshape(QL, 12, 512).transpose(1, 0, 2))
    wkvc_t = np.ascontiguousarray(np.asarray(L0_attn_Wkv_c, np.float32).T)
    wkvup_t = np.ascontiguousarray(
        np.asarray(L0_attn_Wkv_up, np.float32)[perm_kv].T)
    wo_t = np.ascontiguousarray(np.asarray(L0_attn_Wo, np.float32).T)

    bqc = colsT(L0_attn_bq_c, 12)
    bqup = colsT(np.asarray(L0_attn_bq_up, np.float32)[perm_q], 48)
    bkvc_pad = np.zeros(640, np.float32)
    bkvc_pad[:576] = np.asarray(L0_attn_bkv_c, np.float32)
    bkvc = colsT(bkvc_pad, 5)
    bkvup_p = np.asarray(L0_attn_bkv_up, np.float32)[perm_kv]
    bkvupk = colsT(bkvup_p[:4096], 32)
    bkvupv = np.ascontiguousarray(bkvup_p[4096:].reshape(1, 4096))
    bo = colsT(L0_attn_bo, 32)
    gs = colsT(L0_ffn_gate_s, 86)
    us = colsT(L0_ffn_up_s, 86)
    ds = colsT(L0_ffn_down_s, 32)

    gw = np.asarray(L0_ffn_gate_w, np.int32)
    uw = np.asarray(L0_ffn_up_w, np.int32)
    dw = np.asarray(L0_ffn_down_w, np.int32)

    def tile_gu(w):
        wt = np.ascontiguousarray(w.T)
        v = wt.reshape(NDC, 128, NFB, 128)
        return np.ascontiguousarray(
            v.transpose(2, 1, 0, 3).reshape(NFB, 128, NDC * 128))
    wg_t = tile_gu(gw)
    wu_t = tile_gu(uw)
    dwt = np.ascontiguousarray(dw.T)
    v = dwt.reshape(NFB, 128, NHC, 128)
    wd_t = np.ascontiguousarray(
        v.transpose(2, 1, 0, 3).reshape(NHC, 128, NFB * 128))

    identb = np.eye(128).astype(ml_dtypes.bfloat16)
    identf = np.eye(128, dtype=np.float32)

    shared = dict(wqc=wqc_t, wqup=wqup_t, wkvc=wkvc_t, wkvup=wkvup_t, wo=wo_t,
                  bqc=bqc, bqup=bqup, bkvc=bkvc, bkvupk=bkvupk, bkvupv=bkvupv,
                  bo=bo, gs=gs, us=us, ds=ds, identb=identb, identf=identf,
                  wg=wg_t, wu=wu_t, wd=wd_t)
    in_maps = []
    for c in range(8):
        b, hf = c // 2, c % 2
        xs = np.concatenate([x[b, hf * 1024:(hf + 1) * 1024],
                             x[b, (1 - hf) * 1024:(2 - hf) * 1024]], axis=0)
        m = dict(shared)
        m["x_seq"] = np.ascontiguousarray(xs)
        in_maps.append(m)
    return in_maps


def kernel(**inputs):
    in_maps = prep_inputs(**inputs)
    nc, _ = get_nc(debug=False)
    res = run_bass_kernel_spmd(nc, in_maps, core_ids=list(range(8)))
    out = np.empty((4, 2048, 4096), np.float32)
    for c in range(8):
        b, hf = c // 2, c % 2
        out[b, hf * 1024:(hf + 1) * 1024] = res.results[c]["y"]
    return out


# version 09
# speedup vs baseline: 1.1012x; 1.0226x over previous
import numpy as np
import ml_dtypes
from contextlib import ExitStack

import concourse.bass as bass
import concourse.mybir as mybir
from concourse.bass_utils import run_bass_kernel_spmd

F32 = mybir.dt.float32
BF16 = mybir.dt.bfloat16
I32 = mybir.dt.int32
AF = mybir.ActivationFunctionType
ALU = mybir.AluOpType
AX = mybir.AxisListType

DM = 4096
NDC = 32
TOK = 1024
NQT = 8
NTS = 16
SEQ = 2048
QL, NQC = 1536, 12
QF, NQF = 6144, 48
KVL = 512
NH, GH = 32, 4
DFF, NFB = 11008, 86
NHC = 32
SQ = 1.0 / np.sqrt(192.0)
TRUNC = 0
EPS = 1e-5

ENGS = ("sync", "scalar", "vector", "tensor", "gpsimd")
ESEM = {"scalar": "cA", "vector": "cV", "tensor": "cP", "gpsimd": "cG"}


class Sched:
    def __init__(self, nc, ctx):
        self.nc = nc
        self.ctx = ctx
        self.q = {e: [] for e in ENGS}
        self.sem_h = {}
        self.sem_n = {}
        self.waited = {e: {} for e in ENGS}
        self.nbar = 0
        self.phase = "init"

    def sem(self, *names):
        for name in names:
            self.sem_h[name] = self.ctx.enter_context(self.nc.semaphore(name))
            self.sem_n[name] = 0

    def step(self, eng, fn, waits=(), inc=None, inc_by=1):
        own = ESEM.get(eng)
        wl = []
        for w in waits:
            if w is None:
                continue
            sn, v = w
            if sn == own:
                assert v <= self.sem_n[sn], f"self-wait ahead on {sn}: {v} > {self.sem_n[sn]}"
            prev = self.waited[eng].get(sn, 0)
            if v > prev:
                wl.append((sn, v))
                self.waited[eng][sn] = v
        tok = None
        if inc is not None:
            self.sem_n[inc] += inc_by
            tok = (inc, self.sem_n[inc])
        self.q[eng].append((fn, wl, inc, inc_by, self.phase))
        return tok

    def barrier(self):
        self.nbar += 1
        target = 5 * self.nbar
        for e in ENGS:
            self.sem_n["bar"] += 1
            own = ESEM.get(e)
            wl = [(own, self.sem_n[own])] if own else []
            self.q[e].append((lambda eng: eng.nop(), wl, "bar", 1, self.phase))
        for e in ENGS:
            self.q[e].append((None, [("bar", target)], None, 0, self.phase))
            self.waited[e]["bar"] = target

    def build(self, block):
        for e in ENGS:
            ops = self.q[e]

            def mk(ops):
                def body(eng):
                    cur = None
                    sid = None
                    for fn, wl, inc, inc_by, ph in ops:
                        if ph != cur:
                            if cur is not None:
                                self.nc.leave_named_scope(cur, sid, False)
                            sid, _ = self.nc.enter_named_scope(ph, False)
                            cur = ph
                        for sn, v in wl:
                            eng.wait_ge(self.sem_h[sn], v)
                        if fn is None:
                            continue
                        inst = fn(eng)
                        if inc is not None:
                            inst.then_inc(self.sem_h[inc], inc_by)
                    if cur is not None:
                        self.nc.leave_named_scope(cur, sid, False)
                return body
            getattr(block, e)(mk(ops))


def build_nc(debug=False):
    nc = bass.Bass("TRN2", target_bir_lowering=False)
    x_seq = nc.dram_tensor("x_seq", [SEQ, DM], F32, kind="ExternalInput").ap()
    wqc = nc.dram_tensor("wqc", [4, NDC * 128, 384], F32,
                         kind="ExternalInput").ap()
    wqup = nc.dram_tensor("wqup", [12, NQC * 128, 512], F32,
                          kind="ExternalInput").ap()
    wkvc = nc.dram_tensor("wkvc", [NDC * 128, 576], F32,
                          kind="ExternalInput").ap()
    wkvup = nc.dram_tensor("wkvup", [KVL, 8192], F32, kind="ExternalInput").ap()
    wo = nc.dram_tensor("wo", [128, DM], F32, kind="ExternalInput").ap()
    c_in = {}
    for name, sh in [("bqc", [128, 12]), ("bqup", [128, 48]), ("bkvc", [128, 5]),
                     ("bkvupk", [128, 32]), ("bkvupv", [1, 4096]), ("bo", [128, 32]),
                     ("gs", [128, 86]), ("us", [128, 86]), ("ds", [128, 32]),
                     ("identb", [128, 128]), ("identf", [128, 128])]:
        dt = BF16 if name == "identb" else F32
        c_in[name] = nc.dram_tensor(name, sh, dt, kind="ExternalInput").ap()
    wg = nc.dram_tensor("wg", [NFB, 128, NDC * 128], I32, kind="ExternalInput").ap()
    wu = nc.dram_tensor("wu", [NFB, 128, NDC * 128], I32, kind="ExternalInput").ap()
    wd = nc.dram_tensor("wd", [NHC, 128, NFB * 128], I32, kind="ExternalInput").ap()
    y = nc.dram_tensor("y", [TOK, DM], F32, kind="ExternalOutput").ap()
    x1_dram = nc.dram_tensor("x1_dram", [TOK, DM], F32, kind="Internal").ap()
    h_dram = nc.dram_tensor("h_dram", [NFB, 128, TOK], BF16, kind="Internal").ap()
    dbg = {}

    def dbg_out(name, shape, dtype):
        if debug and name not in dbg:
            dbg[name] = nc.dram_tensor("dbg_" + name, shape, dtype,
                                       kind="ExternalOutput").ap()
        return dbg.get(name)

    with ExitStack() as octx:
        S = Sched(nc, octx)
        S.sem("bar", "cA", "cV", "cP", "cG", "dC",
              "dX0", "dX1", "dW0", "dW1", "dKW0", "dKW1",
              "dXO", "dX1W", "dHW0", "dHW1", "dHH",
              "dWG0", "dWG1", "dWU0", "dWU1", "dWD0", "dWD1",
              "dX1R", "dY", "dDBG")
        st = S.step

        def sb(name, shape, dtype, ctx):
            return ctx.enter_context(nc.sbuf_tensor(name, shape, dtype))

        def ps(name, shape, dtype, ctx):
            return ctx.enter_context(nc.psum_tensor(name, shape, dtype))

        def dump(name, ap_src, shape, after, dtype):
            o = dbg_out(name, shape, dtype)
            if o is None:
                return
            st("sync", lambda e, o=o, a=ap_src: e.dma_start(out=o, in_=a),
               waits=[after], inc="dDBG", inc_by=16)

        wctx4 = ExitStack()
        kv_ctx = ExitStack()
        _attn_only = {"bqc", "bqup", "bkvc", "bkvupk", "identf"}
        epsT = sb("epsT", [128, 1], F32, octx)
        cst = {}
        for name in c_in:
            if name not in _attn_only and name != "bkvupv":
                dt = BF16 if name == "identb" else F32
                cst[name] = sb("c_" + name, list(c_in[name].tensor.shape), dt, octx)
        o_n = sb("o_n", [128, TOK], BF16, wctx4)
        for name in c_in:
            if name in _attn_only:
                cst[name] = sb("c_" + name, list(c_in[name].tensor.shape), F32,
                               kv_ctx)
        n_c = 0
        for name in c_in:
            if name == "bkvupv":
                continue
            n_c += 1
            st("sync", lambda e, name=name: e.dma_start(out=cst[name][:],
                                                        in_=c_in[name]),
               inc="dC", inc_by=16)
        tok_consts = ("dC", 16 * n_c)
        ones1 = sb("ones1", [1, 128], F32, kv_ctx)
        ones_b = sb("ones_b", [1, 128], BF16, kv_ctx)
        tok_ones = st("gpsimd", lambda e: e.memset(ones1[:], 1.0), inc="cG")
        tok_ones = st("gpsimd", lambda e: e.memset(ones_b[:], 1.0),
                      waits=[tok_ones], inc="cG")
        tok_eps = st("gpsimd", lambda e: e.memset(epsT[:], float(EPS)),
                     waits=[tok_ones], inc="cG")
        st("gpsimd", lambda e: e.memset(krz0[64:128, :], 0.0),
           waits=[tok_eps], inc="cG")
        tok_krz_z = st("gpsimd", lambda e: e.memset(krz1[0:64, :], 0.0),
                       inc="cG")
        bqup_s = sb("bqup_s", [128, 48], F32, kv_ctx)
        tok_bqups = st("vector", lambda e: e.tensor_scalar(
            out=bqup_s[:], in0=cst["bqup"][:], scalar1=float(SQ), scalar2=None,
            op0=ALU.mult), waits=[tok_consts], inc="cV")
        kv_nope = [sb(f"kv_nope{i}", [128, SEQ], BF16, kv_ctx)
                   for i in range(4)]
        krz0 = sb("krz0", [128, SEQ], BF16, kv_ctx)
        krz1 = sb("krz1", [128, SEQ], BF16, kv_ctx)

        bctx = ExitStack()
        qc_all = sb("qc_all", [128, NQC * TOK], BF16, bctx)
        q_c = [qc_all[:, i * TOK:(i + 1) * TOK] for i in range(NQC)]
        actx = ExitStack()
        wkvc_sb = sb("wkvc_sb", [128, NDC * 576], BF16, actx)
        wqc_sb = [sb(f"wqc_sb{i}", [128, 8 * 384], BF16, actx)
                  for i in range(2)]
        xt = [sb("xt0", [128, DM], BF16, actx)] * 2
        ln1_own = [sb(f"ln1o{i}", [128, TOK], BF16, actx) for i in range(NDC)]
        ln1_par = [sb(f"ln1p{i}", [128, 256], BF16, actx) for i in range(NDC)]
        ln_tok = [sb("lntok0", [128, DM], BF16, actx)] * 2
        stat6 = [sb(f"stat6{i}", [128, 48], F32, actx) for i in range(2)]
        mv = [sb(f"mv{i}", [128, 2], F32, actx) for i in range(2)]
        rstd = [sb(f"rstd{i}", [128, 1], F32, actx) for i in range(2)]
        lnv = [sb(f"lnv{i}", [128, 1], F32, actx) for i in range(2)]
        tpsA = [ps(f"tpsA{i}", [128, 512], BF16, actx) for i in range(2)]
        pj = [ps(f"pj{i}", [128, 1024], F32, actx) for i in range(3)]

        S.phase = "P0"
        for kq in range(4):
            def f_wkvc(e, kq=kq):
                s_ = wkvc[kq * 1024:(kq + 1) * 1024, :]
                s_ = s_.rearrange("(d p) c -> p d c", p=128)
                d_ = wkvc_sb[:, kq * 8 * 576:(kq + 1) * 8 * 576]
                d_ = d_.rearrange("p (d c) -> p d c", d=8)
                return e.dma_start(out=d_, in_=s_)
            tok_wkvc = st("gpsimd", f_wkvc, inc="dW0", inc_by=16)
        lastx = {}
        tok_trp = {}
        tok_lncp = {}
        tok_kvpar = {}
        tok_kv_ev = {}

        for tau in range(NTS):
            sl = tau % 2
            w = [lastx.get(0), lastx.get(1)]
            tok_x = st("gpsimd", lambda e, sl=sl, tau=tau: e.dma_start(
                out=xt[sl][:], in_=x_seq[tau * 128:(tau + 1) * 128, :]),
                waits=w, inc=f"dX{sl}", inc_by=16)

            def f_stats(e, sl=sl):
                last = None
                for i in range(8):
                    last = e.bn_stats(out=stat6[sl][:, 6 * i:6 * i + 6],
                                      in_=xt[sl][:, 512 * i:512 * (i + 1)])
                return last
            tok_st8 = st("vector", f_stats, waits=[tok_x], inc="cV")
            tok_stats = st("vector", lambda e, sl=sl: e.bn_aggr(
                out=mv[sl][:], in_=stat6[sl][:]), waits=[tok_st8], inc="cV")
            tok_ln = st("scalar", lambda e, sl=sl: e.activation(
                out=lnv[sl][:], in_=mv[sl][:, 1:2], func=AF.Ln, bias=epsT[:]),
                waits=[tok_stats, tok_eps], inc="cA")
            tok_rstd = st("scalar", lambda e, sl=sl: e.activation(
                out=rstd[sl][:], in_=lnv[sl][:], func=AF.Exp, scale=-0.5),
                waits=[tok_ln], inc="cA")
            w_lt = [tok_trp.get((tau - 1, 7))]
            tok_nrm = st("vector", lambda e, sl=sl: e.tensor_scalar(
                out=ln_tok[sl][:], in0=xt[sl][:], scalar1=mv[sl][:, 0:1],
                scalar2=rstd[sl][:], op0=ALU.subtract, op1=ALU.mult),
                waits=[tok_rstd] + w_lt, inc="cV")
            lastx[sl] = tok_nrm

            for grp in range(8):
                pp = (tau * 8 + grp) % 2
                prev_cp = tok_lncp.get(tau * 8 + grp - 2)

                def f_tr(e, sl=sl, grp=grp, pp=pp):
                    last = None
                    for i in range(4):
                        dc = grp * 4 + i
                        last = e.transpose(out=tpsA[pp][:, 128 * i:128 * (i + 1)],
                                           in_=ln_tok[sl][:, 128 * dc:128 * (dc + 1)],
                                           identity=cst["identb"][:])
                    return last
                tok_tr = st("tensor", f_tr, waits=[tok_nrm, prev_cp, tok_consts],
                            inc="cP")
                tok_trp[(tau, grp)] = tok_tr
                if tau < NQT:
                    def f_cp(e, tau=tau, grp=grp, pp=pp):
                        last = None
                        for i in range(4):
                            dc = grp * 4 + i
                            last = e.tensor_copy(
                                out=ln1_own[dc][:, 128 * tau:128 * (tau + 1)],
                                in_=tpsA[pp][:, 128 * i:128 * (i + 1)])
                        return last
                else:
                    def f_cp(e, tau=tau, grp=grp, pp=pp):
                        col = ((tau - NQT) % 2) * 128
                        last = None
                        for i in range(4):
                            dc = grp * 4 + i
                            last = e.tensor_copy(
                                out=ln1_par[dc][:, col:col + 128],
                                in_=tpsA[pp][:, 128 * i:128 * (i + 1)])
                        return last
                wkv = [tok_kvpar.get((tau - NQT) // 2 - 1)] if tau >= 10 else []
                tok_lncp[tau * 8 + grp] = st("vector", f_cp,
                                             waits=[tok_tr] + wkv, inc="cV")

            if tau in (9, 11, 13, 15):
                g = (tau - 9) // 2
                w_ln = [tok_lncp[tau * 8 + 7]]
                tok_mm_d = {}
                for dc in range(NDC):
                    def f_mm(e, dc=dc):
                        w0 = wkvc_sb[:, dc * 576:(dc + 1) * 576]
                        for r in range(4):
                            reg = pj[r // 2][:, (r % 2) * 512:(r % 2) * 512 + 256]
                            e.matmul(out=reg,
                                     lhsT=w0[:, r * 128:(r + 1) * 128],
                                     rhs=ln1_par[dc][:],
                                     start=(dc == 0), stop=(dc == NDC - 1))
                        return e.matmul(out=pj[2][0:64, 0:256],
                                        lhsT=w0[:, 512:576],
                                        rhs=ln1_par[dc][:],
                                        start=(dc == 0), stop=(dc == NDC - 1))
                    tok_mm_d[dc] = st("tensor", f_mm,
                                      waits=[tok_wkvc] + w_ln +
                                            ([tok_kv_ev.get(g - 1)] if dc == 0 else []),
                                      inc="cP")
                tok_kvpar[g] = tok_mm_d[NDC - 1]
                col0 = TOK + 256 * g

                def f_ev(e, col0=col0):
                    for r in range(4):
                        reg = pj[r // 2][:, (r % 2) * 512:(r % 2) * 512 + 256]
                        e.activation(out=kv_nope[r][:, col0:col0 + 256],
                                     in_=reg, func=AF.Identity,
                                     bias=cst["bkvc"][:, r:r + 1])
                    e.activation(out=krz0[0:64, col0:col0 + 256],
                                 in_=pj[2][0:64, 0:256], func=AF.Identity,
                                 bias=cst["bkvc"][0:64, 4:5])
                    return e.activation(out=krz1[64:128, col0:col0 + 256],
                                        in_=pj[2][0:64, 0:256], func=AF.Identity,
                                        bias=cst["bkvc"][0:64, 4:5])
                tok_kv_ev[g] = st("scalar", f_ev,
                                  waits=[tok_kvpar[g], tok_consts], inc="cA")

        if TRUNC == 3:
            S.q["sync"].append((None, [("cA", S.sem_n["cA"]),
                                       ("cV", S.sem_n["cV"]),
                                       ("cP", S.sem_n["cP"])], None, 0, S.phase))
            block = octx.enter_context(nc.Block())
            S.build(block)
            actx.close(); qctx_dummy = None
            bctx.close(); kv_ctx.close(); wctx4.close()
            return nc, dbg
        S.phase = "P1a"
        tok_qc_ev = {}
        w_ln_own = [tok_lncp[(NQT - 1) * 8 + 7]]
        last_mm = None
        tok_wqcg = {}
        tok_qcmm_blk = {}
        for blk in range(16):
            fcg, qr = blk // 4, blk % 4
            wsl = blk % 2
            wg_pre = [tok_qcmm_blk.get(blk - 2)]
            def f_wqc(e, fcg=fcg, qr=qr, wsl=wsl):
                s_ = wqc[fcg][qr * 1024:(qr + 1) * 1024, :]
                s_ = s_.rearrange("(d p) c -> p d c", p=128)
                d_ = wqc_sb[wsl][:].rearrange("p (d c) -> p d c", d=8)
                return e.dma_start(out=d_, in_=s_)
            tok_wqcg[blk] = st("gpsimd", f_wqc, waits=wg_pre,
                               inc=f"dW{wsl}", inc_by=16)
            for dc8 in range(8):
                dc = qr * 8 + dc8
                wev = ([tok_qc_ev.get(fcg - 1), tok_kv_ev.get(3)]
                       if dc == 0 else [])
                def f_mm(e, dc=dc, dc8=dc8, wsl=wsl):
                    last = None
                    w0 = wqc_sb[wsl][:, dc8 * 384:(dc8 + 1) * 384]
                    for fc3 in range(3):
                        for col in range(2):
                            last = e.matmul(
                                out=pj[fc3][:, col * 512:(col + 1) * 512],
                                lhsT=w0[:, fc3 * 128:(fc3 + 1) * 128],
                                rhs=ln1_own[dc][:, col * 512:(col + 1) * 512],
                                start=(dc == 0), stop=(dc == NDC - 1))
                    return last
                last_mm = st("tensor", f_mm,
                             waits=[tok_wqcg[blk]] + w_ln_own + wev, inc="cP")
            tok_qcmm_blk[blk] = last_mm
            if qr == 3:
                def f_ev(e, fcg=fcg):
                    last = None
                    for fc3 in range(3):
                        fc = fcg * 3 + fc3
                        last = e.activation(out=q_c[fc], in_=pj[fc3][:],
                                            func=AF.Identity,
                                            bias=cst["bqc"][:, fc:fc + 1])
                    return last
                tok_qc_ev[fcg] = st("scalar", f_ev, waits=[last_mm], inc="cA")

        if TRUNC == 4:
            S.q["sync"].append((None, [("cA", S.sem_n["cA"]),
                                       ("cV", S.sem_n["cV"]),
                                       ("cP", S.sem_n["cP"])], None, 0, S.phase))
            block = octx.enter_context(nc.Block())
            S.build(block)
            actx.close()
            bctx.close(); kv_ctx.close(); wctx4.close()
            return nc, dbg
        tok_kvo_ev = {}
        for p in range(2):
            for dc in range(NDC):
                wev = []
                if dc == 0:
                    wev = [tok_qc_ev[3]] if p == 0 else [tok_kvo_ev[0]]
                def f_mm(e, dc=dc, p=p):
                    w0 = wkvc_sb[:, dc * 576:(dc + 1) * 576]
                    last = None
                    if p == 0:
                        for c in range(3):
                            for col in range(2):
                                last = e.matmul(
                                    out=pj[c][:, col * 512:(col + 1) * 512],
                                    lhsT=w0[:, c * 128:(c + 1) * 128],
                                    rhs=ln1_own[dc][:, col * 512:(col + 1) * 512],
                                    start=(dc == 0), stop=(dc == NDC - 1))
                    else:
                        for col in range(2):
                            e.matmul(out=pj[0][:, col * 512:(col + 1) * 512],
                                     lhsT=w0[:, 384:512],
                                     rhs=ln1_own[dc][:, col * 512:(col + 1) * 512],
                                     start=(dc == 0), stop=(dc == NDC - 1))
                            last = e.matmul(
                                out=pj[1][0:64, col * 512:(col + 1) * 512],
                                lhsT=w0[:, 512:576],
                                rhs=ln1_own[dc][:, col * 512:(col + 1) * 512],
                                start=(dc == 0), stop=(dc == NDC - 1))
                    return last
                last_mm = st("tensor", f_mm, waits=[tok_wkvc] + wev, inc="cP")
            if p == 0:
                def f_ev(e):
                    last = None
                    for c in range(3):
                        last = e.activation(out=kv_nope[c][:, 0:TOK],
                                            in_=pj[c][:], func=AF.Identity,
                                            bias=cst["bkvc"][:, c:c + 1])
                    return last
            else:
                def f_ev(e):
                    e.activation(out=kv_nope[3][:, 0:TOK], in_=pj[0][:],
                                 func=AF.Identity, bias=cst["bkvc"][:, 3:4])
                    e.activation(out=krz0[0:64, 0:TOK],
                                 in_=pj[1][0:64, 0:1024], func=AF.Identity,
                                 bias=cst["bkvc"][0:64, 4:5])
                    return e.activation(out=krz1[64:128, 0:TOK],
                                        in_=pj[1][0:64, 0:1024], func=AF.Identity,
                                        bias=cst["bkvc"][0:64, 4:5])
            tok_kvo_ev[p] = st("scalar", f_ev, waits=[last_mm], inc="cA")
        if TRUNC == 5:
            S.q["sync"].append((None, [("cA", S.sem_n["cA"]),
                                       ("cV", S.sem_n["cV"]),
                                       ("cP", S.sem_n["cP"])], None, 0, S.phase))
            block = octx.enter_context(nc.Block())
            S.build(block)
            actx.close()
            bctx.close(); kv_ctx.close(); wctx4.close()
            return nc, dbg
        tok_kv_done = tok_kvo_ev[1]

        if debug:
            for i in range(4):
                dump(f"kv_nope{i}", kv_nope[i][:], [128, SEQ], tok_kv_done, BF16)
            dump("krope", krz0[:], [128, SEQ], tok_kv_done, BF16)
            dump("qc0", q_c[0], [128, TOK], tok_qc_ev[0], BF16)
            dump("ln1o0", ln1_own[0][:], [128, TOK], w_ln_own[0], BF16)

        S.barrier()
        actx.close()

        S.phase = "P1b"
        qctx = ExitStack()
        q_fm = [sb(f"qfm{i}", [128, TOK], BF16, qctx) for i in range(NQF)]
        p1b_ctx = ExitStack()
        wq2 = [sb(f"wq2_{i}", [128, NQC * 512], BF16, p1b_ctx)
               for i in range(2)]
        p1b_p = ExitStack()
        pu = [ps(f"pu{i}", [128, 1024], F32, p1b_p) for i in range(4)]
        tok_qfm_ev = {}
        last_mm = None
        tok_wq2 = {}
        for fcg in range(NQF // 4):
            wsl = fcg % 2
            wg_pre = [tok_qfm_ev.get(fcg - 2)]
            def f_w(e, fcg=fcg, wsl=wsl):
                s_ = wqup[fcg].rearrange("(q p) c -> p q c", p=128)
                d_ = wq2[wsl][:].rearrange("p (q c) -> p q c", q=NQC)
                return e.dma_start(out=d_, in_=s_)
            tok_wq2[fcg] = st("gpsimd", f_w, waits=wg_pre,
                              inc=f"dW{wsl}", inc_by=16)
            for qc in range(NQC):
                wev = [tok_qfm_ev.get(fcg - 1)] if qc == 0 else []
                def f_mm(e, qc=qc, wsl=wsl):
                    last = None
                    w0 = wq2[wsl][:, qc * 512:(qc + 1) * 512]
                    for fc4 in range(4):
                        for col in range(2):
                            last = e.matmul(
                                out=pu[fc4][:, col * 512:(col + 1) * 512],
                                lhsT=w0[:, fc4 * 128:(fc4 + 1) * 128],
                                rhs=qc_all[:, qc * TOK + col * 512:
                                           qc * TOK + (col + 1) * 512],
                                start=(qc == 0), stop=(qc == NQC - 1))
                    return last
                last_mm = st("tensor", f_mm, waits=[tok_wq2[fcg]] + wev,
                             inc="cP")

            def f_ev(e, fcg=fcg):
                last = None
                for fc4 in range(4):
                    fc = fcg * 4 + fc4
                    last = e.activation(out=q_fm[fc][:], in_=pu[fc4][:],
                                        func=AF.Identity, scale=float(SQ),
                                        bias=bqup_s[:, fc:fc + 1])
                return last
            tok_qfm_ev[fcg] = st("scalar", f_ev, waits=[last_mm, tok_bqups],
                                 inc="cA")
        if debug:
            dump("qfm0", q_fm[0][:], [128, TOK], tok_qfm_ev[0], BF16)
            dump("qfm47", q_fm[47][:], [128, TOK], tok_qfm_ev[11], BF16)

        S.barrier()
        p1b_p.close()
        p1b_ctx.close()

        if TRUNC == 1:
            block = octx.enter_context(nc.Block())
            S.build(block)
            qctx.close(); bctx.close(); kv_ctx.close(); wctx4.close()
            return nc, dbg
        attn_ctx = ExitStack()
        kf = [sb(f"kf{i}", [128, SEQ], BF16, attn_ctx) for i in range(2)]
        vtg = [sb(f"vtg{i}", [128, 16 * 512], BF16, attn_ctx) for i in range(2)]
        wkh = [sb(f"wkh{i}", [128, 512], BF16, attn_ctx) for i in range(2)]
        wkvv = [sb(f"wkvv{i}", [128, 2048], BF16, attn_ctx) for i in range(2)]
        bvg = [sb(f"bvg{i}", [1, 512], BF16, attn_ctx) for i in range(2)]
        ring = qc_all[:, 4096:4096 + 3 * TOK]
        sums = sb("sums", [128, 8], F32, attn_ctx)
        sumsA = sb("sumsA", [128, 8], F32, attn_ctx)
        negm = sb("negm", [128, 8], F32, attn_ctx)
        lnS = sb("lnS", [128, 8], F32, attn_ctx)
        augc = sb("augc", [128, 8], F32, attn_ctx)
        hic_b = sb("hic_b", [128, 8], BF16, attn_ctx)
        hilo_f = sb("hilo_f", [128, 16], F32, attn_ctx)
        augMIX = qc_all[0:2, 3072:3072 + TOK]
        ones2 = sb("ones2", [2, 128], BF16, attn_ctx)
        negmA = sb("negmA", [128, 8], F32, attn_ctx)
        s1 = ps("s1", [128, SEQ], F32, attn_ctx)
        s2 = ps("s2", [128, TOK], F32, attn_ctx)
        osum = ps("osum", [128, TOK], F32, attn_ctx)

        S.phase = "ATTN"
        tok_kw = {}
        tok_kmm = {}
        tok_kfev = {}
        tok_vev = {}
        tok_exp1 = {}
        tok_exp2 = {}
        tok_avmm = {}
        tok_maxs = {}
        tok_stt = {}
        tok_augT = {}
        tok_augrow = {}
        disc = qc_all[:, 0:SEQ]
        tok_o2 = st("gpsimd", lambda e: e.memset(ones2[:], 1.0),
                    waits=[tok_krz_z], inc="cG")

        def emit_augpre(hp):
            def f_augT(e, hp=hp):
                last = None
                for qt in range(8):
                    last = e.matmul(out=s1[0:2, qt * 128:(qt + 1) * 128],
                                    lhsT=hilo_f[:, 2 * qt:2 * qt + 2],
                                    rhs=cst["identf"][:],
                                    start=True, stop=True)
                return last
            tok_augT[hp] = st("tensor", f_augT,
                              waits=[tok_stt[(hp, qt)] for qt in range(8)] +
                                    [tok_exp1[(hp, 7)]], inc="cP")
            tok_augrow[hp] = st("vector", lambda e: e.tensor_copy(
                out=augMIX[:], in_=s1[0:2, 0:TOK]), waits=[tok_augT[hp]],
                inc="cV")

        for i in range(NH + 1):
            ia = i
            ib = i - 1
            if ib >= 0:
                emit_augpre(ib)
            if ia < NH:
                g, hg = ia // GH, ia % GH
                hsl = ia % 2
                gsl = g % 2
                if hg == 0:
                    wpre = [tok_kmm.get(4 * (g - 2) + 3)] if g >= 2 else []
                    st("gpsimd", lambda e, g=g, gsl=gsl: e.dma_start(
                        out=bvg[gsl][:],
                        in_=c_in["bkvupv"][0:1, g * 512:(g + 1) * 512]),
                       waits=wpre, inc=f"dKW{gsl}", inc_by=16)
                    for kvc in range(4):
                        tok_kw[g] = st("gpsimd",
                                       lambda e, g=g, gsl=gsl, kvc=kvc: e.dma_start(
                                           out=wkvv[gsl][:, kvc * 512:(kvc + 1) * 512],
                                           in_=wkvup[kvc * 128:(kvc + 1) * 128,
                                                     4096 + g * 512:4096 + (g + 1) * 512]),
                                       waits=wpre, inc=f"dKW{gsl}", inc_by=16)
                    for rnd in range(4):
                        w_s1 = [tok_augrow.get(ib) if rnd == 0 else
                                tok_vev.get((g, rnd - 1))]
                        def f_vmm(e, g=g, gsl=gsl, rnd=rnd):
                            last = None
                            for t4 in range(4):
                                tc = rnd * 4 + t4
                                reg = s1[:, t4 * 512:(t4 + 1) * 512]
                                for kvc in range(4):
                                    e.matmul(out=reg,
                                             lhsT=kv_nope[kvc][:, tc * 128:(tc + 1) * 128],
                                             rhs=wkvv[gsl][:, kvc * 512:(kvc + 1) * 512],
                                             start=(kvc == 0), stop=False)
                                last = e.matmul(
                                    out=reg, lhsT=ones_b[:],
                                    rhs=bvg[gsl][:],
                                    start=False, stop=True)
                            return last
                        tok_vmm = st("tensor", f_vmm,
                                     waits=w_s1 + [tok_kw[g], tok_ones], inc="cP")
                        w_vtg = [tok_avmm.get((4 * (g - 2) + 3, 15))] if g >= 2 else []
                        def f_vev(e, gsl=gsl, rnd=rnd):
                            return e.activation(
                                out=vtg[gsl][:, rnd * 2048:(rnd + 1) * 2048],
                                in_=s1[:], func=AF.Copy)
                        tok_vev[(g, rnd)] = st("scalar", f_vev,
                                               waits=[tok_vmm] + w_vtg, inc="cA")
                    s1_free = tok_vev[(g, 3)]
                else:
                    s1_free = tok_augrow.get(ib)
                wkpre = [tok_kmm.get(ia - 2)]
                tok_wk = None
                for kvc in range(4):
                    tok_wk = st("gpsimd", lambda e, ia=ia, hsl=hsl, kvc=kvc: e.dma_start(
                        out=wkh[hsl][:, kvc * 128:(kvc + 1) * 128],
                        in_=wkvup[kvc * 128:(kvc + 1) * 128,
                                  ia * 128:(ia + 1) * 128]),
                        waits=wkpre, inc=f"dW{hsl}", inc_by=16)
                tok_kfh = {}
                for kh2 in range(2):
                    def f_kmm(e, hsl=hsl, kh2=kh2):
                        last = None
                        for col in range(2 * kh2, 2 * kh2 + 2):
                            for kvc in range(4):
                                last = e.matmul(
                                    out=s1[:, col * 512:(col + 1) * 512],
                                    lhsT=wkh[hsl][:, kvc * 128:(kvc + 1) * 128],
                                    rhs=kv_nope[kvc][:, col * 512:(col + 1) * 512],
                                    start=(kvc == 0), stop=(kvc == 3))
                        return last
                    tok_kmm[ia] = st("tensor", f_kmm,
                                     waits=[s1_free if kh2 == 0 else None,
                                            tok_wk], inc="cP")
                    def f_kev(e, hsl=hsl, ia=ia, kh2=kh2):
                        return e.activation(
                            out=kf[hsl][:, kh2 * TOK:(kh2 + 1) * TOK],
                            in_=s1[:, kh2 * TOK:(kh2 + 1) * TOK],
                            func=AF.Identity, bias=cst["bkvupk"][:, ia:ia + 1])
                    tok_kfh[kh2] = st("scalar", f_kev, waits=[tok_kmm[ia]],
                                      inc="cA")
                tok_kfev[ia] = tok_kfh[1]
                tok_kfev0 = tok_kfh[0]
                nope_t = q_fm[ia]
                rope_t = q_fm[32 + ia // 2]
                rb = (ia % 2) * 64
                tok_e1a_prev = None

            for kc in range(16):
                if ib >= 0:
                    gb, hgb = ib // GH, ib % GH
                    hslb = ib % 2
                    rbb = (ib % 2) * 64
                    rope_b = q_fm[32 + ib // 2]
                    tok_s2h = {}
                    for ch in range(2):
                        w_s2 = [tok_exp2.get((ib, kc - 1, ch)) if kc > 0 else
                                tok_exp2.get((ib - 1, 15, ch))]
                        krzb = krz0 if ib % 2 == 0 else krz1
                        def f_s2(e, ib=ib, kc=kc, hslb=hslb,
                                 rope_b=rope_b, col=ch, krzb=krzb):
                            reg = s2[:, col * 512:(col + 1) * 512]
                            e.matmul(out=reg,
                                     lhsT=kf[hslb][:, kc * 128:(kc + 1) * 128],
                                     rhs=q_fm[ib][:, col * 512:(col + 1) * 512],
                                     start=True, stop=False)
                            e.matmul(out=reg,
                                     lhsT=krzb[:, kc * 128:(kc + 1) * 128],
                                     rhs=rope_b[:, col * 512:(col + 1) * 512],
                                     start=False, stop=False)
                            return e.matmul(
                                out=reg, lhsT=ones2[:],
                                rhs=augMIX[:, col * 512:(col + 1) * 512],
                                start=False, stop=True)
                        tok_s2h[ch] = st("tensor", f_s2,
                                         waits=w_s2 + [tok_augrow[ib]], inc="cP")
                    if kc >= 1:
                        def f_av(e, ib=ib, kc=kc, hgb=hgb, gb=gb):
                            last = None
                            rsl = (kc - 1) % 3
                            for col in range(2):
                                last = e.matmul(
                                    out=osum[:, col * 512:(col + 1) * 512],
                                    lhsT=vtg[gb % 2][:, (kc - 1) * 512 + hgb * 128:
                                                     (kc - 1) * 512 + (hgb + 1) * 128],
                                    rhs=ring[:, rsl * TOK + col * 512:
                                             rsl * TOK + (col + 1) * 512],
                                    start=(ib == 0 and kc == 1),
                                    stop=False)
                            return last
                        tok_avmm[(ib, kc - 1)] = st(
                            "tensor", f_av,
                            waits=[tok_exp2[(ib, kc - 1, 1)], tok_vev[(gb, 3)]],
                            inc="cP")
                    rsl = kc % 3
                    prev_kc = kc - 3 if kc >= 3 else 15 - ((15 - kc) % 3)
                    w_ring = [tok_avmm.get((ib, prev_kc)) if kc >= 3 else
                              (tok_avmm.get((ib - 1, prev_kc)) if ib > 0 else None)]
                    for ch in range(2):
                        tok_exp2[(ib, kc, ch)] = st(
                            "scalar", lambda e, rsl=rsl, ch=ch: e.activation(
                                out=ring[:, rsl * TOK + ch * 512:
                                         rsl * TOK + (ch + 1) * 512],
                                in_=s2[:, ch * 512:(ch + 1) * 512],
                                func=AF.Exp),
                            waits=[tok_s2h[ch]] + w_ring, inc="cA")
                if ia < NH and kc % 2 == 0:
                    qt = kc // 2
                    w_s1 = [tok_kfev0 if qt == 0 else tok_e1a_prev]
                    krz = krz0 if ia % 2 == 0 else krz1
                    def f_s1a(e, qt=qt, hsl=hsl, nope_t=nope_t,
                              rope_t=rope_t, krz=krz):
                        last = None
                        for col in range(2):
                            e.matmul(out=s1[:, col * 512:(col + 1) * 512],
                                     lhsT=nope_t[:, qt * 128:(qt + 1) * 128],
                                     rhs=kf[hsl][:, col * 512:(col + 1) * 512],
                                     start=True, stop=False)
                            last = e.matmul(
                                out=s1[:, col * 512:(col + 1) * 512],
                                lhsT=rope_t[:, qt * 128:(qt + 1) * 128],
                                rhs=krz[:, col * 512:(col + 1) * 512],
                                start=False, stop=True)
                        return last
                    tok_s1a = st("tensor", f_s1a, waits=w_s1, inc="cP")
                    def f_s1b(e, qt=qt, hsl=hsl, nope_t=nope_t,
                              rope_t=rope_t, krz=krz):
                        last = None
                        for col in range(2, 4):
                            e.matmul(out=s1[:, col * 512:(col + 1) * 512],
                                     lhsT=nope_t[:, qt * 128:(qt + 1) * 128],
                                     rhs=kf[hsl][:, col * 512:(col + 1) * 512],
                                     start=True, stop=False)
                            last = e.matmul(
                                out=s1[:, col * 512:(col + 1) * 512],
                                lhsT=rope_t[:, qt * 128:(qt + 1) * 128],
                                rhs=krz[:, col * 512:(col + 1) * 512],
                                start=False, stop=True)
                        return last
                    tok_s1b = st("tensor", f_s1b,
                                 waits=[tok_kfev[ia] if qt == 0 else
                                        tok_exp1[(ia, qt - 1)]], inc="cP")
                    w_negm = [tok_exp1.get((ia - 1, qt))] if ia > 0 else []
                    t_mA = st("vector", lambda e, qt=qt: e.reduce_max(
                        out=negmA[:, qt:qt + 1], in_=s1[:, 0:TOK], axis=AX.X,
                        negate=True), waits=[tok_s1a] + w_negm, inc="cV")
                    t_mB = st("vector", lambda e, qt=qt: e.reduce_max(
                        out=negm[:, qt:qt + 1], in_=s1[:, TOK:SEQ], axis=AX.X,
                        negate=True), waits=[tok_s1b] + w_negm, inc="cV")
                    tok_maxs[(ia, qt)] = st(
                        "vector", lambda e, qt=qt: e.tensor_tensor(
                            out=negm[:, qt:qt + 1], in0=negm[:, qt:qt + 1],
                            in1=negmA[:, qt:qt + 1], op=ALU.min),
                        waits=[t_mB], inc="cV")
                    tok_e1a = st(
                        "scalar", lambda e, qt=qt: e.activation(
                            out=disc[:, 0:TOK], in_=s1[:, 0:TOK], func=AF.Exp,
                            bias=negm[:, qt:qt + 1],
                            accum_out=sumsA[:, qt:qt + 1]),
                        waits=[tok_maxs[(ia, qt)]], inc="cA")
                    tok_exp1[(ia, qt)] = st(
                        "scalar", lambda e, qt=qt: e.activation(
                            out=disc[:, TOK:SEQ], in_=s1[:, TOK:SEQ],
                            func=AF.Exp, bias=negm[:, qt:qt + 1],
                            accum_out=sums[:, qt:qt + 1]),
                        waits=[tok_e1a], inc="cA")
                    tok_e1a_prev = tok_e1a
                    tok_sad = st("vector", lambda e, qt=qt: e.tensor_tensor(
                        out=sums[:, qt:qt + 1], in0=sums[:, qt:qt + 1],
                        in1=sumsA[:, qt:qt + 1], op=ALU.add),
                        waits=[tok_exp1[(ia, qt)]], inc="cV")
                    tok_lns = st("scalar", lambda e, qt=qt: e.activation(
                        out=lnS[:, qt:qt + 1], in_=sums[:, qt:qt + 1],
                        func=AF.Ln), waits=[tok_sad], inc="cA")
                    t_st = st("vector", lambda e, qt=qt: e.scalar_tensor_tensor(
                        out=augc[:, qt:qt + 1], in0=lnS[:, qt:qt + 1],
                        scalar=-1.0, in1=negm[:, qt:qt + 1],
                        op0=ALU.mult, op1=ALU.add),
                        waits=[tok_lns, tok_maxs[(ia, qt)],
                               tok_augT.get(ia - 1)], inc="cV")
                    t_hb = st("vector", lambda e, qt=qt: e.tensor_copy(
                        out=hic_b[:, qt:qt + 1], in_=augc[:, qt:qt + 1]),
                        waits=[t_st], inc="cV")
                    t_hf = st("vector", lambda e, qt=qt: e.tensor_copy(
                        out=hilo_f[:, 2 * qt:2 * qt + 1],
                        in_=hic_b[:, qt:qt + 1]), waits=[t_hb], inc="cV")
                    tok_stt[(ia, qt)] = st(
                        "vector", lambda e, qt=qt: e.tensor_tensor(
                            out=hilo_f[:, 2 * qt + 1:2 * qt + 2],
                            in0=augc[:, qt:qt + 1],
                            in1=hilo_f[:, 2 * qt:2 * qt + 1],
                            op=ALU.subtract), waits=[t_hf], inc="cV")
            if ib >= 0:
                gb, hgb = ib // GH, ib % GH
                def f_av15(e, ib=ib, hgb=hgb, gb=gb):
                    last = None
                    rsl = 15 % 3
                    for col in range(2):
                        last = e.matmul(
                            out=osum[:, col * 512:(col + 1) * 512],
                            lhsT=vtg[gb % 2][:, 15 * 512 + hgb * 128:
                                             15 * 512 + (hgb + 1) * 128],
                            rhs=ring[:, rsl * TOK + col * 512:
                                     rsl * TOK + (col + 1) * 512],
                            start=False,
                            stop=(ib == NH - 1))
                    return last
                tok_avmm[(ib, 15)] = st(
                    "tensor", f_av15,
                    waits=[tok_exp2[(ib, 15, 1)], tok_vev[(gb, 3)]], inc="cP")
        last_av = tok_avmm[(NH - 1, 15)]

        tok_on = st("vector", lambda e: e.tensor_copy(out=o_n[:], in_=osum[:]),
                    waits=[last_av], inc="cV")
        if debug:
            dump("o_n", o_n[:], [128, TOK], tok_on, BF16)
            dump("augrow", augMIX[0:1, :], [1, TOK], tok_augrow[NH - 1], BF16)

        S.barrier()
        attn_ctx.close()
        qctx.close()
        bctx.close()
        kv_ctx.close()
        if TRUNC == 2:
            block = octx.enter_context(nc.Block())
            S.build(block)
            wctx4.close()
            return nc, dbg

        S.phase = "W"
        ln2_fm = sb("ln2_fm", [128, NDC * TOK], BF16, wctx4)
        wctx = ExitStack()
        wo_sb = sb("wo_sb", [128, DM], BF16, wctx)
        xa_fm = [sb(f"xafm{i}", [128, TOK], BF16, wctx) for i in range(2)]
        xa_tok = sb("xa_tok", [128, NQT * DM], BF16, wctx)
        xo = sb("xo", [128, DM], F32, wctx)
        x1t = sb("x1t", [128, DM], F32, wctx)
        lnA = sb("lnA", [128, DM], BF16, wctx)
        ln2_tok = sb("ln2_tok", [128, DM], BF16, wctx)
        stat6w = sb("stat6w", [128, 48], F32, wctx)
        mvw = sb("mvw", [128, 2], F32, wctx)
        rstdw = sb("rstdw", [128, 1], F32, wctx)
        lnvw = sb("lnvw", [128, 1], F32, wctx)
        wops = [ps(f"wops{i}", [128, TOK], F32, wctx) for i in range(2)]
        tpsW = [ps(f"tpsW{i}", [128, TOK], BF16, wctx) for i in range(2)]

        tok_wo = st("gpsimd", lambda e: e.dma_start(out=wo_sb[:], in_=wo),
                    inc="dKW0", inc_by=16)
        tok_wo_ev = {}
        tok_tcp = {}
        for Hc in range(NHC):
            wsl = Hc % 2
            w_ps = [tok_wo_ev.get(Hc - 2)]

            def f_wmm(e, Hc=Hc, wsl=wsl):
                last = None
                for col in range(2):
                    last = e.matmul(
                        out=wops[wsl][:, col * 512:(col + 1) * 512],
                        lhsT=wo_sb[:, Hc * 128:(Hc + 1) * 128],
                        rhs=o_n[:, col * 512:(col + 1) * 512],
                        start=True, stop=True)
                return last
            tok_wmm = st("tensor", f_wmm, waits=[tok_wo, tok_on] + w_ps,
                         inc="cP")
            w_fm = [tok_tcp.get(Hc - 2)]
            tok_ev = st("scalar", lambda e, Hc=Hc, wsl=wsl: e.activation(
                out=xa_fm[wsl][:], in_=wops[wsl][:], func=AF.Identity,
                bias=cst["bo"][:, Hc:Hc + 1]), waits=[tok_wmm] + w_fm, inc="cA")
            tok_wo_ev[Hc] = tok_ev

            def f_tw(e, wsl=wsl):
                last = None
                for qt in range(8):
                    last = e.transpose(out=tpsW[wsl][:, qt * 128:(qt + 1) * 128],
                                       in_=xa_fm[wsl][:, qt * 128:(qt + 1) * 128],
                                       identity=cst["identb"][:])
                return last
            tok_tw = st("tensor", f_tw, waits=[tok_ev] + [tok_tcp.get(Hc - 2)],
                        inc="cP")

            def f_tc(e, Hc=Hc, wsl=wsl):
                src = tpsW[wsl][:].rearrange("p (q c) -> p q c", q=8)
                dst = xa_tok[:].rearrange("p (q d) -> p q d", q=8)[
                    :, :, Hc * 128:(Hc + 1) * 128]
                return e.tensor_copy(out=dst, in_=src)
            tok_tcp[Hc] = st("vector", f_tc, waits=[tok_tw], inc="cV")

        tok_xa_done = tok_tcp[NHC - 1]
        if debug:
            dump("xa_tok", xa_tok[:, 0:DM], [128, DM], tok_xa_done, BF16)

        tok_ln2cp = {}
        tok_x1_lastread = {}
        for qt in range(NQT):
            xa = xa_tok[:, qt * DM:(qt + 1) * DM]
            tok_xod = st("sync", lambda e, qt=qt: e.dma_start(
                out=xo[:], in_=x_seq[qt * 128:(qt + 1) * 128, :]),
                waits=[tok_x1_lastread.get(qt - 1)], inc="dXO", inc_by=16)

            def f_st(e, xa=xa):
                last = None
                for i in range(8):
                    last = e.bn_stats(out=stat6w[:, 6 * i:6 * i + 6],
                                      in_=xa[:, 512 * i:512 * (i + 1)])
                return last
            tok_sw8 = st("vector", f_st, waits=[tok_xa_done], inc="cV")
            tok_sw = st("vector", lambda e: e.bn_aggr(
                out=mvw[:], in_=stat6w[:]), waits=[tok_sw8], inc="cV")
            tok_lnw = st("scalar", lambda e: e.activation(
                out=lnvw[:], in_=mvw[:, 1:2], func=AF.Ln, bias=epsT[:]),
                waits=[tok_sw], inc="cA")
            tok_rsw = st("scalar", lambda e: e.activation(
                out=rstdw[:], in_=lnvw[:], func=AF.Exp, scale=-0.5),
                waits=[tok_lnw], inc="cA")
            tok_nw = st("vector", lambda e, xa=xa: e.tensor_scalar(
                out=lnA[:], in0=xa, scalar1=mvw[:, 0:1], scalar2=rstdw[:],
                op0=ALU.subtract, op1=ALU.mult), waits=[tok_rsw], inc="cV")
            tok_x1 = st("vector", lambda e: e.tensor_tensor(
                out=x1t[:], in0=lnA[:], in1=xo[:], op=ALU.add),
                waits=[tok_nw, tok_xod, ("dX1W", 16 * qt)], inc="cV")
            st("sync", lambda e, qt=qt: e.dma_start(
                out=x1_dram[qt * 128:(qt + 1) * 128, :], in_=x1t[:]),
                waits=[tok_x1], inc="dX1W", inc_by=16)

            def f_st2(e):
                last = None
                for i in range(8):
                    last = e.bn_stats(out=stat6w[:, 6 * i:6 * i + 6],
                                      in_=x1t[:, 512 * i:512 * (i + 1)])
                return last
            tok_sw2a = st("vector", f_st2, waits=[tok_x1], inc="cV")
            tok_sw2 = st("vector", lambda e: e.bn_aggr(
                out=mvw[:], in_=stat6w[:]), waits=[tok_sw2a], inc="cV")
            tok_lnw2 = st("scalar", lambda e: e.activation(
                out=lnvw[:], in_=mvw[:, 1:2], func=AF.Ln, bias=epsT[:]),
                waits=[tok_sw2], inc="cA")
            tok_rsw2 = st("scalar", lambda e: e.activation(
                out=rstdw[:], in_=lnvw[:], func=AF.Exp, scale=-0.5),
                waits=[tok_lnw2], inc="cA")
            tok_nw2 = st("vector", lambda e: e.tensor_scalar(
                out=ln2_tok[:], in0=x1t[:], scalar1=mvw[:, 0:1],
                scalar2=rstdw[:], op0=ALU.subtract, op1=ALU.mult),
                waits=[tok_rsw2], inc="cV")
            tok_x1_lastread[qt] = tok_nw2
            for grp in range(8):
                pp = grp % 2
                w_tp = [tok_ln2cp.get((qt, grp - 2)) if grp >= 2 else
                        (tok_ln2cp.get((qt - 1, 6 + grp)) if qt > 0 else None)]

                def f_t2(e, pp=pp, grp=grp):
                    last = None
                    for i in range(4):
                        dc = grp * 4 + i
                        last = e.transpose(
                            out=tpsW[pp][:, i * 128:(i + 1) * 128],
                            in_=ln2_tok[:, dc * 128:(dc + 1) * 128],
                            identity=cst["identb"][:])
                    return last
                tok_t2 = st("tensor", f_t2, waits=[tok_nw2] + w_tp, inc="cP")

                def f_c2(e, pp=pp, grp=grp, qt=qt):
                    src = tpsW[pp][:, 0:512].rearrange("p (c f) -> p c f", c=4)
                    dst = ln2_fm[:].rearrange("p (d t) -> p d t", d=NDC)[
                        :, grp * 4:grp * 4 + 4, qt * 128:(qt + 1) * 128]
                    return e.tensor_copy(out=dst, in_=src)
                tok_ln2cp[(qt, grp)] = st("vector", f_c2, waits=[tok_t2],
                                          inc="cV")
        tok_ln2_done = tok_ln2cp[(NQT - 1, 7)]
        if debug:
            dump("x1_7", x1t[:], [128, DM], tok_x1_lastread[NQT - 1], F32)
            dump("ln2fm", ln2_fm[:, 0:TOK], [128, TOK], tok_ln2_done, BF16)

        S.barrier()
        wctx.close()

        S.phase = "F1"
        fctx = ExitStack()
        wgt = [sb(f"wgt{i}", [128, NDC * 128], BF16, fctx) for i in range(2)]
        wut = [sb(f"wut{i}", [128, NDC * 128], BF16, fctx) for i in range(2)]
        tmpg = [sb(f"tmpg{i}", [128, TOK], F32, fctx) for i in range(2)]
        tmpu = [sb(f"tmpu{i}", [128, TOK], F32, fctx) for i in range(2)]
        htile = [sb(f"htile{i}", [128, TOK], BF16, fctx) for i in range(2)]
        psg = [ps(f"psg{i}", [128, TOK], F32, fctx) for i in range(2)]
        psu2 = [ps(f"psu2{i}", [128, TOK], F32, fctx) for i in range(2)]
        tok_gev = {}
        tok_uev = {}
        tok_hmul = {}
        tok_f1mm = {}
        for fb in range(NFB):
            sl = fb % 2
            w_w = [tok_f1mm.get(fb - 2)]
            tok_wgd = st("gpsimd", lambda e, fb=fb, sl=sl: e.dma_start(
                out=wgt[sl][:], in_=wg[fb]), waits=w_w, inc=f"dWG{sl}",
                inc_by=16)
            tok_wud = st("gpsimd", lambda e, fb=fb, sl=sl: e.dma_start(
                out=wut[sl][:], in_=wu[fb]), waits=w_w, inc=f"dWU{sl}",
                inc_by=16)
            w_ps = [tok_gev.get(fb - 2), tok_uev.get(fb - 2)]

            def f_mm(e, sl=sl):
                last = None
                for dc in range(NDC):
                    for col in range(2):
                        e.matmul(out=psg[sl][:, col * 512:(col + 1) * 512],
                                 lhsT=wgt[sl][:, dc * 128:(dc + 1) * 128],
                                 rhs=ln2_fm[:, dc * TOK + col * 512:
                                            dc * TOK + (col + 1) * 512],
                                 start=(dc == 0), stop=(dc == NDC - 1))
                        last = e.matmul(
                            out=psu2[sl][:, col * 512:(col + 1) * 512],
                            lhsT=wut[sl][:, dc * 128:(dc + 1) * 128],
                            rhs=ln2_fm[:, dc * TOK + col * 512:
                                       dc * TOK + (col + 1) * 512],
                            start=(dc == 0), stop=(dc == NDC - 1))
                return last
            tok_f1mm[fb] = st("tensor", f_mm,
                              waits=[tok_wgd, tok_wud, tok_ln2_done] + w_ps,
                              inc="cP")
            w_tmp = [tok_hmul.get(fb - 2)]
            tok_g = st("scalar", lambda e, fb=fb, sl=sl: e.activation(
                out=tmpg[sl][:], in_=psg[sl][:], func=AF.Silu,
                scale=cst["gs"][:, fb:fb + 1]),
                waits=[tok_f1mm[fb]] + w_tmp, inc="cA")
            tok_gev[fb] = tok_g
            tok_u = st("vector", lambda e, fb=fb, sl=sl: e.tensor_scalar(
                out=tmpu[sl][:], in0=psu2[sl][:],
                scalar1=cst["us"][:, fb:fb + 1], scalar2=None, op0=ALU.mult),
                waits=[tok_f1mm[fb]] + w_tmp, inc="cV")
            tok_uev[fb] = tok_u
            hsem = f"dHW{sl}"
            w_h = [(hsem, 16 * (fb // 2))] if fb >= 2 else []
            tok_m = st("vector", lambda e, sl=sl: e.tensor_tensor(
                out=htile[sl][:], in0=tmpg[sl][:], in1=tmpu[sl][:],
                op=ALU.mult), waits=[tok_g, tok_u] + w_h, inc="cV")
            tok_hmul[fb] = tok_m
            st("sync", lambda e, fb=fb, sl=sl: e.dma_start(
                out=h_dram[fb], in_=htile[sl][:]), waits=[tok_m], inc=hsem,
                inc_by=16)
        tok_hw_all = [("dHW0", 16 * 43), ("dHW1", 16 * 43)]

        S.barrier()
        fctx.close()
        wctx4.close()

        S.phase = "F2"
        f2ctx = ExitStack()
        hh = sb("hh", [128, NFB * 512], BF16, f2ctx)
        wdt = [sb(f"wdt{i}", [128, NFB * 128], BF16, f2ctx) for i in range(2)]
        ff_fm = [sb(f"fffm{i}", [128, 512], BF16, f2ctx) for i in range(2)]
        ffn_tok = sb("ffn_tok", [128, 4 * DM], BF16, f2ctx)
        x1r = sb("x1r", [128, DM], F32, f2ctx)
        yt = sb("yt", [128, DM], F32, f2ctx)
        lnF = sb("lnF", [128, DM], BF16, f2ctx)
        stat6f = sb("stat6f", [128, 48], F32, f2ctx)
        mvf = sb("mvf", [128, 2], F32, f2ctx)
        rstdf = sb("rstdf", [128, 1], F32, f2ctx)
        lnvf = sb("lnvf", [128, 1], F32, f2ctx)
        dps = [ps(f"dps{i}", [128, 512], F32, f2ctx) for i in range(4)]
        tpsF = [ps(f"tpsF{i}", [128, 512], BF16, f2ctx) for i in range(2)]

        tok_y = {}
        tok_dmm = {}
        for half in range(2):
            w_hh = list(tok_hw_all)
            if half == 1:
                w_hh.append(tok_dmm[(0, NHC - 1)])

            fb0 = 0
            for n in [11] * 7 + [9]:
                def f_hh(e, half=half, fb0=fb0, n=n):
                    s_ = h_dram[fb0:fb0 + n, :, half * 512:(half + 1) * 512]
                    s_ = s_.rearrange("f p c -> p f c")
                    dst = hh[:].rearrange("p (f c) -> p f c", f=NFB)[
                        :, fb0:fb0 + n, :]
                    return e.dma_start(out=dst, in_=s_)
                tok_hh = st("sync", f_hh, waits=w_hh, inc="dHH", inc_by=16)
                fb0 += n
            tok_dev = {}
            tok_fcp = {}
            for j in range(NHC):
                sl = j % 2
                w_wd = [tok_dmm.get((half, j - 2)) if j >= 2 else
                        (tok_dmm.get((0, NHC - 2 + j)) if half == 1 else None)]
                tok_wdd = st("gpsimd", lambda e, j=j, sl=sl: e.dma_start(
                    out=wdt[sl][:], in_=wd[j]), waits=w_wd, inc=f"dWD{sl}",
                    inc_by=16)
                w_ps = [tok_dev.get(j - 4)]

                def f_dmm(e, j=j, sl=sl):
                    last = None
                    for fc in range(NFB):
                        last = e.matmul(
                            out=dps[j % 4][:],
                            lhsT=wdt[sl][:, fc * 128:(fc + 1) * 128],
                            rhs=hh[:, fc * 512:(fc + 1) * 512],
                            start=(fc == 0), stop=(fc == NFB - 1))
                    return last
                tok_dmm[(half, j)] = st("tensor", f_dmm,
                                        waits=[tok_wdd, ("dHH", 128 * (half + 1))] + w_ps,
                                        inc="cP")
                w_ff = [tok_fcp.get(j - 2)]
                tok_de = st("vector", lambda e, j=j, sl=sl: e.tensor_scalar(
                    out=ff_fm[sl][:], in0=dps[j % 4][:],
                    scalar1=cst["ds"][:, j:j + 1], scalar2=None, op0=ALU.mult),
                    waits=[tok_dmm[(half, j)]] + w_ff, inc="cV")
                tok_dev[j] = tok_de

                def f_tf(e, sl=sl):
                    last = None
                    for tt in range(4):
                        last = e.transpose(
                            out=tpsF[sl][:, tt * 128:(tt + 1) * 128],
                            in_=ff_fm[sl][:, tt * 128:(tt + 1) * 128],
                            identity=cst["identb"][:])
                    return last
                tok_tf = st("tensor", f_tf,
                            waits=[tok_de, tok_fcp.get(j - 2)], inc="cP")

                def f_cf(e, j=j, sl=sl):
                    src = tpsF[sl][:, 0:512].rearrange("p (t c) -> p t c", t=4)
                    dst = ffn_tok[:].rearrange("p (t d) -> p t d", t=4)[
                        :, :, j * 128:(j + 1) * 128]
                    return e.tensor_copy(out=dst, in_=src)
                tok_fcp[j] = st("vector", f_cf, waits=[tok_tf], inc="cV")
            tok_ffn_done = tok_fcp[NHC - 1]

            for tt in range(4):
                qt = half * 4 + tt
                ft = ffn_tok[:, tt * DM:(tt + 1) * DM]
                tok_x1r = st("sync", lambda e, qt=qt: e.dma_start(
                    out=x1r[:], in_=x1_dram[qt * 128:(qt + 1) * 128, :]),
                    waits=[tok_y.get(qt - 1), ("dX1W", 16 * NQT)],
                    inc="dX1R", inc_by=16)

                def f_sf(e, ft=ft):
                    last = None
                    for i in range(8):
                        last = e.bn_stats(out=stat6f[:, 6 * i:6 * i + 6],
                                          in_=ft[:, 512 * i:512 * (i + 1)])
                    return last
                tok_sf8 = st("vector", f_sf, waits=[tok_ffn_done], inc="cV")
                tok_sf = st("vector", lambda e: e.bn_aggr(
                    out=mvf[:], in_=stat6f[:]), waits=[tok_sf8], inc="cV")
                tok_lf = st("scalar", lambda e: e.activation(
                    out=lnvf[:], in_=mvf[:, 1:2], func=AF.Ln, bias=epsT[:]),
                    waits=[tok_sf], inc="cA")
                tok_rf = st("scalar", lambda e: e.activation(
                    out=rstdf[:], in_=lnvf[:], func=AF.Exp, scale=-0.5),
                    waits=[tok_lf], inc="cA")
                tok_nf = st("vector", lambda e, ft=ft: e.tensor_scalar(
                    out=lnF[:], in0=ft, scalar1=mvf[:, 0:1], scalar2=rstdf[:],
                    op0=ALU.subtract, op1=ALU.mult), waits=[tok_rf], inc="cV")
                tok_yt = st("vector", lambda e: e.tensor_tensor(
                    out=yt[:], in0=lnF[:], in1=x1r[:], op=ALU.add),
                    waits=[tok_nf, tok_x1r, ("dY", 16 * qt)], inc="cV")
                tok_y[qt] = st("sync", lambda e, qt=qt: e.dma_start(
                    out=y[qt * 128:(qt + 1) * 128, :], in_=yt[:]),
                    waits=[tok_yt], inc="dY", inc_by=16)

        S.q["sync"].append((None, [("dY", 16 * NQT)], None, 0, S.phase))
        if debug and dbg:
            S.q["sync"].append((None, [("dDBG", S.sem_n["dDBG"])], None, 0,
                                S.phase))

        block = octx.enter_context(nc.Block())
        S.build(block)
        f2ctx.close()
    return nc, dbg


_NC_CACHE = {}


def get_nc(debug=False):
    if debug not in _NC_CACHE:
        _NC_CACHE[debug] = build_nc(debug)
    return _NC_CACHE[debug]


def prep_inputs(L0_attn_hidden_size, L0_attn_Wq_c, L0_attn_bq_c, L0_attn_Wq_up,
                L0_attn_bq_up, L0_attn_Wkv_c, L0_attn_bkv_c, L0_attn_Wkv_up,
                L0_attn_bkv_up, L0_attn_Wo, L0_attn_bo, L0_ffn_gate_w,
                L0_ffn_gate_s, L0_ffn_up_w, L0_ffn_up_s, L0_ffn_down_w,
                L0_ffn_down_s):
    x = np.asarray(L0_attn_hidden_size, np.float32)

    def colsT(v, n):
        return np.ascontiguousarray(np.asarray(v, np.float32).reshape(n, 128).T)

    perm_q = np.concatenate(
        [(np.arange(32)[:, None] * 192 + np.arange(128)[None, :]).ravel(),
         (np.arange(32)[:, None] * 192 + 128 + np.arange(64)[None, :]).ravel()])
    perm_kv = np.concatenate(
        [(np.arange(32)[:, None] * 256 + np.arange(128)[None, :]).ravel(),
         (np.arange(32)[:, None] * 256 + 128 + np.arange(128)[None, :]).ravel()])

    wqc_T = np.asarray(L0_attn_Wq_c, np.float32).T
    wqc_t = np.ascontiguousarray(
        wqc_T.reshape(DM, 4, 384).transpose(1, 0, 2))
    wqup_T = np.asarray(L0_attn_Wq_up, np.float32)[perm_q].T
    wqup_t = np.ascontiguousarray(
        wqup_T.reshape(QL, 12, 512).transpose(1, 0, 2))
    wkvc_t = np.ascontiguousarray(np.asarray(L0_attn_Wkv_c, np.float32).T)
    wkvup_t = np.ascontiguousarray(
        np.asarray(L0_attn_Wkv_up, np.float32)[perm_kv].T)
    wo_t = np.ascontiguousarray(np.asarray(L0_attn_Wo, np.float32).T)

    bqc = colsT(L0_attn_bq_c, 12)
    bqup = colsT(np.asarray(L0_attn_bq_up, np.float32)[perm_q], 48)
    bkvc_pad = np.zeros(640, np.float32)
    bkvc_pad[:576] = np.asarray(L0_attn_bkv_c, np.float32)
    bkvc = colsT(bkvc_pad, 5)
    bkvup_p = np.asarray(L0_attn_bkv_up, np.float32)[perm_kv]
    bkvupk = colsT(bkvup_p[:4096], 32)
    bkvupv = np.ascontiguousarray(bkvup_p[4096:].reshape(1, 4096))
    bo = colsT(L0_attn_bo, 32)
    gs = colsT(L0_ffn_gate_s, 86)
    us = colsT(L0_ffn_up_s, 86)
    ds = colsT(L0_ffn_down_s, 32)

    gw = np.asarray(L0_ffn_gate_w, np.int32)
    uw = np.asarray(L0_ffn_up_w, np.int32)
    dw = np.asarray(L0_ffn_down_w, np.int32)

    def tile_gu(w):
        wt = np.ascontiguousarray(w.T)
        v = wt.reshape(NDC, 128, NFB, 128)
        return np.ascontiguousarray(
            v.transpose(2, 1, 0, 3).reshape(NFB, 128, NDC * 128))
    wg_t = tile_gu(gw)
    wu_t = tile_gu(uw)
    dwt = np.ascontiguousarray(dw.T)
    v = dwt.reshape(NFB, 128, NHC, 128)
    wd_t = np.ascontiguousarray(
        v.transpose(2, 1, 0, 3).reshape(NHC, 128, NFB * 128))

    identb = np.eye(128).astype(ml_dtypes.bfloat16)
    identf = np.eye(128, dtype=np.float32)

    shared = dict(wqc=wqc_t, wqup=wqup_t, wkvc=wkvc_t, wkvup=wkvup_t, wo=wo_t,
                  bqc=bqc, bqup=bqup, bkvc=bkvc, bkvupk=bkvupk, bkvupv=bkvupv,
                  bo=bo, gs=gs, us=us, ds=ds, identb=identb, identf=identf,
                  wg=wg_t, wu=wu_t, wd=wd_t)
    in_maps = []
    for c in range(8):
        b, hf = c // 2, c % 2
        xs = np.concatenate([x[b, hf * 1024:(hf + 1) * 1024],
                             x[b, (1 - hf) * 1024:(2 - hf) * 1024]], axis=0)
        m = dict(shared)
        m["x_seq"] = np.ascontiguousarray(xs)
        in_maps.append(m)
    return in_maps


def kernel(**inputs):
    in_maps = prep_inputs(**inputs)
    nc, _ = get_nc(debug=False)
    res = run_bass_kernel_spmd(nc, in_maps, core_ids=list(range(8)))
    out = np.empty((4, 2048, 4096), np.float32)
    for c in range(8):
        b, hf = c // 2, c % 2
        out[b, hf * 1024:(hf + 1) * 1024] = res.results[c]["y"]
    return out


# version 10
# speedup vs baseline: 1.1017x; 1.0005x over previous
import numpy as np
import ml_dtypes
from contextlib import ExitStack

import concourse.bass as bass
import concourse.mybir as mybir
from concourse.bass_utils import run_bass_kernel_spmd

F32 = mybir.dt.float32
BF16 = mybir.dt.bfloat16
I32 = mybir.dt.int32
AF = mybir.ActivationFunctionType
ALU = mybir.AluOpType
AX = mybir.AxisListType

DM = 4096
NDC = 32
TOK = 1024
NQT = 8
NTS = 16
SEQ = 2048
QL, NQC = 1536, 12
QF, NQF = 6144, 48
KVL = 512
NH, GH = 32, 4
DFF, NFB = 11008, 86
NHC = 32
SQ = 1.0 / np.sqrt(192.0)
TRUNC = 0
EPS = 1e-5

ENGS = ("sync", "scalar", "vector", "tensor", "gpsimd")
ESEM = {"scalar": "cA", "vector": "cV", "tensor": "cP", "gpsimd": "cG"}


class Sched:
    def __init__(self, nc, ctx):
        self.nc = nc
        self.ctx = ctx
        self.q = {e: [] for e in ENGS}
        self.sem_h = {}
        self.sem_n = {}
        self.waited = {e: {} for e in ENGS}
        self.nbar = 0
        self.phase = "init"

    def sem(self, *names):
        for name in names:
            self.sem_h[name] = self.ctx.enter_context(self.nc.semaphore(name))
            self.sem_n[name] = 0

    def step(self, eng, fn, waits=(), inc=None, inc_by=1):
        own = ESEM.get(eng)
        wl = []
        for w in waits:
            if w is None:
                continue
            sn, v = w
            if sn == own:
                assert v <= self.sem_n[sn], f"self-wait ahead on {sn}: {v} > {self.sem_n[sn]}"
            prev = self.waited[eng].get(sn, 0)
            if v > prev:
                wl.append((sn, v))
                self.waited[eng][sn] = v
        tok = None
        if inc is not None:
            self.sem_n[inc] += inc_by
            tok = (inc, self.sem_n[inc])
        self.q[eng].append((fn, wl, inc, inc_by, self.phase))
        return tok

    def barrier(self):
        self.nbar += 1
        target = 5 * self.nbar
        for e in ENGS:
            self.sem_n["bar"] += 1
            own = ESEM.get(e)
            wl = [(own, self.sem_n[own])] if own else []
            self.q[e].append((lambda eng: eng.nop(), wl, "bar", 1, self.phase))
        for e in ENGS:
            self.q[e].append((None, [("bar", target)], None, 0, self.phase))
            self.waited[e]["bar"] = target

    def build(self, block):
        for e in ENGS:
            ops = self.q[e]

            def mk(ops):
                def body(eng):
                    cur = None
                    sid = None
                    for fn, wl, inc, inc_by, ph in ops:
                        if ph != cur:
                            if cur is not None:
                                self.nc.leave_named_scope(cur, sid, False)
                            sid, _ = self.nc.enter_named_scope(ph, False)
                            cur = ph
                        for sn, v in wl:
                            eng.wait_ge(self.sem_h[sn], v)
                        if fn is None:
                            continue
                        inst = fn(eng)
                        if inc is not None:
                            inst.then_inc(self.sem_h[inc], inc_by)
                    if cur is not None:
                        self.nc.leave_named_scope(cur, sid, False)
                return body
            getattr(block, e)(mk(ops))


def build_nc(debug=False):
    nc = bass.Bass("TRN2", target_bir_lowering=False)
    x_seq = nc.dram_tensor("x_seq", [SEQ, DM], F32, kind="ExternalInput").ap()
    wqc = nc.dram_tensor("wqc", [4, NDC * 128, 384], F32,
                         kind="ExternalInput").ap()
    wqup = nc.dram_tensor("wqup", [12, NQC * 128, 512], F32,
                          kind="ExternalInput").ap()
    wkvc = nc.dram_tensor("wkvc", [NDC * 128, 576], F32,
                          kind="ExternalInput").ap()
    wkvup = nc.dram_tensor("wkvup", [KVL, 8192], F32, kind="ExternalInput").ap()
    wo = nc.dram_tensor("wo", [128, DM], F32, kind="ExternalInput").ap()
    c_in = {}
    for name, sh in [("bqc", [128, 12]), ("bqup", [128, 48]), ("bkvc", [128, 5]),
                     ("bkvupk", [128, 32]), ("bkvupv", [1, 4096]), ("bo", [128, 32]),
                     ("gs", [128, 86]), ("us", [128, 86]), ("ds", [128, 32]),
                     ("identb", [128, 128]), ("identf", [128, 128])]:
        dt = BF16 if name == "identb" else F32
        c_in[name] = nc.dram_tensor(name, sh, dt, kind="ExternalInput").ap()
    wg = nc.dram_tensor("wg", [NFB, 128, NDC * 128], I32, kind="ExternalInput").ap()
    wu = nc.dram_tensor("wu", [NFB, 128, NDC * 128], I32, kind="ExternalInput").ap()
    wd = nc.dram_tensor("wd", [NHC, 128, NFB * 128], I32, kind="ExternalInput").ap()
    y = nc.dram_tensor("y", [TOK, DM], F32, kind="ExternalOutput").ap()
    x1_dram = nc.dram_tensor("x1_dram", [TOK, DM], F32, kind="Internal").ap()
    h_dram = nc.dram_tensor("h_dram", [NFB, 128, TOK], BF16, kind="Internal").ap()
    dbg = {}

    def dbg_out(name, shape, dtype):
        if debug and name not in dbg:
            dbg[name] = nc.dram_tensor("dbg_" + name, shape, dtype,
                                       kind="ExternalOutput").ap()
        return dbg.get(name)

    with ExitStack() as octx:
        S = Sched(nc, octx)
        S.sem("bar", "cA", "cV", "cP", "cG", "dC",
              "dX0", "dX1", "dW0", "dW1", "dKW0", "dKW1",
              "dXO", "dX1W", "dHW0", "dHW1", "dHH",
              "dWG0", "dWG1", "dWU0", "dWU1", "dWD0", "dWD1",
              "dX1R", "dY", "dDBG")
        st = S.step

        def sb(name, shape, dtype, ctx):
            return ctx.enter_context(nc.sbuf_tensor(name, shape, dtype))

        def ps(name, shape, dtype, ctx):
            return ctx.enter_context(nc.psum_tensor(name, shape, dtype))

        def dump(name, ap_src, shape, after, dtype):
            o = dbg_out(name, shape, dtype)
            if o is None:
                return
            st("sync", lambda e, o=o, a=ap_src: e.dma_start(out=o, in_=a),
               waits=[after], inc="dDBG", inc_by=16)

        wctx4 = ExitStack()
        kv_ctx = ExitStack()
        _attn_only = {"bqc", "bqup", "bkvc", "bkvupk", "identf"}
        epsT = sb("epsT", [128, 1], F32, octx)
        cst = {}
        for name in c_in:
            if name not in _attn_only and name != "bkvupv":
                dt = BF16 if name == "identb" else F32
                cst[name] = sb("c_" + name, list(c_in[name].tensor.shape), dt, octx)
        o_n = sb("o_n", [128, TOK], BF16, wctx4)
        for name in c_in:
            if name in _attn_only:
                cst[name] = sb("c_" + name, list(c_in[name].tensor.shape), F32,
                               kv_ctx)
        n_c = 0
        for name in c_in:
            if name == "bkvupv":
                continue
            n_c += 1
            st("sync", lambda e, name=name: e.dma_start(out=cst[name][:],
                                                        in_=c_in[name]),
               inc="dC", inc_by=16)
        tok_consts = ("dC", 16 * n_c)
        ones1 = sb("ones1", [1, 128], F32, kv_ctx)
        ones_b = sb("ones_b", [1, 128], BF16, kv_ctx)
        tok_ones = st("gpsimd", lambda e: e.memset(ones1[:], 1.0), inc="cG")
        tok_ones = st("gpsimd", lambda e: e.memset(ones_b[:], 1.0),
                      waits=[tok_ones], inc="cG")
        tok_eps = st("gpsimd", lambda e: e.memset(epsT[:], float(EPS)),
                     waits=[tok_ones], inc="cG")
        st("gpsimd", lambda e: e.memset(krz0[64:128, :], 0.0),
           waits=[tok_eps], inc="cG")
        tok_krz_z = st("gpsimd", lambda e: e.memset(krz1[0:64, :], 0.0),
                       inc="cG")
        bqup_s = sb("bqup_s", [128, 48], F32, kv_ctx)
        tok_bqups = st("vector", lambda e: e.tensor_scalar(
            out=bqup_s[:], in0=cst["bqup"][:], scalar1=float(SQ), scalar2=None,
            op0=ALU.mult), waits=[tok_consts], inc="cV")
        kv_nope = [sb(f"kv_nope{i}", [128, SEQ], BF16, kv_ctx)
                   for i in range(4)]
        krz0 = sb("krz0", [128, SEQ], BF16, kv_ctx)
        krz1 = sb("krz1", [128, SEQ], BF16, kv_ctx)

        bctx = ExitStack()
        qc_all = sb("qc_all", [128, NQC * TOK], BF16, bctx)
        q_c = [qc_all[:, i * TOK:(i + 1) * TOK] for i in range(NQC)]
        actx = ExitStack()
        wkvc_sb = sb("wkvc_sb", [128, NDC * 576], BF16, actx)
        wqc_sb = [sb(f"wqc_sb{i}", [128, 8 * 384], BF16, actx)
                  for i in range(2)]
        xt = [sb("xt0", [128, DM], BF16, actx)] * 2
        ln1_own = [sb(f"ln1o{i}", [128, TOK], BF16, actx) for i in range(NDC)]
        ln1_par = [sb(f"ln1p{i}", [128, 256], BF16, actx) for i in range(NDC)]
        ln_tok = [sb("lntok0", [128, DM], BF16, actx)] * 2
        stat6 = [sb(f"stat6{i}", [128, 48], F32, actx) for i in range(2)]
        mv = [sb(f"mv{i}", [128, 2], F32, actx) for i in range(2)]
        rstd = [sb(f"rstd{i}", [128, 1], F32, actx) for i in range(2)]
        lnv = [sb(f"lnv{i}", [128, 1], F32, actx) for i in range(2)]
        tpsA = [ps(f"tpsA{i}", [128, 512], BF16, actx) for i in range(2)]
        pj = [ps(f"pj{i}", [128, 1024], F32, actx) for i in range(3)]

        S.phase = "P0"
        for kq in range(4):
            def f_wkvc(e, kq=kq):
                s_ = wkvc[kq * 1024:(kq + 1) * 1024, :]
                s_ = s_.rearrange("(d p) c -> p d c", p=128)
                d_ = wkvc_sb[:, kq * 8 * 576:(kq + 1) * 8 * 576]
                d_ = d_.rearrange("p (d c) -> p d c", d=8)
                return e.dma_start(out=d_, in_=s_)
            tok_wkvc = st("gpsimd", f_wkvc, inc="dW0", inc_by=16)
        lastx = {}
        tok_trp = {}
        tok_lncp = {}
        tok_kvpar = {}
        tok_kv_ev = {}

        for tau in range(NTS):
            sl = tau % 2
            w = [lastx.get(0), lastx.get(1)]
            tok_x = st("gpsimd", lambda e, sl=sl, tau=tau: e.dma_start(
                out=xt[sl][:], in_=x_seq[tau * 128:(tau + 1) * 128, :]),
                waits=w, inc=f"dX{sl}", inc_by=16)

            def f_stats(e, sl=sl):
                last = None
                for i in range(8):
                    last = e.bn_stats(out=stat6[sl][:, 6 * i:6 * i + 6],
                                      in_=xt[sl][:, 512 * i:512 * (i + 1)])
                return last
            tok_st8 = st("vector", f_stats, waits=[tok_x], inc="cV")
            tok_stats = st("vector", lambda e, sl=sl: e.bn_aggr(
                out=mv[sl][:], in_=stat6[sl][:]), waits=[tok_st8], inc="cV")
            tok_ln = st("scalar", lambda e, sl=sl: e.activation(
                out=lnv[sl][:], in_=mv[sl][:, 1:2], func=AF.Ln, bias=epsT[:]),
                waits=[tok_stats, tok_eps], inc="cA")
            tok_rstd = st("scalar", lambda e, sl=sl: e.activation(
                out=rstd[sl][:], in_=lnv[sl][:], func=AF.Exp, scale=-0.5),
                waits=[tok_ln], inc="cA")
            w_lt = [tok_trp.get((tau - 1, 7))]
            tok_nrm = st("vector", lambda e, sl=sl: e.tensor_scalar(
                out=ln_tok[sl][:], in0=xt[sl][:], scalar1=mv[sl][:, 0:1],
                scalar2=rstd[sl][:], op0=ALU.subtract, op1=ALU.mult),
                waits=[tok_rstd] + w_lt, inc="cV")
            lastx[sl] = tok_nrm

            for grp in range(8):
                pp = (tau * 8 + grp) % 2
                prev_cp = tok_lncp.get(tau * 8 + grp - 2)

                def f_tr(e, sl=sl, grp=grp, pp=pp):
                    last = None
                    for i in range(4):
                        dc = grp * 4 + i
                        last = e.transpose(out=tpsA[pp][:, 128 * i:128 * (i + 1)],
                                           in_=ln_tok[sl][:, 128 * dc:128 * (dc + 1)],
                                           identity=cst["identb"][:])
                    return last
                tok_tr = st("tensor", f_tr, waits=[tok_nrm, prev_cp, tok_consts],
                            inc="cP")
                tok_trp[(tau, grp)] = tok_tr
                if tau < NQT:
                    def f_cp(e, tau=tau, grp=grp, pp=pp):
                        last = None
                        for i in range(4):
                            dc = grp * 4 + i
                            last = e.tensor_copy(
                                out=ln1_own[dc][:, 128 * tau:128 * (tau + 1)],
                                in_=tpsA[pp][:, 128 * i:128 * (i + 1)])
                        return last
                else:
                    def f_cp(e, tau=tau, grp=grp, pp=pp):
                        col = ((tau - NQT) % 2) * 128
                        last = None
                        for i in range(4):
                            dc = grp * 4 + i
                            last = e.tensor_copy(
                                out=ln1_par[dc][:, col:col + 128],
                                in_=tpsA[pp][:, 128 * i:128 * (i + 1)])
                        return last
                wkv = [tok_kvpar.get((tau - NQT) // 2 - 1)] if tau >= 10 else []
                tok_lncp[tau * 8 + grp] = st("vector", f_cp,
                                             waits=[tok_tr] + wkv, inc="cV")

            if tau in (9, 11, 13, 15):
                g = (tau - 9) // 2
                w_ln = [tok_lncp[tau * 8 + 7]]
                tok_mm_d = {}
                for dc in range(NDC):
                    def f_mm(e, dc=dc):
                        w0 = wkvc_sb[:, dc * 576:(dc + 1) * 576]
                        for r in range(4):
                            reg = pj[r // 2][:, (r % 2) * 512:(r % 2) * 512 + 256]
                            e.matmul(out=reg,
                                     lhsT=w0[:, r * 128:(r + 1) * 128],
                                     rhs=ln1_par[dc][:],
                                     start=(dc == 0), stop=(dc == NDC - 1))
                        return e.matmul(out=pj[2][0:64, 0:256],
                                        lhsT=w0[:, 512:576],
                                        rhs=ln1_par[dc][:],
                                        start=(dc == 0), stop=(dc == NDC - 1))
                    tok_mm_d[dc] = st("tensor", f_mm,
                                      waits=[tok_wkvc] + w_ln +
                                            ([tok_kv_ev.get(g - 1)] if dc == 0 else []),
                                      inc="cP")
                tok_kvpar[g] = tok_mm_d[NDC - 1]
                col0 = TOK + 256 * g

                def f_ev(e, col0=col0):
                    for r in range(4):
                        reg = pj[r // 2][:, (r % 2) * 512:(r % 2) * 512 + 256]
                        e.activation(out=kv_nope[r][:, col0:col0 + 256],
                                     in_=reg, func=AF.Identity,
                                     bias=cst["bkvc"][:, r:r + 1])
                    e.activation(out=krz0[0:64, col0:col0 + 256],
                                 in_=pj[2][0:64, 0:256], func=AF.Identity,
                                 bias=cst["bkvc"][0:64, 4:5])
                    return e.activation(out=krz1[64:128, col0:col0 + 256],
                                        in_=pj[2][0:64, 0:256], func=AF.Identity,
                                        bias=cst["bkvc"][0:64, 4:5])
                tok_kv_ev[g] = st("scalar", f_ev,
                                  waits=[tok_kvpar[g], tok_consts], inc="cA")

        if TRUNC == 3:
            S.q["sync"].append((None, [("cA", S.sem_n["cA"]),
                                       ("cV", S.sem_n["cV"]),
                                       ("cP", S.sem_n["cP"])], None, 0, S.phase))
            block = octx.enter_context(nc.Block())
            S.build(block)
            actx.close(); qctx_dummy = None
            bctx.close(); kv_ctx.close(); wctx4.close()
            return nc, dbg
        S.phase = "P1a"
        tok_qc_ev = {}
        w_ln_own = [tok_lncp[(NQT - 1) * 8 + 7]]
        last_mm = None
        tok_wqcg = {}
        tok_qcmm_blk = {}
        for blk in range(16):
            fcg, qr = blk // 4, blk % 4
            wsl = blk % 2
            wg_pre = [tok_qcmm_blk.get(blk - 2)]
            def f_wqc(e, fcg=fcg, qr=qr, wsl=wsl):
                s_ = wqc[fcg][qr * 1024:(qr + 1) * 1024, :]
                s_ = s_.rearrange("(d p) c -> p d c", p=128)
                d_ = wqc_sb[wsl][:].rearrange("p (d c) -> p d c", d=8)
                return e.dma_start(out=d_, in_=s_)
            tok_wqcg[blk] = st("gpsimd", f_wqc, waits=wg_pre,
                               inc=f"dW{wsl}", inc_by=16)
            for dc8 in range(8):
                dc = qr * 8 + dc8
                wev = ([tok_qc_ev.get(fcg - 1), tok_kv_ev.get(3)]
                       if dc == 0 else [])
                def f_mm(e, dc=dc, dc8=dc8, wsl=wsl):
                    last = None
                    w0 = wqc_sb[wsl][:, dc8 * 384:(dc8 + 1) * 384]
                    for fc3 in range(3):
                        for col in range(2):
                            last = e.matmul(
                                out=pj[fc3][:, col * 512:(col + 1) * 512],
                                lhsT=w0[:, fc3 * 128:(fc3 + 1) * 128],
                                rhs=ln1_own[dc][:, col * 512:(col + 1) * 512],
                                start=(dc == 0), stop=(dc == NDC - 1))
                    return last
                last_mm = st("tensor", f_mm,
                             waits=[tok_wqcg[blk]] + w_ln_own + wev, inc="cP")
            tok_qcmm_blk[blk] = last_mm
            if qr == 3:
                def f_ev(e, fcg=fcg):
                    last = None
                    for fc3 in range(3):
                        fc = fcg * 3 + fc3
                        last = e.activation(out=q_c[fc], in_=pj[fc3][:],
                                            func=AF.Identity,
                                            bias=cst["bqc"][:, fc:fc + 1])
                    return last
                tok_qc_ev[fcg] = st("scalar", f_ev, waits=[last_mm], inc="cA")

        if TRUNC == 4:
            S.q["sync"].append((None, [("cA", S.sem_n["cA"]),
                                       ("cV", S.sem_n["cV"]),
                                       ("cP", S.sem_n["cP"])], None, 0, S.phase))
            block = octx.enter_context(nc.Block())
            S.build(block)
            actx.close()
            bctx.close(); kv_ctx.close(); wctx4.close()
            return nc, dbg
        tok_kvo_ev = {}
        for p in range(2):
            for dc in range(NDC):
                wev = []
                if dc == 0:
                    wev = [tok_qc_ev[3]] if p == 0 else [tok_kvo_ev[0]]
                def f_mm(e, dc=dc, p=p):
                    w0 = wkvc_sb[:, dc * 576:(dc + 1) * 576]
                    last = None
                    if p == 0:
                        for c in range(3):
                            for col in range(2):
                                last = e.matmul(
                                    out=pj[c][:, col * 512:(col + 1) * 512],
                                    lhsT=w0[:, c * 128:(c + 1) * 128],
                                    rhs=ln1_own[dc][:, col * 512:(col + 1) * 512],
                                    start=(dc == 0), stop=(dc == NDC - 1))
                    else:
                        for col in range(2):
                            e.matmul(out=pj[0][:, col * 512:(col + 1) * 512],
                                     lhsT=w0[:, 384:512],
                                     rhs=ln1_own[dc][:, col * 512:(col + 1) * 512],
                                     start=(dc == 0), stop=(dc == NDC - 1))
                            last = e.matmul(
                                out=pj[1][0:64, col * 512:(col + 1) * 512],
                                lhsT=w0[:, 512:576],
                                rhs=ln1_own[dc][:, col * 512:(col + 1) * 512],
                                start=(dc == 0), stop=(dc == NDC - 1))
                    return last
                last_mm = st("tensor", f_mm, waits=[tok_wkvc] + wev, inc="cP")
            if p == 0:
                def f_ev(e):
                    last = None
                    for c in range(3):
                        last = e.activation(out=kv_nope[c][:, 0:TOK],
                                            in_=pj[c][:], func=AF.Identity,
                                            bias=cst["bkvc"][:, c:c + 1])
                    return last
            else:
                def f_ev(e):
                    e.activation(out=kv_nope[3][:, 0:TOK], in_=pj[0][:],
                                 func=AF.Identity, bias=cst["bkvc"][:, 3:4])
                    e.activation(out=krz0[0:64, 0:TOK],
                                 in_=pj[1][0:64, 0:1024], func=AF.Identity,
                                 bias=cst["bkvc"][0:64, 4:5])
                    return e.activation(out=krz1[64:128, 0:TOK],
                                        in_=pj[1][0:64, 0:1024], func=AF.Identity,
                                        bias=cst["bkvc"][0:64, 4:5])
            tok_kvo_ev[p] = st("scalar", f_ev, waits=[last_mm], inc="cA")
        if TRUNC == 5:
            S.q["sync"].append((None, [("cA", S.sem_n["cA"]),
                                       ("cV", S.sem_n["cV"]),
                                       ("cP", S.sem_n["cP"])], None, 0, S.phase))
            block = octx.enter_context(nc.Block())
            S.build(block)
            actx.close()
            bctx.close(); kv_ctx.close(); wctx4.close()
            return nc, dbg
        tok_kv_done = tok_kvo_ev[1]

        if debug:
            for i in range(4):
                dump(f"kv_nope{i}", kv_nope[i][:], [128, SEQ], tok_kv_done, BF16)
            dump("krope", krz0[:], [128, SEQ], tok_kv_done, BF16)
            dump("qc0", q_c[0], [128, TOK], tok_qc_ev[0], BF16)
            dump("ln1o0", ln1_own[0][:], [128, TOK], w_ln_own[0], BF16)

        S.barrier()
        actx.close()

        S.phase = "P1b"
        qctx = ExitStack()
        q_fm = [sb(f"qfm{i}", [128, TOK], BF16, qctx) for i in range(NQF)]
        p1b_ctx = ExitStack()
        wq2 = [sb(f"wq2_{i}", [128, NQC * 512], BF16, p1b_ctx)
               for i in range(2)]
        p1b_p = ExitStack()
        pu = [ps(f"pu{i}", [128, 1024], F32, p1b_p) for i in range(4)]
        tok_qfm_ev = {}
        last_mm = None
        tok_wq2 = {}
        for fcg in range(NQF // 4):
            wsl = fcg % 2
            wg_pre = [tok_qfm_ev.get(fcg - 2)]
            def f_w(e, fcg=fcg, wsl=wsl):
                s_ = wqup[fcg].rearrange("(q p) c -> p q c", p=128)
                d_ = wq2[wsl][:].rearrange("p (q c) -> p q c", q=NQC)
                return e.dma_start(out=d_, in_=s_)
            tok_wq2[fcg] = st("gpsimd", f_w, waits=wg_pre,
                              inc=f"dW{wsl}", inc_by=16)
            for qc in range(NQC):
                wev = [tok_qfm_ev.get(fcg - 1)] if qc == 0 else []
                def f_mm(e, qc=qc, wsl=wsl):
                    last = None
                    w0 = wq2[wsl][:, qc * 512:(qc + 1) * 512]
                    for fc4 in range(4):
                        for col in range(2):
                            last = e.matmul(
                                out=pu[fc4][:, col * 512:(col + 1) * 512],
                                lhsT=w0[:, fc4 * 128:(fc4 + 1) * 128],
                                rhs=qc_all[:, qc * TOK + col * 512:
                                           qc * TOK + (col + 1) * 512],
                                start=(qc == 0), stop=(qc == NQC - 1))
                    return last
                last_mm = st("tensor", f_mm, waits=[tok_wq2[fcg]] + wev,
                             inc="cP")

            def f_ev(e, fcg=fcg):
                last = None
                for fc4 in range(4):
                    fc = fcg * 4 + fc4
                    last = e.activation(out=q_fm[fc][:], in_=pu[fc4][:],
                                        func=AF.Identity, scale=float(SQ),
                                        bias=bqup_s[:, fc:fc + 1])
                return last
            tok_qfm_ev[fcg] = st("scalar", f_ev, waits=[last_mm, tok_bqups],
                                 inc="cA")
        if debug:
            dump("qfm0", q_fm[0][:], [128, TOK], tok_qfm_ev[0], BF16)
            dump("qfm47", q_fm[47][:], [128, TOK], tok_qfm_ev[11], BF16)

        S.barrier()
        p1b_p.close()
        p1b_ctx.close()

        if TRUNC == 1:
            block = octx.enter_context(nc.Block())
            S.build(block)
            qctx.close(); bctx.close(); kv_ctx.close(); wctx4.close()
            return nc, dbg
        attn_ctx = ExitStack()
        kf = [sb(f"kf{i}", [128, SEQ], BF16, attn_ctx) for i in range(2)]
        vtg = [sb(f"vtg{i}", [128, 16 * 512], BF16, attn_ctx) for i in range(2)]
        wkh = [sb(f"wkh{i}", [128, 512], BF16, attn_ctx) for i in range(2)]
        wkvv = [sb(f"wkvv{i}", [128, 2048], BF16, attn_ctx) for i in range(2)]
        bvg = [sb(f"bvg{i}", [1, 512], BF16, attn_ctx) for i in range(2)]
        ring = qc_all[:, 4096:4096 + 3 * TOK]
        sums = sb("sums", [128, 8], F32, attn_ctx)
        sumsA = sb("sumsA", [128, 8], F32, attn_ctx)
        negm = sb("negm", [128, 8], F32, attn_ctx)
        lnS = sb("lnS", [128, 8], F32, attn_ctx)
        augc = sb("augc", [128, 8], F32, attn_ctx)
        hic_b = sb("hic_b", [128, 8], BF16, attn_ctx)
        hilo_f = sb("hilo_f", [128, 16], F32, attn_ctx)
        augMIX = qc_all[0:2, 3072:3072 + TOK]
        ones2 = sb("ones2", [2, 128], BF16, attn_ctx)
        negmA = sb("negmA", [128, 8], F32, attn_ctx)
        s1 = ps("s1", [128, SEQ], F32, attn_ctx)
        s2 = ps("s2", [128, TOK], F32, attn_ctx)
        osum = ps("osum", [128, TOK], F32, attn_ctx)

        S.phase = "ATTN"
        tok_kw = {}
        tok_kmm = {}
        tok_kfev = {}
        tok_vev = {}
        tok_exp1 = {}
        tok_exp2 = {}
        tok_avmm = {}
        tok_maxs = {}
        tok_stt = {}
        tok_augT = {}
        tok_augrow = {}
        disc = qc_all[:, 0:SEQ]
        tok_o2 = st("gpsimd", lambda e: e.memset(ones2[:], 1.0),
                    waits=[tok_krz_z], inc="cG")

        def emit_augpre(hp):
            def f_augT(e, hp=hp):
                last = None
                for qt in range(8):
                    last = e.matmul(out=s1[0:2, qt * 128:(qt + 1) * 128],
                                    lhsT=hilo_f[:, 2 * qt:2 * qt + 2],
                                    rhs=cst["identf"][:],
                                    start=True, stop=True)
                return last
            tok_augT[hp] = st("tensor", f_augT,
                              waits=[tok_stt[(hp, qt)] for qt in range(8)] +
                                    [tok_exp1[(hp, 7)]], inc="cP")
            tok_augrow[hp] = st("vector", lambda e: e.tensor_copy(
                out=augMIX[:], in_=s1[0:2, 0:TOK]), waits=[tok_augT[hp]],
                inc="cV")

        for i in range(NH + 1):
            ia = i
            ib = i - 1
            if ib >= 0:
                emit_augpre(ib)
            if ia < NH:
                g, hg = ia // GH, ia % GH
                hsl = ia % 2
                gsl = g % 2
                if hg == 0:
                    wpre = [tok_kmm.get(4 * (g - 2) + 3)] if g >= 2 else []
                    st("gpsimd", lambda e, g=g, gsl=gsl: e.dma_start(
                        out=bvg[gsl][:],
                        in_=c_in["bkvupv"][0:1, g * 512:(g + 1) * 512]),
                       waits=wpre, inc=f"dKW{gsl}", inc_by=16)
                    for kvc in range(4):
                        tok_kw[g] = st("gpsimd",
                                       lambda e, g=g, gsl=gsl, kvc=kvc: e.dma_start(
                                           out=wkvv[gsl][:, kvc * 512:(kvc + 1) * 512],
                                           in_=wkvup[kvc * 128:(kvc + 1) * 128,
                                                     4096 + g * 512:4096 + (g + 1) * 512]),
                                       waits=wpre, inc=f"dKW{gsl}", inc_by=16)
                    for rnd in range(4):
                        w_s1 = [tok_augrow.get(ib) if rnd == 0 else
                                tok_vev.get((g, rnd - 1))]
                        def f_vmm(e, g=g, gsl=gsl, rnd=rnd):
                            last = None
                            for t4 in range(4):
                                tc = rnd * 4 + t4
                                reg = s1[:, t4 * 512:(t4 + 1) * 512]
                                for kvc in range(4):
                                    e.matmul(out=reg,
                                             lhsT=kv_nope[kvc][:, tc * 128:(tc + 1) * 128],
                                             rhs=wkvv[gsl][:, kvc * 512:(kvc + 1) * 512],
                                             start=(kvc == 0), stop=False)
                                last = e.matmul(
                                    out=reg, lhsT=ones_b[:],
                                    rhs=bvg[gsl][:],
                                    start=False, stop=True)
                            return last
                        tok_vmm = st("tensor", f_vmm,
                                     waits=w_s1 + [tok_kw[g], tok_ones], inc="cP")
                        w_vtg = [tok_avmm.get((4 * (g - 2) + 3, 15))] if g >= 2 else []
                        def f_vev(e, gsl=gsl, rnd=rnd):
                            return e.activation(
                                out=vtg[gsl][:, rnd * 2048:(rnd + 1) * 2048],
                                in_=s1[:], func=AF.Copy)
                        tok_vev[(g, rnd)] = st("scalar", f_vev,
                                               waits=[tok_vmm] + w_vtg, inc="cA")
                    s1_free = tok_vev[(g, 3)]
                else:
                    s1_free = tok_augrow.get(ib)
                wkpre = [tok_kmm.get(ia - 2)]
                tok_wk = None
                for kvc in range(4):
                    tok_wk = st("gpsimd", lambda e, ia=ia, hsl=hsl, kvc=kvc: e.dma_start(
                        out=wkh[hsl][:, kvc * 128:(kvc + 1) * 128],
                        in_=wkvup[kvc * 128:(kvc + 1) * 128,
                                  ia * 128:(ia + 1) * 128]),
                        waits=wkpre, inc=f"dW{hsl}", inc_by=16)
                tok_kfh = {}
                for kh2 in range(2):
                    def f_kmm(e, hsl=hsl, kh2=kh2):
                        last = None
                        for col in range(2 * kh2, 2 * kh2 + 2):
                            for kvc in range(4):
                                last = e.matmul(
                                    out=s1[:, col * 512:(col + 1) * 512],
                                    lhsT=wkh[hsl][:, kvc * 128:(kvc + 1) * 128],
                                    rhs=kv_nope[kvc][:, col * 512:(col + 1) * 512],
                                    start=(kvc == 0), stop=(kvc == 3))
                        return last
                    tok_kmm[ia] = st("tensor", f_kmm,
                                     waits=[s1_free if kh2 == 0 else None,
                                            tok_wk], inc="cP")
                    def f_kev(e, hsl=hsl, ia=ia, kh2=kh2):
                        return e.activation(
                            out=kf[hsl][:, kh2 * TOK:(kh2 + 1) * TOK],
                            in_=s1[:, kh2 * TOK:(kh2 + 1) * TOK],
                            func=AF.Identity, bias=cst["bkvupk"][:, ia:ia + 1])
                    tok_kfh[kh2] = st("scalar", f_kev, waits=[tok_kmm[ia]],
                                      inc="cA")
                tok_kfev[ia] = tok_kfh[1]
                tok_kfev0 = tok_kfh[0]
                nope_t = q_fm[ia]
                rope_t = q_fm[32 + ia // 2]
                rb = (ia % 2) * 64
                tok_e1a_prev = None

            for kc in range(16):
                if ib >= 0:
                    gb, hgb = ib // GH, ib % GH
                    hslb = ib % 2
                    rbb = (ib % 2) * 64
                    rope_b = q_fm[32 + ib // 2]
                    tok_s2h = {}
                    for ch in range(2):
                        w_s2 = [tok_exp2.get((ib, kc - 1, ch)) if kc > 0 else
                                tok_exp2.get((ib - 1, 15, ch))]
                        krzb = krz0 if ib % 2 == 0 else krz1
                        def f_s2(e, ib=ib, kc=kc, hslb=hslb,
                                 rope_b=rope_b, col=ch, krzb=krzb):
                            reg = s2[:, col * 512:(col + 1) * 512]
                            e.matmul(out=reg,
                                     lhsT=kf[hslb][:, kc * 128:(kc + 1) * 128],
                                     rhs=q_fm[ib][:, col * 512:(col + 1) * 512],
                                     start=True, stop=False)
                            e.matmul(out=reg,
                                     lhsT=krzb[:, kc * 128:(kc + 1) * 128],
                                     rhs=rope_b[:, col * 512:(col + 1) * 512],
                                     start=False, stop=False)
                            return e.matmul(
                                out=reg, lhsT=ones2[:],
                                rhs=augMIX[:, col * 512:(col + 1) * 512],
                                start=False, stop=True)
                        tok_s2h[ch] = st("tensor", f_s2,
                                         waits=w_s2 + [tok_augrow[ib]], inc="cP")
                    if kc >= 1:
                        def f_av(e, ib=ib, kc=kc, hgb=hgb, gb=gb):
                            last = None
                            rsl = (kc - 1) % 3
                            for col in range(2):
                                last = e.matmul(
                                    out=osum[:, col * 512:(col + 1) * 512],
                                    lhsT=vtg[gb % 2][:, (kc - 1) * 512 + hgb * 128:
                                                     (kc - 1) * 512 + (hgb + 1) * 128],
                                    rhs=ring[:, rsl * TOK + col * 512:
                                             rsl * TOK + (col + 1) * 512],
                                    start=(ib == 0 and kc == 1),
                                    stop=False)
                            return last
                        tok_avmm[(ib, kc - 1)] = st(
                            "tensor", f_av,
                            waits=[tok_exp2[(ib, kc - 1, 1)], tok_vev[(gb, 3)]],
                            inc="cP")
                    rsl = kc % 3
                    prev_kc = kc - 3 if kc >= 3 else 15 - ((15 - kc) % 3)
                    w_ring = [tok_avmm.get((ib, prev_kc)) if kc >= 3 else
                              (tok_avmm.get((ib - 1, prev_kc)) if ib > 0 else None)]
                    for ch in range(2):
                        tok_exp2[(ib, kc, ch)] = st(
                            "scalar", lambda e, rsl=rsl, ch=ch: e.activation(
                                out=ring[:, rsl * TOK + ch * 512:
                                         rsl * TOK + (ch + 1) * 512],
                                in_=s2[:, ch * 512:(ch + 1) * 512],
                                func=AF.Exp),
                            waits=[tok_s2h[ch]] + w_ring, inc="cA")
                if ia < NH and kc % 2 == 0:
                    qt = kc // 2
                    w_s1 = [tok_kfev0 if qt == 0 else tok_e1a_prev]
                    krz = krz0 if ia % 2 == 0 else krz1
                    def f_s1a(e, qt=qt, hsl=hsl, nope_t=nope_t,
                              rope_t=rope_t, krz=krz):
                        last = None
                        for col in range(2):
                            e.matmul(out=s1[:, col * 512:(col + 1) * 512],
                                     lhsT=nope_t[:, qt * 128:(qt + 1) * 128],
                                     rhs=kf[hsl][:, col * 512:(col + 1) * 512],
                                     start=True, stop=False)
                            last = e.matmul(
                                out=s1[:, col * 512:(col + 1) * 512],
                                lhsT=rope_t[:, qt * 128:(qt + 1) * 128],
                                rhs=krz[:, col * 512:(col + 1) * 512],
                                start=False, stop=True)
                        return last
                    tok_s1a = st("tensor", f_s1a, waits=w_s1, inc="cP")
                    def f_s1b(e, qt=qt, hsl=hsl, nope_t=nope_t,
                              rope_t=rope_t, krz=krz):
                        last = None
                        for col in range(2, 4):
                            e.matmul(out=s1[:, col * 512:(col + 1) * 512],
                                     lhsT=nope_t[:, qt * 128:(qt + 1) * 128],
                                     rhs=kf[hsl][:, col * 512:(col + 1) * 512],
                                     start=True, stop=False)
                            last = e.matmul(
                                out=s1[:, col * 512:(col + 1) * 512],
                                lhsT=rope_t[:, qt * 128:(qt + 1) * 128],
                                rhs=krz[:, col * 512:(col + 1) * 512],
                                start=False, stop=True)
                        return last
                    tok_s1b = st("tensor", f_s1b,
                                 waits=[tok_kfev[ia] if qt == 0 else
                                        tok_exp1[(ia, qt - 1)]], inc="cP")
                    w_negm = [tok_exp1.get((ia - 1, qt))] if ia > 0 else []
                    t_mA = st("vector", lambda e, qt=qt: e.reduce_max(
                        out=negmA[:, qt:qt + 1], in_=s1[:, 0:TOK], axis=AX.X,
                        negate=True), waits=[tok_s1a] + w_negm, inc="cV")
                    t_mB = st("vector", lambda e, qt=qt: e.reduce_max(
                        out=negm[:, qt:qt + 1], in_=s1[:, TOK:SEQ], axis=AX.X,
                        negate=True), waits=[tok_s1b] + w_negm, inc="cV")
                    tok_maxs[(ia, qt)] = st(
                        "vector", lambda e, qt=qt: e.tensor_tensor(
                            out=negm[:, qt:qt + 1], in0=negm[:, qt:qt + 1],
                            in1=negmA[:, qt:qt + 1], op=ALU.min),
                        waits=[t_mB], inc="cV")
                    tok_e1a = st(
                        "scalar", lambda e, qt=qt: e.activation(
                            out=disc[:, 0:TOK], in_=s1[:, 0:TOK], func=AF.Exp,
                            bias=negm[:, qt:qt + 1],
                            accum_out=sumsA[:, qt:qt + 1]),
                        waits=[tok_maxs[(ia, qt)]], inc="cA")
                    tok_exp1[(ia, qt)] = st(
                        "scalar", lambda e, qt=qt: e.activation(
                            out=disc[:, TOK:SEQ], in_=s1[:, TOK:SEQ],
                            func=AF.Exp, bias=negm[:, qt:qt + 1],
                            accum_out=sums[:, qt:qt + 1]),
                        waits=[tok_e1a], inc="cA")
                    tok_e1a_prev = tok_e1a
                    tok_sad = st("vector", lambda e, qt=qt: e.tensor_tensor(
                        out=sums[:, qt:qt + 1], in0=sums[:, qt:qt + 1],
                        in1=sumsA[:, qt:qt + 1], op=ALU.add),
                        waits=[tok_exp1[(ia, qt)]], inc="cV")
                    tok_lns = st("scalar", lambda e, qt=qt: e.activation(
                        out=lnS[:, qt:qt + 1], in_=sums[:, qt:qt + 1],
                        func=AF.Ln), waits=[tok_sad], inc="cA")
                    t_st = st("vector", lambda e, qt=qt: e.scalar_tensor_tensor(
                        out=augc[:, qt:qt + 1], in0=lnS[:, qt:qt + 1],
                        scalar=-1.0, in1=negm[:, qt:qt + 1],
                        op0=ALU.mult, op1=ALU.add),
                        waits=[tok_lns, tok_maxs[(ia, qt)],
                               tok_augT.get(ia - 1)], inc="cV")
                    t_hb = st("vector", lambda e, qt=qt: e.tensor_copy(
                        out=hic_b[:, qt:qt + 1], in_=augc[:, qt:qt + 1]),
                        waits=[t_st], inc="cV")
                    t_hf = st("vector", lambda e, qt=qt: e.tensor_copy(
                        out=hilo_f[:, 2 * qt:2 * qt + 1],
                        in_=hic_b[:, qt:qt + 1]), waits=[t_hb], inc="cV")
                    tok_stt[(ia, qt)] = st(
                        "vector", lambda e, qt=qt: e.tensor_tensor(
                            out=hilo_f[:, 2 * qt + 1:2 * qt + 2],
                            in0=augc[:, qt:qt + 1],
                            in1=hilo_f[:, 2 * qt:2 * qt + 1],
                            op=ALU.subtract), waits=[t_hf], inc="cV")
            if ib >= 0:
                gb, hgb = ib // GH, ib % GH
                def f_av15(e, ib=ib, hgb=hgb, gb=gb):
                    last = None
                    rsl = 15 % 3
                    for col in range(2):
                        last = e.matmul(
                            out=osum[:, col * 512:(col + 1) * 512],
                            lhsT=vtg[gb % 2][:, 15 * 512 + hgb * 128:
                                             15 * 512 + (hgb + 1) * 128],
                            rhs=ring[:, rsl * TOK + col * 512:
                                     rsl * TOK + (col + 1) * 512],
                            start=False,
                            stop=(ib == NH - 1))
                    return last
                tok_avmm[(ib, 15)] = st(
                    "tensor", f_av15,
                    waits=[tok_exp2[(ib, 15, 1)], tok_vev[(gb, 3)]], inc="cP")
        last_av = tok_avmm[(NH - 1, 15)]

        tok_on = st("vector", lambda e: e.tensor_copy(out=o_n[:], in_=osum[:]),
                    waits=[last_av], inc="cV")
        if debug:
            dump("o_n", o_n[:], [128, TOK], tok_on, BF16)
            dump("augrow", augMIX[0:1, :], [1, TOK], tok_augrow[NH - 1], BF16)

        S.barrier()
        attn_ctx.close()
        qctx.close()
        bctx.close()
        kv_ctx.close()
        if TRUNC == 2:
            block = octx.enter_context(nc.Block())
            S.build(block)
            wctx4.close()
            return nc, dbg

        S.phase = "W"
        ln2_fm = sb("ln2_fm", [128, NDC * TOK], BF16, wctx4)
        wctx = ExitStack()
        wo_sb = sb("wo_sb", [128, DM], BF16, wctx)
        xa_fm = [sb(f"xafm{i}", [128, TOK], BF16, wctx) for i in range(2)]
        xa_tok = sb("xa_tok", [128, NQT * DM], BF16, wctx)
        xo = sb("xo", [128, DM], F32, wctx)
        x1t = sb("x1t", [128, DM], F32, wctx)
        lnA = sb("lnA", [128, DM], BF16, wctx)
        ln2_tok = sb("ln2_tok", [128, DM], BF16, wctx)
        stat6w = sb("stat6w", [128, 48], F32, wctx)
        mvw = sb("mvw", [128, 2], F32, wctx)
        rstdw = sb("rstdw", [128, 1], F32, wctx)
        lnvw = sb("lnvw", [128, 1], F32, wctx)
        wops = [ps(f"wops{i}", [128, TOK], F32, wctx) for i in range(2)]
        tpsW = [ps(f"tpsW{i}", [128, TOK], BF16, wctx) for i in range(2)]

        tok_wo = st("gpsimd", lambda e: e.dma_start(out=wo_sb[:], in_=wo),
                    inc="dKW0", inc_by=16)
        tok_wo_ev = {}
        tok_tcp = {}
        for Hc in range(NHC):
            wsl = Hc % 2
            w_ps = [tok_wo_ev.get(Hc - 2)]

            def f_wmm(e, Hc=Hc, wsl=wsl):
                last = None
                for col in range(2):
                    last = e.matmul(
                        out=wops[wsl][:, col * 512:(col + 1) * 512],
                        lhsT=wo_sb[:, Hc * 128:(Hc + 1) * 128],
                        rhs=o_n[:, col * 512:(col + 1) * 512],
                        start=True, stop=True)
                return last
            tok_wmm = st("tensor", f_wmm, waits=[tok_wo, tok_on] + w_ps,
                         inc="cP")
            w_fm = [tok_tcp.get(Hc - 2)]
            tok_ev = st("scalar", lambda e, Hc=Hc, wsl=wsl: e.activation(
                out=xa_fm[wsl][:], in_=wops[wsl][:], func=AF.Identity,
                bias=cst["bo"][:, Hc:Hc + 1]), waits=[tok_wmm] + w_fm, inc="cA")
            tok_wo_ev[Hc] = tok_ev

            def f_tw(e, wsl=wsl):
                last = None
                for qt in range(8):
                    last = e.transpose(out=tpsW[wsl][:, qt * 128:(qt + 1) * 128],
                                       in_=xa_fm[wsl][:, qt * 128:(qt + 1) * 128],
                                       identity=cst["identb"][:])
                return last
            tok_tw = st("tensor", f_tw, waits=[tok_ev] + [tok_tcp.get(Hc - 2)],
                        inc="cP")

            def f_tc(e, Hc=Hc, wsl=wsl):
                src = tpsW[wsl][:].rearrange("p (q c) -> p q c", q=8)
                dst = xa_tok[:].rearrange("p (q d) -> p q d", q=8)[
                    :, :, Hc * 128:(Hc + 1) * 128]
                return e.tensor_copy(out=dst, in_=src)
            tok_tcp[Hc] = st("vector", f_tc, waits=[tok_tw], inc="cV")

        tok_xa_done = tok_tcp[NHC - 1]
        if debug:
            dump("xa_tok", xa_tok[:, 0:DM], [128, DM], tok_xa_done, BF16)

        tok_ln2cp = {}
        tok_x1_lastread = {}
        for qt in range(NQT):
            xa = xa_tok[:, qt * DM:(qt + 1) * DM]
            tok_xod = st("sync", lambda e, qt=qt: e.dma_start(
                out=xo[:], in_=x_seq[qt * 128:(qt + 1) * 128, :]),
                waits=[tok_x1_lastread.get(qt - 1)], inc="dXO", inc_by=16)

            def f_st(e, xa=xa):
                last = None
                for i in range(8):
                    last = e.bn_stats(out=stat6w[:, 6 * i:6 * i + 6],
                                      in_=xa[:, 512 * i:512 * (i + 1)])
                return last
            tok_sw8 = st("vector", f_st, waits=[tok_xa_done], inc="cV")
            tok_sw = st("vector", lambda e: e.bn_aggr(
                out=mvw[:], in_=stat6w[:]), waits=[tok_sw8], inc="cV")
            tok_lnw = st("scalar", lambda e: e.activation(
                out=lnvw[:], in_=mvw[:, 1:2], func=AF.Ln, bias=epsT[:]),
                waits=[tok_sw], inc="cA")
            tok_rsw = st("scalar", lambda e: e.activation(
                out=rstdw[:], in_=lnvw[:], func=AF.Exp, scale=-0.5),
                waits=[tok_lnw], inc="cA")
            tok_nw = st("vector", lambda e, xa=xa: e.tensor_scalar(
                out=lnA[:], in0=xa, scalar1=mvw[:, 0:1], scalar2=rstdw[:],
                op0=ALU.subtract, op1=ALU.mult), waits=[tok_rsw], inc="cV")
            tok_x1 = st("vector", lambda e: e.tensor_tensor(
                out=x1t[:], in0=lnA[:], in1=xo[:], op=ALU.add),
                waits=[tok_nw, tok_xod, ("dX1W", 16 * qt)], inc="cV")
            st("sync", lambda e, qt=qt: e.dma_start(
                out=x1_dram[qt * 128:(qt + 1) * 128, :], in_=x1t[:]),
                waits=[tok_x1], inc="dX1W", inc_by=16)

            def f_st2(e):
                last = None
                for i in range(8):
                    last = e.bn_stats(out=stat6w[:, 6 * i:6 * i + 6],
                                      in_=x1t[:, 512 * i:512 * (i + 1)])
                return last
            tok_sw2a = st("vector", f_st2, waits=[tok_x1], inc="cV")
            tok_sw2 = st("vector", lambda e: e.bn_aggr(
                out=mvw[:], in_=stat6w[:]), waits=[tok_sw2a], inc="cV")
            tok_lnw2 = st("scalar", lambda e: e.activation(
                out=lnvw[:], in_=mvw[:, 1:2], func=AF.Ln, bias=epsT[:]),
                waits=[tok_sw2], inc="cA")
            tok_rsw2 = st("scalar", lambda e: e.activation(
                out=rstdw[:], in_=lnvw[:], func=AF.Exp, scale=-0.5),
                waits=[tok_lnw2], inc="cA")
            tok_nw2 = st("vector", lambda e: e.tensor_scalar(
                out=ln2_tok[:], in0=x1t[:], scalar1=mvw[:, 0:1],
                scalar2=rstdw[:], op0=ALU.subtract, op1=ALU.mult),
                waits=[tok_rsw2], inc="cV")
            tok_x1_lastread[qt] = tok_nw2
            for grp in range(8):
                pp = grp % 2
                w_tp = [tok_ln2cp.get((qt, grp - 2)) if grp >= 2 else
                        (tok_ln2cp.get((qt - 1, 6 + grp)) if qt > 0 else None)]

                def f_t2(e, pp=pp, grp=grp):
                    last = None
                    for i in range(4):
                        dc = grp * 4 + i
                        last = e.transpose(
                            out=tpsW[pp][:, i * 128:(i + 1) * 128],
                            in_=ln2_tok[:, dc * 128:(dc + 1) * 128],
                            identity=cst["identb"][:])
                    return last
                tok_t2 = st("tensor", f_t2, waits=[tok_nw2] + w_tp, inc="cP")

                def f_c2(e, pp=pp, grp=grp, qt=qt):
                    src = tpsW[pp][:, 0:512].rearrange("p (c f) -> p c f", c=4)
                    dst = ln2_fm[:].rearrange("p (d t) -> p d t", d=NDC)[
                        :, grp * 4:grp * 4 + 4, qt * 128:(qt + 1) * 128]
                    return e.tensor_copy(out=dst, in_=src)
                tok_ln2cp[(qt, grp)] = st("vector", f_c2, waits=[tok_t2],
                                          inc="cV")
        tok_ln2_done = tok_ln2cp[(NQT - 1, 7)]
        if debug:
            dump("x1_7", x1t[:], [128, DM], tok_x1_lastread[NQT - 1], F32)
            dump("ln2fm", ln2_fm[:, 0:TOK], [128, TOK], tok_ln2_done, BF16)

        S.q["sync"].append((None, [("dX1W", 16 * NQT)], None, 0, S.phase))
        S.waited["sync"]["dX1W"] = 16 * NQT
        S.barrier()
        wctx.close()

        S.phase = "F1"
        fctx = ExitStack()
        wgt = [sb(f"wgt{i}", [128, NDC * 128], BF16, fctx) for i in range(2)]
        wut = [sb(f"wut{i}", [128, NDC * 128], BF16, fctx) for i in range(2)]
        tmpg = [sb(f"tmpg{i}", [128, TOK], F32, fctx) for i in range(2)]
        tmpu = [sb(f"tmpu{i}", [128, TOK], F32, fctx) for i in range(2)]
        htile = [sb(f"htile{i}", [128, TOK], BF16, fctx) for i in range(2)]
        psg = [ps(f"psg{i}", [128, TOK], F32, fctx) for i in range(2)]
        psu2 = [ps(f"psu2{i}", [128, TOK], F32, fctx) for i in range(2)]
        tok_gev = {}
        tok_uev = {}
        tok_hmul = {}
        tok_f1mm = {}
        for fb in range(NFB):
            sl = fb % 2
            w_w = [tok_f1mm.get(fb - 2)]
            tok_wgd = st("gpsimd", lambda e, fb=fb, sl=sl: e.dma_start(
                out=wgt[sl][:], in_=wg[fb]), waits=w_w, inc=f"dWG{sl}",
                inc_by=16)
            tok_wud = st("gpsimd", lambda e, fb=fb, sl=sl: e.dma_start(
                out=wut[sl][:], in_=wu[fb]), waits=w_w, inc=f"dWU{sl}",
                inc_by=16)
            w_ps = [tok_gev.get(fb - 2), tok_uev.get(fb - 2)]

            def f_mm(e, sl=sl):
                last = None
                for dc in range(NDC):
                    for col in range(2):
                        e.matmul(out=psg[sl][:, col * 512:(col + 1) * 512],
                                 lhsT=wgt[sl][:, dc * 128:(dc + 1) * 128],
                                 rhs=ln2_fm[:, dc * TOK + col * 512:
                                            dc * TOK + (col + 1) * 512],
                                 start=(dc == 0), stop=(dc == NDC - 1))
                        last = e.matmul(
                            out=psu2[sl][:, col * 512:(col + 1) * 512],
                            lhsT=wut[sl][:, dc * 128:(dc + 1) * 128],
                            rhs=ln2_fm[:, dc * TOK + col * 512:
                                       dc * TOK + (col + 1) * 512],
                            start=(dc == 0), stop=(dc == NDC - 1))
                return last
            tok_f1mm[fb] = st("tensor", f_mm,
                              waits=[tok_wgd, tok_wud, tok_ln2_done] + w_ps,
                              inc="cP")
            w_tmp = [tok_hmul.get(fb - 2)]
            tok_g = st("scalar", lambda e, fb=fb, sl=sl: e.activation(
                out=tmpg[sl][:], in_=psg[sl][:], func=AF.Silu,
                scale=cst["gs"][:, fb:fb + 1]),
                waits=[tok_f1mm[fb]] + w_tmp, inc="cA")
            tok_gev[fb] = tok_g
            tok_u = st("vector", lambda e, fb=fb, sl=sl: e.tensor_scalar(
                out=tmpu[sl][:], in0=psu2[sl][:],
                scalar1=cst["us"][:, fb:fb + 1], scalar2=None, op0=ALU.mult),
                waits=[tok_f1mm[fb]] + w_tmp, inc="cV")
            tok_uev[fb] = tok_u
            hsem = f"dHW{sl}"
            w_h = [(hsem, 16 * (fb // 2))] if fb >= 2 else []
            tok_m = st("vector", lambda e, sl=sl: e.tensor_tensor(
                out=htile[sl][:], in0=tmpg[sl][:], in1=tmpu[sl][:],
                op=ALU.mult), waits=[tok_g, tok_u] + w_h, inc="cV")
            tok_hmul[fb] = tok_m
            st("sync", lambda e, fb=fb, sl=sl: e.dma_start(
                out=h_dram[fb], in_=htile[sl][:]), waits=[tok_m], inc=hsem,
                inc_by=16)
        tok_hw_all = [("dHW0", 16 * 43), ("dHW1", 16 * 43)]

        S.q["sync"].append((None, [("dHW0", 16 * 43), ("dHW1", 16 * 43)],
                            None, 0, S.phase))
        S.waited["sync"]["dHW0"] = 16 * 43
        S.waited["sync"]["dHW1"] = 16 * 43
        S.barrier()
        fctx.close()
        wctx4.close()

        S.phase = "F2"
        f2ctx = ExitStack()
        hh = sb("hh", [128, NFB * 512], BF16, f2ctx)
        wdt = [sb(f"wdt{i}", [128, NFB * 128], BF16, f2ctx) for i in range(2)]
        ff_fm = [sb(f"fffm{i}", [128, 512], BF16, f2ctx) for i in range(2)]
        ffn_tok = sb("ffn_tok", [128, 4 * DM], BF16, f2ctx)
        x1r = sb("x1r", [128, DM], F32, f2ctx)
        yt = sb("yt", [128, DM], F32, f2ctx)
        lnF = sb("lnF", [128, DM], BF16, f2ctx)
        stat6f = sb("stat6f", [128, 48], F32, f2ctx)
        mvf = sb("mvf", [128, 2], F32, f2ctx)
        rstdf = sb("rstdf", [128, 1], F32, f2ctx)
        lnvf = sb("lnvf", [128, 1], F32, f2ctx)
        dps = [ps(f"dps{i}", [128, 512], F32, f2ctx) for i in range(4)]
        tpsF = [ps(f"tpsF{i}", [128, 512], BF16, f2ctx) for i in range(2)]

        tok_y = {}
        tok_dmm = {}
        for half in range(2):
            w_hh = list(tok_hw_all)
            if half == 1:
                w_hh.append(tok_dmm[(0, NHC - 1)])

            fb0 = 0
            for n in [11] * 7 + [9]:
                def f_hh(e, half=half, fb0=fb0, n=n):
                    s_ = h_dram[fb0:fb0 + n, :, half * 512:(half + 1) * 512]
                    s_ = s_.rearrange("f p c -> p f c")
                    dst = hh[:].rearrange("p (f c) -> p f c", f=NFB)[
                        :, fb0:fb0 + n, :]
                    return e.dma_start(out=dst, in_=s_)
                tok_hh = st("sync", f_hh, waits=w_hh, inc="dHH", inc_by=16)
                fb0 += n
            tok_dev = {}
            tok_fcp = {}
            for j in range(NHC):
                sl = j % 2
                w_wd = [tok_dmm.get((half, j - 2)) if j >= 2 else
                        (tok_dmm.get((0, NHC - 2 + j)) if half == 1 else None)]
                tok_wdd = st("gpsimd", lambda e, j=j, sl=sl: e.dma_start(
                    out=wdt[sl][:], in_=wd[j]), waits=w_wd, inc=f"dWD{sl}",
                    inc_by=16)
                w_ps = [tok_dev.get(j - 4)]

                def f_dmm(e, j=j, sl=sl):
                    last = None
                    for fc in range(NFB):
                        last = e.matmul(
                            out=dps[j % 4][:],
                            lhsT=wdt[sl][:, fc * 128:(fc + 1) * 128],
                            rhs=hh[:, fc * 512:(fc + 1) * 512],
                            start=(fc == 0), stop=(fc == NFB - 1))
                    return last
                tok_dmm[(half, j)] = st("tensor", f_dmm,
                                        waits=[tok_wdd, ("dHH", 128 * (half + 1))] + w_ps,
                                        inc="cP")
                w_ff = [tok_fcp.get(j - 2)]
                tok_de = st("vector", lambda e, j=j, sl=sl: e.tensor_scalar(
                    out=ff_fm[sl][:], in0=dps[j % 4][:],
                    scalar1=cst["ds"][:, j:j + 1], scalar2=None, op0=ALU.mult),
                    waits=[tok_dmm[(half, j)]] + w_ff, inc="cV")
                tok_dev[j] = tok_de

                def f_tf(e, sl=sl):
                    last = None
                    for tt in range(4):
                        last = e.transpose(
                            out=tpsF[sl][:, tt * 128:(tt + 1) * 128],
                            in_=ff_fm[sl][:, tt * 128:(tt + 1) * 128],
                            identity=cst["identb"][:])
                    return last
                tok_tf = st("tensor", f_tf,
                            waits=[tok_de, tok_fcp.get(j - 2)], inc="cP")

                def f_cf(e, j=j, sl=sl):
                    src = tpsF[sl][:, 0:512].rearrange("p (t c) -> p t c", t=4)
                    dst = ffn_tok[:].rearrange("p (t d) -> p t d", t=4)[
                        :, :, j * 128:(j + 1) * 128]
                    return e.tensor_copy(out=dst, in_=src)
                tok_fcp[j] = st("vector", f_cf, waits=[tok_tf], inc="cV")
            tok_ffn_done = tok_fcp[NHC - 1]

            for tt in range(4):
                qt = half * 4 + tt
                ft = ffn_tok[:, tt * DM:(tt + 1) * DM]
                tok_x1r = st("sync", lambda e, qt=qt: e.dma_start(
                    out=x1r[:], in_=x1_dram[qt * 128:(qt + 1) * 128, :]),
                    waits=[tok_y.get(qt - 1), ("dX1W", 16 * NQT)],
                    inc="dX1R", inc_by=16)

                def f_sf(e, ft=ft):
                    last = None
                    for i in range(8):
                        last = e.bn_stats(out=stat6f[:, 6 * i:6 * i + 6],
                                          in_=ft[:, 512 * i:512 * (i + 1)])
                    return last
                tok_sf8 = st("vector", f_sf, waits=[tok_ffn_done], inc="cV")
                tok_sf = st("vector", lambda e: e.bn_aggr(
                    out=mvf[:], in_=stat6f[:]), waits=[tok_sf8], inc="cV")
                tok_lf = st("scalar", lambda e: e.activation(
                    out=lnvf[:], in_=mvf[:, 1:2], func=AF.Ln, bias=epsT[:]),
                    waits=[tok_sf], inc="cA")
                tok_rf = st("scalar", lambda e: e.activation(
                    out=rstdf[:], in_=lnvf[:], func=AF.Exp, scale=-0.5),
                    waits=[tok_lf], inc="cA")
                tok_nf = st("vector", lambda e, ft=ft: e.tensor_scalar(
                    out=lnF[:], in0=ft, scalar1=mvf[:, 0:1], scalar2=rstdf[:],
                    op0=ALU.subtract, op1=ALU.mult), waits=[tok_rf], inc="cV")
                tok_yt = st("vector", lambda e: e.tensor_tensor(
                    out=yt[:], in0=lnF[:], in1=x1r[:], op=ALU.add),
                    waits=[tok_nf, tok_x1r, ("dY", 16 * qt)], inc="cV")
                tok_y[qt] = st("sync", lambda e, qt=qt: e.dma_start(
                    out=y[qt * 128:(qt + 1) * 128, :], in_=yt[:]),
                    waits=[tok_yt], inc="dY", inc_by=16)

        S.q["sync"].append((None, [("dY", 16 * NQT)], None, 0, S.phase))
        if debug and dbg:
            S.q["sync"].append((None, [("dDBG", S.sem_n["dDBG"])], None, 0,
                                S.phase))

        block = octx.enter_context(nc.Block())
        S.build(block)
        f2ctx.close()
    return nc, dbg


_NC_CACHE = {}


def get_nc(debug=False):
    if debug not in _NC_CACHE:
        _NC_CACHE[debug] = build_nc(debug)
    return _NC_CACHE[debug]


def prep_inputs(L0_attn_hidden_size, L0_attn_Wq_c, L0_attn_bq_c, L0_attn_Wq_up,
                L0_attn_bq_up, L0_attn_Wkv_c, L0_attn_bkv_c, L0_attn_Wkv_up,
                L0_attn_bkv_up, L0_attn_Wo, L0_attn_bo, L0_ffn_gate_w,
                L0_ffn_gate_s, L0_ffn_up_w, L0_ffn_up_s, L0_ffn_down_w,
                L0_ffn_down_s):
    x = np.asarray(L0_attn_hidden_size, np.float32)

    def colsT(v, n):
        return np.ascontiguousarray(np.asarray(v, np.float32).reshape(n, 128).T)

    perm_q = np.concatenate(
        [(np.arange(32)[:, None] * 192 + np.arange(128)[None, :]).ravel(),
         (np.arange(32)[:, None] * 192 + 128 + np.arange(64)[None, :]).ravel()])
    perm_kv = np.concatenate(
        [(np.arange(32)[:, None] * 256 + np.arange(128)[None, :]).ravel(),
         (np.arange(32)[:, None] * 256 + 128 + np.arange(128)[None, :]).ravel()])

    wqc_T = np.asarray(L0_attn_Wq_c, np.float32).T
    wqc_t = np.ascontiguousarray(
        wqc_T.reshape(DM, 4, 384).transpose(1, 0, 2))
    wqup_T = np.asarray(L0_attn_Wq_up, np.float32)[perm_q].T
    wqup_t = np.ascontiguousarray(
        wqup_T.reshape(QL, 12, 512).transpose(1, 0, 2))
    wkvc_t = np.ascontiguousarray(np.asarray(L0_attn_Wkv_c, np.float32).T)
    wkvup_t = np.ascontiguousarray(
        np.asarray(L0_attn_Wkv_up, np.float32)[perm_kv].T)
    wo_t = np.ascontiguousarray(np.asarray(L0_attn_Wo, np.float32).T)

    bqc = colsT(L0_attn_bq_c, 12)
    bqup = colsT(np.asarray(L0_attn_bq_up, np.float32)[perm_q], 48)
    bkvc_pad = np.zeros(640, np.float32)
    bkvc_pad[:576] = np.asarray(L0_attn_bkv_c, np.float32)
    bkvc = colsT(bkvc_pad, 5)
    bkvup_p = np.asarray(L0_attn_bkv_up, np.float32)[perm_kv]
    bkvupk = colsT(bkvup_p[:4096], 32)
    bkvupv = np.ascontiguousarray(bkvup_p[4096:].reshape(1, 4096))
    bo = colsT(L0_attn_bo, 32)
    gs = colsT(L0_ffn_gate_s, 86)
    us = colsT(L0_ffn_up_s, 86)
    ds = colsT(L0_ffn_down_s, 32)

    gw = np.asarray(L0_ffn_gate_w, np.int32)
    uw = np.asarray(L0_ffn_up_w, np.int32)
    dw = np.asarray(L0_ffn_down_w, np.int32)

    def tile_gu(w):
        wt = np.ascontiguousarray(w.T)
        v = wt.reshape(NDC, 128, NFB, 128)
        return np.ascontiguousarray(
            v.transpose(2, 1, 0, 3).reshape(NFB, 128, NDC * 128))
    wg_t = tile_gu(gw)
    wu_t = tile_gu(uw)
    dwt = np.ascontiguousarray(dw.T)
    v = dwt.reshape(NFB, 128, NHC, 128)
    wd_t = np.ascontiguousarray(
        v.transpose(2, 1, 0, 3).reshape(NHC, 128, NFB * 128))

    identb = np.eye(128).astype(ml_dtypes.bfloat16)
    identf = np.eye(128, dtype=np.float32)

    shared = dict(wqc=wqc_t, wqup=wqup_t, wkvc=wkvc_t, wkvup=wkvup_t, wo=wo_t,
                  bqc=bqc, bqup=bqup, bkvc=bkvc, bkvupk=bkvupk, bkvupv=bkvupv,
                  bo=bo, gs=gs, us=us, ds=ds, identb=identb, identf=identf,
                  wg=wg_t, wu=wu_t, wd=wd_t)
    in_maps = []
    for c in range(8):
        b, hf = c // 2, c % 2
        xs = np.concatenate([x[b, hf * 1024:(hf + 1) * 1024],
                             x[b, (1 - hf) * 1024:(2 - hf) * 1024]], axis=0)
        m = dict(shared)
        m["x_seq"] = np.ascontiguousarray(xs)
        in_maps.append(m)
    return in_maps


def kernel(**inputs):
    in_maps = prep_inputs(**inputs)
    nc, _ = get_nc(debug=False)
    res = run_bass_kernel_spmd(nc, in_maps, core_ids=list(range(8)))
    out = np.empty((4, 2048, 4096), np.float32)
    for c in range(8):
        b, hf = c // 2, c % 2
        out[b, hf * 1024:(hf + 1) * 1024] = res.results[c]["y"]
    return out


# version 12 (final)
# speedup vs baseline: 1.1297x; 1.0254x over previous
import numpy as np
import ml_dtypes
from contextlib import ExitStack

import concourse.bass as bass
import concourse.mybir as mybir
from concourse.bass_utils import run_bass_kernel_spmd

F32 = mybir.dt.float32
BF16 = mybir.dt.bfloat16
I32 = mybir.dt.int32
AF = mybir.ActivationFunctionType
ALU = mybir.AluOpType
AX = mybir.AxisListType

DM = 4096
NDC = 32
TOK = 1024
NQT = 8
NTS = 16
SEQ = 2048
QL, NQC = 1536, 12
QF, NQF = 6144, 48
KVL = 512
NH, GH = 32, 4
DFF, NFB = 11008, 86
NHC = 32
SQ = 1.0 / np.sqrt(192.0)
TRUNC = 0
EPS = 1e-5

ENGS = ("sync", "scalar", "vector", "tensor", "gpsimd")
ESEM = {"scalar": "cA", "vector": "cV", "tensor": "cP", "gpsimd": "cG"}


class Sched:
    def __init__(self, nc, ctx):
        self.nc = nc
        self.ctx = ctx
        self.q = {e: [] for e in ENGS}
        self.sem_h = {}
        self.sem_n = {}
        self.waited = {e: {} for e in ENGS}
        self.nbar = 0
        self.phase = "init"

    def sem(self, *names):
        for name in names:
            self.sem_h[name] = self.ctx.enter_context(self.nc.semaphore(name))
            self.sem_n[name] = 0

    def step(self, eng, fn, waits=(), inc=None, inc_by=1):
        own = ESEM.get(eng)
        wl = []
        for w in waits:
            if w is None:
                continue
            sn, v = w
            if sn == own:
                assert v <= self.sem_n[sn], f"self-wait ahead on {sn}: {v} > {self.sem_n[sn]}"
            prev = self.waited[eng].get(sn, 0)
            if v > prev:
                wl.append((sn, v))
                self.waited[eng][sn] = v
        tok = None
        if inc is not None:
            self.sem_n[inc] += inc_by
            tok = (inc, self.sem_n[inc])
        self.q[eng].append((fn, wl, inc, inc_by, self.phase))
        return tok

    def barrier(self):
        self.nbar += 1
        target = 5 * self.nbar
        for e in ENGS:
            self.sem_n["bar"] += 1
            own = ESEM.get(e)
            wl = [(own, self.sem_n[own])] if own else []
            self.q[e].append((lambda eng: eng.nop(), wl, "bar", 1, self.phase))
        for e in ENGS:
            self.q[e].append((None, [("bar", target)], None, 0, self.phase))
            self.waited[e]["bar"] = target

    def build(self, block):
        for e in ENGS:
            ops = self.q[e]

            def mk(ops):
                def body(eng):
                    cur = None
                    sid = None
                    for fn, wl, inc, inc_by, ph in ops:
                        if ph != cur:
                            if cur is not None:
                                self.nc.leave_named_scope(cur, sid, False)
                            sid, _ = self.nc.enter_named_scope(ph, False)
                            cur = ph
                        for sn, v in wl:
                            eng.wait_ge(self.sem_h[sn], v)
                        if fn is None:
                            continue
                        inst = fn(eng)
                        if inc is not None:
                            inst.then_inc(self.sem_h[inc], inc_by)
                    if cur is not None:
                        self.nc.leave_named_scope(cur, sid, False)
                return body
            getattr(block, e)(mk(ops))


def build_nc(debug=False):
    nc = bass.Bass("TRN2", target_bir_lowering=False)
    x_seq = nc.dram_tensor("x_seq", [SEQ, DM], F32, kind="ExternalInput").ap()
    wqc = nc.dram_tensor("wqc", [4, NDC * 128, 384], F32,
                         kind="ExternalInput").ap()
    wqup = nc.dram_tensor("wqup", [12, NQC * 128, 512], F32,
                          kind="ExternalInput").ap()
    wkvc = nc.dram_tensor("wkvc", [NDC * 128, 576], F32,
                          kind="ExternalInput").ap()
    wkvup = nc.dram_tensor("wkvup", [KVL, 8192], F32, kind="ExternalInput").ap()
    wo = nc.dram_tensor("wo", [128, DM], F32, kind="ExternalInput").ap()
    c_in = {}
    for name, sh in [("bqc", [128, 12]), ("bqup", [128, 48]), ("bkvc", [128, 5]),
                     ("bkvupk", [128, 32]), ("bkvupv", [1, 4096]), ("bo", [128, 32]),
                     ("gs", [128, 86]), ("us", [128, 86]), ("ds", [128, 32]),
                     ("identb", [128, 128]), ("identf", [128, 128])]:
        dt = BF16 if name == "identb" else F32
        c_in[name] = nc.dram_tensor(name, sh, dt, kind="ExternalInput").ap()
    wg = nc.dram_tensor("wg", [NFB, 128, NDC * 128], I32, kind="ExternalInput").ap()
    wu = nc.dram_tensor("wu", [NFB, 128, NDC * 128], I32, kind="ExternalInput").ap()
    wd = nc.dram_tensor("wd", [NHC, 128, NFB * 128], I32, kind="ExternalInput").ap()
    y = nc.dram_tensor("y", [TOK, DM], F32, kind="ExternalOutput").ap()
    x1_dram = nc.dram_tensor("x1_dram", [TOK, DM], F32, kind="Internal").ap()
    h_dram = nc.dram_tensor("h_dram", [NFB, 128, TOK], BF16, kind="Internal").ap()
    dbg = {}

    def dbg_out(name, shape, dtype):
        if debug and name not in dbg:
            dbg[name] = nc.dram_tensor("dbg_" + name, shape, dtype,
                                       kind="ExternalOutput").ap()
        return dbg.get(name)

    with ExitStack() as octx:
        S = Sched(nc, octx)
        S.sem("bar", "cA", "cV", "cP", "cG", "dC",
              "dX0", "dX1", "dW0", "dW1", "dKW0", "dKW1",
              "dXO", "dX1W", "dHW0", "dHW1", "dHH",
              "dWG0", "dWG1", "dWU0", "dWU1", "dWD0", "dWD1",
              "dX1R", "dY", "dDBG")
        st = S.step

        def sb(name, shape, dtype, ctx):
            return ctx.enter_context(nc.sbuf_tensor(name, shape, dtype))

        def ps(name, shape, dtype, ctx):
            return ctx.enter_context(nc.psum_tensor(name, shape, dtype))

        def dump(name, ap_src, shape, after, dtype):
            o = dbg_out(name, shape, dtype)
            if o is None:
                return
            st("sync", lambda e, o=o, a=ap_src: e.dma_start(out=o, in_=a),
               waits=[after], inc="dDBG", inc_by=16)

        wctx4 = ExitStack()
        kv_ctx = ExitStack()
        _attn_only = {"bqc", "bqup", "bkvc", "bkvupk", "identf"}
        epsT = sb("epsT", [128, 1], F32, octx)
        cst = {}
        for name in c_in:
            if name not in _attn_only and name != "bkvupv":
                dt = BF16 if name == "identb" else F32
                cst[name] = sb("c_" + name, list(c_in[name].tensor.shape), dt, octx)
        o_n = sb("o_n", [128, TOK], BF16, wctx4)
        for name in c_in:
            if name in _attn_only:
                cst[name] = sb("c_" + name, list(c_in[name].tensor.shape), F32,
                               kv_ctx)
        n_c = 0
        for name in c_in:
            if name == "bkvupv":
                continue
            n_c += 1
            st("sync", lambda e, name=name: e.dma_start(out=cst[name][:],
                                                        in_=c_in[name]),
               inc="dC", inc_by=16)
        tok_consts = ("dC", 16 * n_c)
        ones1 = sb("ones1", [1, 128], F32, kv_ctx)
        ones_b = sb("ones_b", [1, 128], BF16, kv_ctx)
        tok_ones = st("gpsimd", lambda e: e.memset(ones1[:], 1.0), inc="cG")
        tok_ones = st("gpsimd", lambda e: e.memset(ones_b[:], 1.0),
                      waits=[tok_ones], inc="cG")
        tok_eps = st("gpsimd", lambda e: e.memset(epsT[:], float(EPS)),
                     waits=[tok_ones], inc="cG")
        st("gpsimd", lambda e: e.memset(krz0[64:128, :], 0.0),
           waits=[tok_eps], inc="cG")
        tok_krz_z = st("gpsimd", lambda e: e.memset(krz1[0:64, :], 0.0),
                       inc="cG")
        bqup_s = sb("bqup_s", [128, 48], F32, kv_ctx)
        tok_bqups = st("vector", lambda e: e.tensor_scalar(
            out=bqup_s[:], in0=cst["bqup"][:], scalar1=float(SQ), scalar2=None,
            op0=ALU.mult), waits=[tok_consts], inc="cV")
        kv_nope = [sb(f"kv_nope{i}", [128, SEQ], BF16, kv_ctx)
                   for i in range(4)]
        krz0 = sb("krz0", [128, SEQ], BF16, kv_ctx)
        krz1 = sb("krz1", [128, SEQ], BF16, kv_ctx)

        bctx = ExitStack()
        qc_all = sb("qc_all", [128, NQC * TOK], BF16, bctx)
        q_c = [qc_all[:, i * TOK:(i + 1) * TOK] for i in range(NQC)]
        actx = ExitStack()
        wkvc_sb = sb("wkvc_sb", [128, NDC * 576], BF16, actx)
        wqc_sb = [sb(f"wqc_sb{i}", [128, 8 * 384], BF16, actx)
                  for i in range(2)]
        xt = [sb("xt0", [128, DM], BF16, actx)] * 2
        ln1_own = [sb(f"ln1o{i}", [128, TOK], BF16, actx) for i in range(NDC)]
        ln1_par = [sb(f"ln1p{i}", [128, 256], BF16, actx) for i in range(NDC)]
        ln_tok = [sb("lntok0", [128, DM], BF16, actx)] * 2
        stat6 = [sb(f"stat6{i}", [128, 48], F32, actx) for i in range(2)]
        mv = [sb(f"mv{i}", [128, 2], F32, actx) for i in range(2)]
        rstd = [sb(f"rstd{i}", [128, 1], F32, actx) for i in range(2)]
        lnv = [sb(f"lnv{i}", [128, 1], F32, actx) for i in range(2)]
        tpsA = [ps(f"tpsA{i}", [128, 512], BF16, actx) for i in range(2)]
        pj = [ps(f"pj{i}", [128, 1024], F32, actx) for i in range(3)]

        S.phase = "P0"
        for kq in range(4):
            def f_wkvc(e, kq=kq):
                s_ = wkvc[kq * 1024:(kq + 1) * 1024, :]
                s_ = s_.rearrange("(d p) c -> p d c", p=128)
                d_ = wkvc_sb[:, kq * 8 * 576:(kq + 1) * 8 * 576]
                d_ = d_.rearrange("p (d c) -> p d c", d=8)
                return e.dma_start(out=d_, in_=s_)
            tok_wkvc = st("gpsimd", f_wkvc, inc="dW0", inc_by=16)
        lastx = {}
        tok_trp = {}
        tok_lncp = {}
        tok_kvpar = {}
        tok_kv_ev = {}

        for tau in range(NTS):
            sl = tau % 2
            w = [lastx.get(0), lastx.get(1)]
            tok_x = st("gpsimd", lambda e, sl=sl, tau=tau: e.dma_start(
                out=xt[sl][:], in_=x_seq[tau * 128:(tau + 1) * 128, :]),
                waits=w, inc=f"dX{sl}", inc_by=16)

            def f_stats(e, sl=sl):
                last = None
                for i in range(8):
                    last = e.bn_stats(out=stat6[sl][:, 6 * i:6 * i + 6],
                                      in_=xt[sl][:, 512 * i:512 * (i + 1)])
                return last
            tok_st8 = st("vector", f_stats, waits=[tok_x], inc="cV")
            tok_stats = st("vector", lambda e, sl=sl: e.bn_aggr(
                out=mv[sl][:], in_=stat6[sl][:]), waits=[tok_st8], inc="cV")
            tok_ln = st("scalar", lambda e, sl=sl: e.activation(
                out=lnv[sl][:], in_=mv[sl][:, 1:2], func=AF.Ln, bias=epsT[:]),
                waits=[tok_stats, tok_eps], inc="cA")
            tok_rstd = st("scalar", lambda e, sl=sl: e.activation(
                out=rstd[sl][:], in_=lnv[sl][:], func=AF.Exp, scale=-0.5),
                waits=[tok_ln], inc="cA")
            w_lt = [tok_trp.get((tau - 1, 7))]
            tok_nrm = st("vector", lambda e, sl=sl: e.tensor_scalar(
                out=ln_tok[sl][:], in0=xt[sl][:], scalar1=mv[sl][:, 0:1],
                scalar2=rstd[sl][:], op0=ALU.subtract, op1=ALU.mult),
                waits=[tok_rstd] + w_lt, inc="cV")
            lastx[sl] = tok_nrm

            for grp in range(8):
                pp = (tau * 8 + grp) % 2
                prev_cp = tok_lncp.get(tau * 8 + grp - 2)

                def f_tr(e, sl=sl, grp=grp, pp=pp):
                    last = None
                    for i in range(4):
                        dc = grp * 4 + i
                        last = e.transpose(out=tpsA[pp][:, 128 * i:128 * (i + 1)],
                                           in_=ln_tok[sl][:, 128 * dc:128 * (dc + 1)],
                                           identity=cst["identb"][:])
                    return last
                tok_tr = st("tensor", f_tr, waits=[tok_nrm, prev_cp, tok_consts],
                            inc="cP")
                tok_trp[(tau, grp)] = tok_tr
                if tau < NQT:
                    def f_cp(e, tau=tau, grp=grp, pp=pp):
                        last = None
                        for i in range(4):
                            dc = grp * 4 + i
                            last = e.tensor_copy(
                                out=ln1_own[dc][:, 128 * tau:128 * (tau + 1)],
                                in_=tpsA[pp][:, 128 * i:128 * (i + 1)])
                        return last
                else:
                    def f_cp(e, tau=tau, grp=grp, pp=pp):
                        col = ((tau - NQT) % 2) * 128
                        last = None
                        for i in range(4):
                            dc = grp * 4 + i
                            last = e.tensor_copy(
                                out=ln1_par[dc][:, col:col + 128],
                                in_=tpsA[pp][:, 128 * i:128 * (i + 1)])
                        return last
                wkv = [tok_kvpar.get((tau - NQT) // 2 - 1)] if tau >= 10 else []
                tok_lncp[tau * 8 + grp] = st("vector", f_cp,
                                             waits=[tok_tr] + wkv, inc="cV")

            if tau in (9, 11, 13, 15):
                g = (tau - 9) // 2
                w_ln = [tok_lncp[tau * 8 + 7]]
                tok_mm_d = {}
                for dc in range(NDC):
                    def f_mm(e, dc=dc):
                        w0 = wkvc_sb[:, dc * 576:(dc + 1) * 576]
                        for r in range(4):
                            reg = pj[r // 2][:, (r % 2) * 512:(r % 2) * 512 + 256]
                            e.matmul(out=reg,
                                     lhsT=w0[:, r * 128:(r + 1) * 128],
                                     rhs=ln1_par[dc][:],
                                     start=(dc == 0), stop=(dc == NDC - 1))
                        return e.matmul(out=pj[2][0:64, 0:256],
                                        lhsT=w0[:, 512:576],
                                        rhs=ln1_par[dc][:],
                                        start=(dc == 0), stop=(dc == NDC - 1))
                    tok_mm_d[dc] = st("tensor", f_mm,
                                      waits=[tok_wkvc] + w_ln +
                                            ([tok_kv_ev.get(g - 1)] if dc == 0 else []),
                                      inc="cP")
                tok_kvpar[g] = tok_mm_d[NDC - 1]
                col0 = TOK + 256 * g

                def f_ev(e, col0=col0):
                    for r in range(4):
                        reg = pj[r // 2][:, (r % 2) * 512:(r % 2) * 512 + 256]
                        e.activation(out=kv_nope[r][:, col0:col0 + 256],
                                     in_=reg, func=AF.Identity,
                                     bias=cst["bkvc"][:, r:r + 1])
                    e.activation(out=krz0[0:64, col0:col0 + 256],
                                 in_=pj[2][0:64, 0:256], func=AF.Identity,
                                 bias=cst["bkvc"][0:64, 4:5])
                    return e.activation(out=krz1[64:128, col0:col0 + 256],
                                        in_=pj[2][0:64, 0:256], func=AF.Identity,
                                        bias=cst["bkvc"][0:64, 4:5])
                tok_kv_ev[g] = st("scalar", f_ev,
                                  waits=[tok_kvpar[g], tok_consts], inc="cA")

        if TRUNC == 3:
            S.q["sync"].append((None, [("cA", S.sem_n["cA"]),
                                       ("cV", S.sem_n["cV"]),
                                       ("cP", S.sem_n["cP"])], None, 0, S.phase))
            block = octx.enter_context(nc.Block())
            S.build(block)
            actx.close(); qctx_dummy = None
            bctx.close(); kv_ctx.close(); wctx4.close()
            return nc, dbg
        S.phase = "P1a"
        tok_qc_ev = {}
        w_ln_own = [tok_lncp[(NQT - 1) * 8 + 7]]
        last_mm = None
        tok_wqcg = {}
        tok_qcmm_blk = {}
        for blk in range(16):
            fcg, qr = blk // 4, blk % 4
            wsl = blk % 2
            wg_pre = [tok_qcmm_blk.get(blk - 2)]
            def f_wqc(e, fcg=fcg, qr=qr, wsl=wsl):
                s_ = wqc[fcg][qr * 1024:(qr + 1) * 1024, :]
                s_ = s_.rearrange("(d p) c -> p d c", p=128)
                d_ = wqc_sb[wsl][:].rearrange("p (d c) -> p d c", d=8)
                return e.dma_start(out=d_, in_=s_)
            tok_wqcg[blk] = st("gpsimd", f_wqc, waits=wg_pre,
                               inc=f"dW{wsl}", inc_by=16)
            for dc8 in range(8):
                dc = qr * 8 + dc8
                wev = ([tok_qc_ev.get(fcg - 1), tok_kv_ev.get(3)]
                       if dc == 0 else [])
                def f_mm(e, dc=dc, dc8=dc8, wsl=wsl):
                    last = None
                    w0 = wqc_sb[wsl][:, dc8 * 384:(dc8 + 1) * 384]
                    for fc3 in range(3):
                        for col in range(2):
                            last = e.matmul(
                                out=pj[fc3][:, col * 512:(col + 1) * 512],
                                lhsT=w0[:, fc3 * 128:(fc3 + 1) * 128],
                                rhs=ln1_own[dc][:, col * 512:(col + 1) * 512],
                                start=(dc == 0), stop=(dc == NDC - 1))
                    return last
                last_mm = st("tensor", f_mm,
                             waits=[tok_wqcg[blk]] + w_ln_own + wev, inc="cP")
            tok_qcmm_blk[blk] = last_mm
            if qr == 3:
                def f_ev(e, fcg=fcg):
                    last = None
                    for fc3 in range(3):
                        fc = fcg * 3 + fc3
                        last = e.activation(out=q_c[fc], in_=pj[fc3][:],
                                            func=AF.Identity,
                                            bias=cst["bqc"][:, fc:fc + 1])
                    return last
                tok_qc_ev[fcg] = st("scalar", f_ev, waits=[last_mm], inc="cA")

        if TRUNC == 4:
            S.q["sync"].append((None, [("cA", S.sem_n["cA"]),
                                       ("cV", S.sem_n["cV"]),
                                       ("cP", S.sem_n["cP"])], None, 0, S.phase))
            block = octx.enter_context(nc.Block())
            S.build(block)
            actx.close()
            bctx.close(); kv_ctx.close(); wctx4.close()
            return nc, dbg
        tok_kvo_ev = {}
        for p in range(2):
            for dc in range(NDC):
                wev = []
                if dc == 0:
                    wev = [tok_qc_ev[3]] if p == 0 else [tok_kvo_ev[0]]
                def f_mm(e, dc=dc, p=p):
                    w0 = wkvc_sb[:, dc * 576:(dc + 1) * 576]
                    last = None
                    if p == 0:
                        for c in range(3):
                            for col in range(2):
                                last = e.matmul(
                                    out=pj[c][:, col * 512:(col + 1) * 512],
                                    lhsT=w0[:, c * 128:(c + 1) * 128],
                                    rhs=ln1_own[dc][:, col * 512:(col + 1) * 512],
                                    start=(dc == 0), stop=(dc == NDC - 1))
                    else:
                        for col in range(2):
                            e.matmul(out=pj[0][:, col * 512:(col + 1) * 512],
                                     lhsT=w0[:, 384:512],
                                     rhs=ln1_own[dc][:, col * 512:(col + 1) * 512],
                                     start=(dc == 0), stop=(dc == NDC - 1))
                            last = e.matmul(
                                out=pj[1][0:64, col * 512:(col + 1) * 512],
                                lhsT=w0[:, 512:576],
                                rhs=ln1_own[dc][:, col * 512:(col + 1) * 512],
                                start=(dc == 0), stop=(dc == NDC - 1))
                    return last
                last_mm = st("tensor", f_mm, waits=[tok_wkvc] + wev, inc="cP")
            if p == 0:
                def f_ev(e):
                    last = None
                    for c in range(3):
                        last = e.activation(out=kv_nope[c][:, 0:TOK],
                                            in_=pj[c][:], func=AF.Identity,
                                            bias=cst["bkvc"][:, c:c + 1])
                    return last
            else:
                def f_ev(e):
                    e.activation(out=kv_nope[3][:, 0:TOK], in_=pj[0][:],
                                 func=AF.Identity, bias=cst["bkvc"][:, 3:4])
                    e.activation(out=krz0[0:64, 0:TOK],
                                 in_=pj[1][0:64, 0:1024], func=AF.Identity,
                                 bias=cst["bkvc"][0:64, 4:5])
                    return e.activation(out=krz1[64:128, 0:TOK],
                                        in_=pj[1][0:64, 0:1024], func=AF.Identity,
                                        bias=cst["bkvc"][0:64, 4:5])
            tok_kvo_ev[p] = st("scalar", f_ev, waits=[last_mm], inc="cA")
        if TRUNC == 5:
            S.q["sync"].append((None, [("cA", S.sem_n["cA"]),
                                       ("cV", S.sem_n["cV"]),
                                       ("cP", S.sem_n["cP"])], None, 0, S.phase))
            block = octx.enter_context(nc.Block())
            S.build(block)
            actx.close()
            bctx.close(); kv_ctx.close(); wctx4.close()
            return nc, dbg
        tok_kv_done = tok_kvo_ev[1]

        if debug:
            for i in range(4):
                dump(f"kv_nope{i}", kv_nope[i][:], [128, SEQ], tok_kv_done, BF16)
            dump("krope", krz0[:], [128, SEQ], tok_kv_done, BF16)
            dump("qc0", q_c[0], [128, TOK], tok_qc_ev[0], BF16)
            dump("ln1o0", ln1_own[0][:], [128, TOK], w_ln_own[0], BF16)

        S.barrier()
        actx.close()

        S.phase = "P1b"
        qctx = ExitStack()
        q_fm = [sb(f"qfm{i}", [128, TOK], BF16, qctx) for i in range(NQF)]
        p1b_ctx = ExitStack()
        wq2 = [sb(f"wq2_{i}", [128, NQC * 512], BF16, p1b_ctx)
               for i in range(2)]
        p1b_p = ExitStack()
        pu = [ps(f"pu{i}", [128, 1024], F32, p1b_p) for i in range(4)]
        tok_qfm_ev = {}
        last_mm = None
        tok_wq2 = {}
        for fcg in range(NQF // 4):
            wsl = fcg % 2
            wg_pre = [tok_qfm_ev.get(fcg - 2)]
            def f_w(e, fcg=fcg, wsl=wsl):
                s_ = wqup[fcg].rearrange("(q p) c -> p q c", p=128)
                d_ = wq2[wsl][:].rearrange("p (q c) -> p q c", q=NQC)
                return e.dma_start(out=d_, in_=s_)
            tok_wq2[fcg] = st("gpsimd", f_w, waits=wg_pre,
                              inc=f"dW{wsl}", inc_by=16)
            for qc in range(NQC):
                wev = [tok_qfm_ev.get(fcg - 1)] if qc == 0 else []
                def f_mm(e, qc=qc, wsl=wsl):
                    last = None
                    w0 = wq2[wsl][:, qc * 512:(qc + 1) * 512]
                    for fc4 in range(4):
                        for col in range(2):
                            last = e.matmul(
                                out=pu[fc4][:, col * 512:(col + 1) * 512],
                                lhsT=w0[:, fc4 * 128:(fc4 + 1) * 128],
                                rhs=qc_all[:, qc * TOK + col * 512:
                                           qc * TOK + (col + 1) * 512],
                                start=(qc == 0), stop=(qc == NQC - 1))
                    return last
                last_mm = st("tensor", f_mm, waits=[tok_wq2[fcg]] + wev,
                             inc="cP")

            def f_ev(e, fcg=fcg):
                last = None
                for fc4 in range(4):
                    fc = fcg * 4 + fc4
                    last = e.activation(out=q_fm[fc][:], in_=pu[fc4][:],
                                        func=AF.Identity, scale=float(SQ),
                                        bias=bqup_s[:, fc:fc + 1])
                return last
            tok_qfm_ev[fcg] = st("scalar", f_ev, waits=[last_mm, tok_bqups],
                                 inc="cA")
        if debug:
            dump("qfm0", q_fm[0][:], [128, TOK], tok_qfm_ev[0], BF16)
            dump("qfm47", q_fm[47][:], [128, TOK], tok_qfm_ev[11], BF16)

        S.barrier()
        p1b_p.close()
        p1b_ctx.close()

        if TRUNC == 1:
            block = octx.enter_context(nc.Block())
            S.build(block)
            qctx.close(); bctx.close(); kv_ctx.close(); wctx4.close()
            return nc, dbg
        attn_ctx = ExitStack()
        kf = [sb(f"kf{i}", [128, SEQ], BF16, attn_ctx) for i in range(2)]
        vtg = [sb(f"vtg{i}", [128, 16 * 512], BF16, attn_ctx) for i in range(2)]
        wkh = [sb(f"wkh{i}", [128, 512], BF16, attn_ctx) for i in range(2)]
        wkvv = [sb(f"wkvv{i}", [128, 2048], BF16, attn_ctx) for i in range(2)]
        bvg = [sb(f"bvg{i}", [1, 512], BF16, attn_ctx) for i in range(2)]
        ring = qc_all[:, 4096:4096 + 3 * TOK]
        sums = sb("sums", [128, 8], F32, attn_ctx)
        sumsA = sb("sumsA", [128, 8], F32, attn_ctx)
        negm = sb("negm", [128, 8], F32, attn_ctx)
        lnS = sb("lnS", [128, 8], F32, attn_ctx)
        augc = sb("augc", [128, 8], F32, attn_ctx)
        hilo_b = sb("hilo_b", [128, 16], BF16, attn_ctx)
        augMIX = qc_all[0:2, 3072:3072 + TOK]
        ones2 = sb("ones2", [2, 128], BF16, attn_ctx)
        negmA = sb("negmA", [128, 8], F32, attn_ctx)
        s1 = ps("s1", [128, SEQ], F32, attn_ctx)
        s2 = ps("s2", [128, TOK], F32, attn_ctx)
        osum = ps("osum", [128, TOK], F32, attn_ctx)

        S.phase = "ATTN"
        tok_kw = {}
        tok_kmm = {}
        tok_kfev = {}
        tok_vev = {}
        tok_exp1 = {}
        tok_exp2 = {}
        tok_avmm = {}
        tok_maxs = {}
        tok_stt = {}
        tok_augT = {}
        tok_augrow = {}
        disc = qc_all[:, 0:SEQ]
        tok_o2 = st("gpsimd", lambda e: e.memset(ones2[:], 1.0),
                    waits=[tok_krz_z], inc="cG")

        def emit_augpre(hp):
            def f_augT(e, hp=hp):
                last = None
                for qt in range(8):
                    last = e.matmul(out=s1[0:2, qt * 128:(qt + 1) * 128],
                                    lhsT=hilo_b[:, 2 * qt:2 * qt + 2],
                                    rhs=cst["identb"][:],
                                    start=True, stop=True)
                return last
            tok_augT[hp] = st("tensor", f_augT,
                              waits=[tok_stt[(hp, qt)] for qt in range(8)] +
                                    [tok_exp1[(hp, 7)]], inc="cP")
            tok_augrow[hp] = st("vector", lambda e: e.tensor_copy(
                out=augMIX[:], in_=s1[0:2, 0:TOK]), waits=[tok_augT[hp]],
                inc="cV")

        for i in range(NH + 1):
            ia = i
            ib = i - 1
            if ib >= 0:
                emit_augpre(ib)
            if ia < NH:
                g, hg = ia // GH, ia % GH
                hsl = ia % 2
                gsl = g % 2
                if hg == 0:
                    wpre = [tok_kmm.get(4 * (g - 2) + 3)] if g >= 2 else []
                    st("gpsimd", lambda e, g=g, gsl=gsl: e.dma_start(
                        out=bvg[gsl][:],
                        in_=c_in["bkvupv"][0:1, g * 512:(g + 1) * 512]),
                       waits=wpre, inc=f"dKW{gsl}", inc_by=16)
                    for kvc in range(4):
                        tok_kw[g] = st("gpsimd",
                                       lambda e, g=g, gsl=gsl, kvc=kvc: e.dma_start(
                                           out=wkvv[gsl][:, kvc * 512:(kvc + 1) * 512],
                                           in_=wkvup[kvc * 128:(kvc + 1) * 128,
                                                     4096 + g * 512:4096 + (g + 1) * 512]),
                                       waits=wpre, inc=f"dKW{gsl}", inc_by=16)
                    for rnd in range(4):
                        w_s1 = [tok_augrow.get(ib) if rnd == 0 else
                                tok_vev.get((g, rnd - 1))]
                        def f_vmm(e, g=g, gsl=gsl, rnd=rnd):
                            last = None
                            for t4 in range(4):
                                tc = rnd * 4 + t4
                                reg = s1[:, t4 * 512:(t4 + 1) * 512]
                                for kvc in range(4):
                                    e.matmul(out=reg,
                                             lhsT=kv_nope[kvc][:, tc * 128:(tc + 1) * 128],
                                             rhs=wkvv[gsl][:, kvc * 512:(kvc + 1) * 512],
                                             start=(kvc == 0), stop=False)
                                last = e.matmul(
                                    out=reg, lhsT=ones_b[:],
                                    rhs=bvg[gsl][:],
                                    start=False, stop=True)
                            return last
                        tok_vmm = st("tensor", f_vmm,
                                     waits=w_s1 + [tok_kw[g], tok_ones], inc="cP")
                        w_vtg = [tok_avmm.get((4 * (g - 2) + 3, 15))] if g >= 2 else []
                        def f_vev(e, gsl=gsl, rnd=rnd):
                            return e.activation(
                                out=vtg[gsl][:, rnd * 2048:(rnd + 1) * 2048],
                                in_=s1[:], func=AF.Copy)
                        tok_vev[(g, rnd)] = st("scalar", f_vev,
                                               waits=[tok_vmm] + w_vtg, inc="cA")
                    s1_free = tok_vev[(g, 3)]
                else:
                    s1_free = tok_augrow.get(ib)
                wkpre = [tok_kmm.get(ia - 2)]
                tok_wk = None
                for kvc in range(4):
                    tok_wk = st("gpsimd", lambda e, ia=ia, hsl=hsl, kvc=kvc: e.dma_start(
                        out=wkh[hsl][:, kvc * 128:(kvc + 1) * 128],
                        in_=wkvup[kvc * 128:(kvc + 1) * 128,
                                  ia * 128:(ia + 1) * 128]),
                        waits=wkpre, inc=f"dW{hsl}", inc_by=16)
                tok_kfh = {}
                for kh2 in range(2):
                    def f_kmm(e, hsl=hsl, kh2=kh2):
                        last = None
                        for col in range(2 * kh2, 2 * kh2 + 2):
                            for kvc in range(4):
                                last = e.matmul(
                                    out=s1[:, col * 512:(col + 1) * 512],
                                    lhsT=wkh[hsl][:, kvc * 128:(kvc + 1) * 128],
                                    rhs=kv_nope[kvc][:, col * 512:(col + 1) * 512],
                                    start=(kvc == 0), stop=(kvc == 3))
                        return last
                    tok_kmm[ia] = st("tensor", f_kmm,
                                     waits=[s1_free if kh2 == 0 else None,
                                            tok_wk], inc="cP")
                    def f_kev(e, hsl=hsl, ia=ia, kh2=kh2):
                        return e.activation(
                            out=kf[hsl][:, kh2 * TOK:(kh2 + 1) * TOK],
                            in_=s1[:, kh2 * TOK:(kh2 + 1) * TOK],
                            func=AF.Identity, bias=cst["bkvupk"][:, ia:ia + 1])
                    tok_kfh[kh2] = st("scalar", f_kev, waits=[tok_kmm[ia]],
                                      inc="cA")
                tok_kfev[ia] = tok_kfh[1]
                tok_kfev0 = tok_kfh[0]
                nope_t = q_fm[ia]
                rope_t = q_fm[32 + ia // 2]
                rb = (ia % 2) * 64
                tok_e1a_prev = None

            for kc in range(16):
                if ib >= 0:
                    gb, hgb = ib // GH, ib % GH
                    hslb = ib % 2
                    rbb = (ib % 2) * 64
                    rope_b = q_fm[32 + ib // 2]
                    tok_s2h = {}
                    for ch in range(2):
                        w_s2 = [tok_exp2.get((ib, kc - 1, ch)) if kc > 0 else
                                tok_exp2.get((ib - 1, 15, ch))]
                        krzb = krz0 if ib % 2 == 0 else krz1
                        def f_s2(e, ib=ib, kc=kc, hslb=hslb,
                                 rope_b=rope_b, col=ch, krzb=krzb):
                            reg = s2[:, col * 512:(col + 1) * 512]
                            e.matmul(out=reg,
                                     lhsT=kf[hslb][:, kc * 128:(kc + 1) * 128],
                                     rhs=q_fm[ib][:, col * 512:(col + 1) * 512],
                                     start=True, stop=False)
                            e.matmul(out=reg,
                                     lhsT=krzb[:, kc * 128:(kc + 1) * 128],
                                     rhs=rope_b[:, col * 512:(col + 1) * 512],
                                     start=False, stop=False)
                            return e.matmul(
                                out=reg, lhsT=ones2[:],
                                rhs=augMIX[:, col * 512:(col + 1) * 512],
                                start=False, stop=True)
                        tok_s2h[ch] = st("tensor", f_s2,
                                         waits=w_s2 + [tok_augrow[ib]], inc="cP")
                    if kc >= 1:
                        def f_av(e, ib=ib, kc=kc, hgb=hgb, gb=gb):
                            last = None
                            rsl = (kc - 1) % 3
                            for col in range(2):
                                last = e.matmul(
                                    out=osum[:, col * 512:(col + 1) * 512],
                                    lhsT=vtg[gb % 2][:, (kc - 1) * 512 + hgb * 128:
                                                     (kc - 1) * 512 + (hgb + 1) * 128],
                                    rhs=ring[:, rsl * TOK + col * 512:
                                             rsl * TOK + (col + 1) * 512],
                                    start=(ib == 0 and kc == 1),
                                    stop=False)
                            return last
                        tok_avmm[(ib, kc - 1)] = st(
                            "tensor", f_av,
                            waits=[tok_exp2[(ib, kc - 1, 1)], tok_vev[(gb, 3)]],
                            inc="cP")
                    rsl = kc % 3
                    prev_kc = kc - 3 if kc >= 3 else 15 - ((15 - kc) % 3)
                    w_ring = [tok_avmm.get((ib, prev_kc)) if kc >= 3 else
                              (tok_avmm.get((ib - 1, prev_kc)) if ib > 0 else None)]
                    for ch in range(2):
                        tok_exp2[(ib, kc, ch)] = st(
                            "scalar", lambda e, rsl=rsl, ch=ch: e.activation(
                                out=ring[:, rsl * TOK + ch * 512:
                                         rsl * TOK + (ch + 1) * 512],
                                in_=s2[:, ch * 512:(ch + 1) * 512],
                                func=AF.Exp),
                            waits=[tok_s2h[ch]] + w_ring, inc="cA")
                if ia < NH and kc % 2 == 0:
                    qt = kc // 2
                    w_s1 = [tok_kfev0 if qt == 0 else tok_e1a_prev]
                    krz = krz0 if ia % 2 == 0 else krz1
                    def f_s1a(e, qt=qt, hsl=hsl, nope_t=nope_t,
                              rope_t=rope_t, krz=krz):
                        last = None
                        for col in range(2):
                            e.matmul(out=s1[:, col * 512:(col + 1) * 512],
                                     lhsT=nope_t[:, qt * 128:(qt + 1) * 128],
                                     rhs=kf[hsl][:, col * 512:(col + 1) * 512],
                                     start=True, stop=False)
                            last = e.matmul(
                                out=s1[:, col * 512:(col + 1) * 512],
                                lhsT=rope_t[:, qt * 128:(qt + 1) * 128],
                                rhs=krz[:, col * 512:(col + 1) * 512],
                                start=False, stop=True)
                        return last
                    tok_s1a = st("tensor", f_s1a, waits=w_s1, inc="cP")
                    def f_s1b(e, qt=qt, hsl=hsl, nope_t=nope_t,
                              rope_t=rope_t, krz=krz):
                        last = None
                        for col in range(2, 4):
                            e.matmul(out=s1[:, col * 512:(col + 1) * 512],
                                     lhsT=nope_t[:, qt * 128:(qt + 1) * 128],
                                     rhs=kf[hsl][:, col * 512:(col + 1) * 512],
                                     start=True, stop=False)
                            last = e.matmul(
                                out=s1[:, col * 512:(col + 1) * 512],
                                lhsT=rope_t[:, qt * 128:(qt + 1) * 128],
                                rhs=krz[:, col * 512:(col + 1) * 512],
                                start=False, stop=True)
                        return last
                    tok_s1b = st("tensor", f_s1b,
                                 waits=[tok_kfev[ia] if qt == 0 else
                                        tok_exp1[(ia, qt - 1)]], inc="cP")
                    w_negm = [tok_exp1.get((ia - 1, qt))] if ia > 0 else []
                    t_mA = st("vector", lambda e, qt=qt: e.reduce_max(
                        out=negmA[:, qt:qt + 1], in_=s1[:, 0:TOK], axis=AX.X,
                        negate=True), waits=[tok_s1a] + w_negm, inc="cV")
                    t_mB = st("vector", lambda e, qt=qt: e.reduce_max(
                        out=negm[:, qt:qt + 1], in_=s1[:, TOK:SEQ], axis=AX.X,
                        negate=True), waits=[tok_s1b] + w_negm, inc="cV")
                    tok_maxs[(ia, qt)] = st(
                        "vector", lambda e, qt=qt: e.tensor_tensor(
                            out=negm[:, qt:qt + 1], in0=negm[:, qt:qt + 1],
                            in1=negmA[:, qt:qt + 1], op=ALU.min),
                        waits=[t_mB], inc="cV")
                    tok_e1a = st(
                        "scalar", lambda e, qt=qt: e.activation(
                            out=disc[:, 0:TOK], in_=s1[:, 0:TOK], func=AF.Exp,
                            bias=negm[:, qt:qt + 1],
                            accum_out=sumsA[:, qt:qt + 1]),
                        waits=[tok_maxs[(ia, qt)]], inc="cA")
                    tok_exp1[(ia, qt)] = st(
                        "scalar", lambda e, qt=qt: e.activation(
                            out=disc[:, TOK:SEQ], in_=s1[:, TOK:SEQ],
                            func=AF.Exp, bias=negm[:, qt:qt + 1],
                            accum_out=sums[:, qt:qt + 1]),
                        waits=[tok_e1a], inc="cA")
                    tok_e1a_prev = tok_e1a
                    tok_sad = st("vector", lambda e, qt=qt: e.tensor_tensor(
                        out=sums[:, qt:qt + 1], in0=sums[:, qt:qt + 1],
                        in1=sumsA[:, qt:qt + 1], op=ALU.add),
                        waits=[tok_exp1[(ia, qt)]], inc="cV")
                    tok_lns = st("scalar", lambda e, qt=qt: e.activation(
                        out=lnS[:, qt:qt + 1], in_=sums[:, qt:qt + 1],
                        func=AF.Ln), waits=[tok_sad], inc="cA")
                    t_st = st("vector", lambda e, qt=qt: e.scalar_tensor_tensor(
                        out=augc[:, qt:qt + 1], in0=lnS[:, qt:qt + 1],
                        scalar=-1.0, in1=negm[:, qt:qt + 1],
                        op0=ALU.mult, op1=ALU.add),
                        waits=[tok_lns, tok_maxs[(ia, qt)],
                               tok_augT.get(ia - 1)], inc="cV")
                    t_hb = st("vector", lambda e, qt=qt: e.tensor_copy(
                        out=hilo_b[:, 2 * qt:2 * qt + 1],
                        in_=augc[:, qt:qt + 1]), waits=[t_st], inc="cV")
                    tok_stt[(ia, qt)] = st(
                        "vector", lambda e, qt=qt: e.tensor_tensor(
                            out=hilo_b[:, 2 * qt + 1:2 * qt + 2],
                            in0=augc[:, qt:qt + 1],
                            in1=hilo_b[:, 2 * qt:2 * qt + 1],
                            op=ALU.subtract), waits=[t_hb], inc="cV")
            if ib >= 0:
                gb, hgb = ib // GH, ib % GH
                def f_av15(e, ib=ib, hgb=hgb, gb=gb):
                    last = None
                    rsl = 15 % 3
                    for col in range(2):
                        last = e.matmul(
                            out=osum[:, col * 512:(col + 1) * 512],
                            lhsT=vtg[gb % 2][:, 15 * 512 + hgb * 128:
                                             15 * 512 + (hgb + 1) * 128],
                            rhs=ring[:, rsl * TOK + col * 512:
                                     rsl * TOK + (col + 1) * 512],
                            start=False,
                            stop=(ib == NH - 1))
                    return last
                tok_avmm[(ib, 15)] = st(
                    "tensor", f_av15,
                    waits=[tok_exp2[(ib, 15, 1)], tok_vev[(gb, 3)]], inc="cP")
        last_av = tok_avmm[(NH - 1, 15)]

        tok_on = st("vector", lambda e: e.tensor_copy(out=o_n[:], in_=osum[:]),
                    waits=[last_av], inc="cV")
        if debug:
            dump("o_n", o_n[:], [128, TOK], tok_on, BF16)
            dump("augrow", augMIX[0:1, :], [1, TOK], tok_augrow[NH - 1], BF16)

        S.barrier()
        attn_ctx.close()
        qctx.close()
        bctx.close()
        kv_ctx.close()
        if TRUNC == 2:
            block = octx.enter_context(nc.Block())
            S.build(block)
            wctx4.close()
            return nc, dbg

        S.phase = "W"
        ln2_fm = sb("ln2_fm", [128, NDC * TOK], BF16, wctx4)
        wctx = ExitStack()
        wo_sb = sb("wo_sb", [128, DM], BF16, wctx)
        xa_fm = [sb(f"xafm{i}", [128, TOK], BF16, wctx) for i in range(2)]
        xa_tok = sb("xa_tok", [128, NQT * DM], BF16, wctx)
        xo = sb("xo", [128, DM], F32, wctx)
        x1t = sb("x1t", [128, DM], F32, wctx)
        lnA = sb("lnA", [128, DM], BF16, wctx)
        ln2_tok = sb("ln2_tok", [128, DM], BF16, wctx)
        stat6w = sb("stat6w", [128, 48], F32, wctx)
        mvw = sb("mvw", [128, 2], F32, wctx)
        rstdw = sb("rstdw", [128, 1], F32, wctx)
        lnvw = sb("lnvw", [128, 1], F32, wctx)
        wops = [ps(f"wops{i}", [128, TOK], F32, wctx) for i in range(2)]
        tpsW = [ps(f"tpsW{i}", [128, TOK], BF16, wctx) for i in range(2)]

        tok_wo = st("gpsimd", lambda e: e.dma_start(out=wo_sb[:], in_=wo),
                    inc="dKW0", inc_by=16)
        tok_wo_ev = {}
        tok_tcp = {}
        for Hc in range(NHC):
            wsl = Hc % 2
            w_ps = [tok_wo_ev.get(Hc - 2)]

            def f_wmm(e, Hc=Hc, wsl=wsl):
                last = None
                for col in range(2):
                    last = e.matmul(
                        out=wops[wsl][:, col * 512:(col + 1) * 512],
                        lhsT=wo_sb[:, Hc * 128:(Hc + 1) * 128],
                        rhs=o_n[:, col * 512:(col + 1) * 512],
                        start=True, stop=True)
                return last
            tok_wmm = st("tensor", f_wmm, waits=[tok_wo, tok_on] + w_ps,
                         inc="cP")
            w_fm = [tok_tcp.get(Hc - 2)]
            tok_ev = st("scalar", lambda e, Hc=Hc, wsl=wsl: e.activation(
                out=xa_fm[wsl][:], in_=wops[wsl][:], func=AF.Identity,
                bias=cst["bo"][:, Hc:Hc + 1]), waits=[tok_wmm] + w_fm, inc="cA")
            tok_wo_ev[Hc] = tok_ev

            def f_tw(e, wsl=wsl):
                last = None
                for qt in range(8):
                    last = e.transpose(out=tpsW[wsl][:, qt * 128:(qt + 1) * 128],
                                       in_=xa_fm[wsl][:, qt * 128:(qt + 1) * 128],
                                       identity=cst["identb"][:])
                return last
            tok_tw = st("tensor", f_tw, waits=[tok_ev] + [tok_tcp.get(Hc - 2)],
                        inc="cP")

            def f_tc(e, Hc=Hc, wsl=wsl):
                src = tpsW[wsl][:].rearrange("p (q c) -> p q c", q=8)
                dst = xa_tok[:].rearrange("p (q d) -> p q d", q=8)[
                    :, :, Hc * 128:(Hc + 1) * 128]
                return e.tensor_copy(out=dst, in_=src)
            tok_tcp[Hc] = st("vector", f_tc, waits=[tok_tw], inc="cV")

        tok_xa_done = tok_tcp[NHC - 1]
        if debug:
            dump("xa_tok", xa_tok[:, 0:DM], [128, DM], tok_xa_done, BF16)

        tok_ln2cp = {}
        tok_x1_lastread = {}
        for qt in range(NQT):
            xa = xa_tok[:, qt * DM:(qt + 1) * DM]
            tok_xod = st("sync", lambda e, qt=qt: e.dma_start(
                out=xo[:], in_=x_seq[qt * 128:(qt + 1) * 128, :]),
                waits=[tok_x1_lastread.get(qt - 1)], inc="dXO", inc_by=16)

            def f_st(e, xa=xa):
                last = None
                for i in range(8):
                    last = e.bn_stats(out=stat6w[:, 6 * i:6 * i + 6],
                                      in_=xa[:, 512 * i:512 * (i + 1)])
                return last
            tok_sw8 = st("vector", f_st, waits=[tok_xa_done], inc="cV")
            tok_sw = st("vector", lambda e: e.bn_aggr(
                out=mvw[:], in_=stat6w[:]), waits=[tok_sw8], inc="cV")
            tok_lnw = st("scalar", lambda e: e.activation(
                out=lnvw[:], in_=mvw[:, 1:2], func=AF.Ln, bias=epsT[:]),
                waits=[tok_sw], inc="cA")
            tok_rsw = st("scalar", lambda e: e.activation(
                out=rstdw[:], in_=lnvw[:], func=AF.Exp, scale=-0.5),
                waits=[tok_lnw], inc="cA")
            tok_nw = st("vector", lambda e, xa=xa: e.tensor_scalar(
                out=lnA[:], in0=xa, scalar1=mvw[:, 0:1], scalar2=rstdw[:],
                op0=ALU.subtract, op1=ALU.mult), waits=[tok_rsw], inc="cV")
            tok_x1 = st("vector", lambda e: e.tensor_tensor(
                out=x1t[:], in0=lnA[:], in1=xo[:], op=ALU.add),
                waits=[tok_nw, tok_xod, ("dX1W", 16 * qt)], inc="cV")
            st("sync", lambda e, qt=qt: e.dma_start(
                out=x1_dram[qt * 128:(qt + 1) * 128, :], in_=x1t[:]),
                waits=[tok_x1], inc="dX1W", inc_by=16)

            def f_st2(e):
                last = None
                for i in range(8):
                    last = e.bn_stats(out=stat6w[:, 6 * i:6 * i + 6],
                                      in_=x1t[:, 512 * i:512 * (i + 1)])
                return last
            tok_sw2a = st("vector", f_st2, waits=[tok_x1], inc="cV")
            tok_sw2 = st("vector", lambda e: e.bn_aggr(
                out=mvw[:], in_=stat6w[:]), waits=[tok_sw2a], inc="cV")
            tok_lnw2 = st("scalar", lambda e: e.activation(
                out=lnvw[:], in_=mvw[:, 1:2], func=AF.Ln, bias=epsT[:]),
                waits=[tok_sw2], inc="cA")
            tok_rsw2 = st("scalar", lambda e: e.activation(
                out=rstdw[:], in_=lnvw[:], func=AF.Exp, scale=-0.5),
                waits=[tok_lnw2], inc="cA")
            tok_nw2 = st("vector", lambda e: e.tensor_scalar(
                out=ln2_tok[:], in0=x1t[:], scalar1=mvw[:, 0:1],
                scalar2=rstdw[:], op0=ALU.subtract, op1=ALU.mult),
                waits=[tok_rsw2], inc="cV")
            tok_x1_lastread[qt] = tok_nw2
            for grp in range(8):
                pp = grp % 2
                w_tp = [tok_ln2cp.get((qt, grp - 2)) if grp >= 2 else
                        (tok_ln2cp.get((qt - 1, 6 + grp)) if qt > 0 else None)]

                def f_t2(e, pp=pp, grp=grp):
                    last = None
                    for i in range(4):
                        dc = grp * 4 + i
                        last = e.transpose(
                            out=tpsW[pp][:, i * 128:(i + 1) * 128],
                            in_=ln2_tok[:, dc * 128:(dc + 1) * 128],
                            identity=cst["identb"][:])
                    return last
                tok_t2 = st("tensor", f_t2, waits=[tok_nw2] + w_tp, inc="cP")

                def f_c2(e, pp=pp, grp=grp, qt=qt):
                    src = tpsW[pp][:, 0:512].rearrange("p (c f) -> p c f", c=4)
                    dst = ln2_fm[:].rearrange("p (d t) -> p d t", d=NDC)[
                        :, grp * 4:grp * 4 + 4, qt * 128:(qt + 1) * 128]
                    return e.tensor_copy(out=dst, in_=src)
                tok_ln2cp[(qt, grp)] = st("vector", f_c2, waits=[tok_t2],
                                          inc="cV")
        tok_ln2_done = tok_ln2cp[(NQT - 1, 7)]
        if debug:
            dump("x1_7", x1t[:], [128, DM], tok_x1_lastread[NQT - 1], F32)
            dump("ln2fm", ln2_fm[:, 0:TOK], [128, TOK], tok_ln2_done, BF16)

        S.q["sync"].append((None, [("dX1W", 16 * NQT)], None, 0, S.phase))
        S.waited["sync"]["dX1W"] = 16 * NQT
        S.barrier()
        wctx.close()

        S.phase = "F1"
        fctx = ExitStack()
        wgt = [sb(f"wgt{i}", [128, NDC * 128], BF16, fctx) for i in range(2)]
        wut = [sb(f"wut{i}", [128, NDC * 128], BF16, fctx) for i in range(2)]
        tmpg = [sb(f"tmpg{i}", [128, TOK], F32, fctx) for i in range(2)]
        tmpu = [sb(f"tmpu{i}", [128, TOK], F32, fctx) for i in range(2)]
        htile = [sb(f"htile{i}", [128, TOK], BF16, fctx) for i in range(2)]
        psg = [ps(f"psg{i}", [128, TOK], F32, fctx) for i in range(2)]
        psu2 = [ps(f"psu2{i}", [128, TOK], F32, fctx) for i in range(2)]
        tok_gev = {}
        tok_uev = {}
        tok_hmul = {}
        tok_f1mm = {}
        for fb in range(NFB):
            sl = fb % 2
            w_w = [tok_f1mm.get(fb - 2)]
            tok_wgd = st("gpsimd", lambda e, fb=fb, sl=sl: e.dma_start(
                out=wgt[sl][:], in_=wg[fb]), waits=w_w, inc=f"dWG{sl}",
                inc_by=16)
            tok_wud = st("gpsimd", lambda e, fb=fb, sl=sl: e.dma_start(
                out=wut[sl][:], in_=wu[fb]), waits=w_w, inc=f"dWU{sl}",
                inc_by=16)
            w_ps = [tok_gev.get(fb - 2), tok_uev.get(fb - 2)]

            def f_mm(e, sl=sl):
                last = None
                for dc in range(NDC):
                    for col in range(2):
                        e.matmul(out=psg[sl][:, col * 512:(col + 1) * 512],
                                 lhsT=wgt[sl][:, dc * 128:(dc + 1) * 128],
                                 rhs=ln2_fm[:, dc * TOK + col * 512:
                                            dc * TOK + (col + 1) * 512],
                                 start=(dc == 0), stop=(dc == NDC - 1))
                        last = e.matmul(
                            out=psu2[sl][:, col * 512:(col + 1) * 512],
                            lhsT=wut[sl][:, dc * 128:(dc + 1) * 128],
                            rhs=ln2_fm[:, dc * TOK + col * 512:
                                       dc * TOK + (col + 1) * 512],
                            start=(dc == 0), stop=(dc == NDC - 1))
                return last
            tok_f1mm[fb] = st("tensor", f_mm,
                              waits=[tok_wgd, tok_wud, tok_ln2_done] + w_ps,
                              inc="cP")
            w_tmp = [tok_hmul.get(fb - 2)]
            tok_g = st("scalar", lambda e, fb=fb, sl=sl: e.activation(
                out=tmpg[sl][:], in_=psg[sl][:], func=AF.Silu,
                scale=cst["gs"][:, fb:fb + 1]),
                waits=[tok_f1mm[fb]] + w_tmp, inc="cA")
            tok_gev[fb] = tok_g
            tok_u = st("vector", lambda e, fb=fb, sl=sl: e.tensor_scalar(
                out=tmpu[sl][:], in0=psu2[sl][:],
                scalar1=cst["us"][:, fb:fb + 1], scalar2=None, op0=ALU.mult),
                waits=[tok_f1mm[fb]] + w_tmp, inc="cV")
            tok_uev[fb] = tok_u
            hsem = f"dHW{sl}"
            w_h = [(hsem, 16 * (fb // 2))] if fb >= 2 else []
            tok_m = st("vector", lambda e, sl=sl: e.tensor_tensor(
                out=htile[sl][:], in0=tmpg[sl][:], in1=tmpu[sl][:],
                op=ALU.mult), waits=[tok_g, tok_u] + w_h, inc="cV")
            tok_hmul[fb] = tok_m
            st("sync", lambda e, fb=fb, sl=sl: e.dma_start(
                out=h_dram[fb], in_=htile[sl][:]), waits=[tok_m], inc=hsem,
                inc_by=16)
        tok_hw_all = [("dHW0", 16 * 43), ("dHW1", 16 * 43)]

        S.q["sync"].append((None, [("dHW0", 16 * 43), ("dHW1", 16 * 43)],
                            None, 0, S.phase))
        S.waited["sync"]["dHW0"] = 16 * 43
        S.waited["sync"]["dHW1"] = 16 * 43
        S.barrier()
        fctx.close()
        wctx4.close()

        S.phase = "F2"
        f2ctx = ExitStack()
        hh = sb("hh", [128, NFB * 512], BF16, f2ctx)
        wdt = [sb(f"wdt{i}", [128, NFB * 128], BF16, f2ctx) for i in range(2)]
        ff_fm = [sb(f"fffm{i}", [128, 512], BF16, f2ctx) for i in range(2)]
        ffn_tok = sb("ffn_tok", [128, 4 * DM], BF16, f2ctx)
        x1r = sb("x1r", [128, DM], F32, f2ctx)
        yt = sb("yt", [128, DM], F32, f2ctx)
        lnF = sb("lnF", [128, DM], BF16, f2ctx)
        stat6f = sb("stat6f", [128, 48], F32, f2ctx)
        mvf = sb("mvf", [128, 2], F32, f2ctx)
        rstdf = sb("rstdf", [128, 1], F32, f2ctx)
        lnvf = sb("lnvf", [128, 1], F32, f2ctx)
        dps = [ps(f"dps{i}", [128, 512], F32, f2ctx) for i in range(4)]
        tpsF = [ps(f"tpsF{i}", [128, 512], BF16, f2ctx) for i in range(2)]

        tok_y = {}
        tok_dmm = {}
        for half in range(2):
            w_hh = list(tok_hw_all)
            if half == 1:
                w_hh.append(tok_dmm[(0, NHC - 1)])

            fb0 = 0
            for n in [11] * 7 + [9]:
                def f_hh(e, half=half, fb0=fb0, n=n):
                    s_ = h_dram[fb0:fb0 + n, :, half * 512:(half + 1) * 512]
                    s_ = s_.rearrange("f p c -> p f c")
                    dst = hh[:].rearrange("p (f c) -> p f c", f=NFB)[
                        :, fb0:fb0 + n, :]
                    return e.dma_start(out=dst, in_=s_)
                tok_hh = st("sync", f_hh, waits=w_hh, inc="dHH", inc_by=16)
                fb0 += n
            tok_dev = {}
            tok_fcp = {}
            for j in range(NHC):
                sl = j % 2
                w_wd = [tok_dmm.get((half, j - 2)) if j >= 2 else
                        (tok_dmm.get((0, NHC - 2 + j)) if half == 1 else None)]
                tok_wdd = st("gpsimd", lambda e, j=j, sl=sl: e.dma_start(
                    out=wdt[sl][:], in_=wd[j]), waits=w_wd, inc=f"dWD{sl}",
                    inc_by=16)
                w_ps = [tok_dev.get(j - 4)]

                def f_dmm(e, j=j, sl=sl):
                    last = None
                    for fc in range(NFB):
                        last = e.matmul(
                            out=dps[j % 4][:],
                            lhsT=wdt[sl][:, fc * 128:(fc + 1) * 128],
                            rhs=hh[:, fc * 512:(fc + 1) * 512],
                            start=(fc == 0), stop=(fc == NFB - 1))
                    return last
                tok_dmm[(half, j)] = st("tensor", f_dmm,
                                        waits=[tok_wdd, ("dHH", 128 * (half + 1))] + w_ps,
                                        inc="cP")
                w_ff = [tok_fcp.get(j - 2)]
                tok_de = st("vector", lambda e, j=j, sl=sl: e.tensor_scalar(
                    out=ff_fm[sl][:], in0=dps[j % 4][:],
                    scalar1=cst["ds"][:, j:j + 1], scalar2=None, op0=ALU.mult),
                    waits=[tok_dmm[(half, j)]] + w_ff, inc="cV")
                tok_dev[j] = tok_de

                def f_tf(e, sl=sl):
                    last = None
                    for tt in range(4):
                        last = e.transpose(
                            out=tpsF[sl][:, tt * 128:(tt + 1) * 128],
                            in_=ff_fm[sl][:, tt * 128:(tt + 1) * 128],
                            identity=cst["identb"][:])
                    return last
                tok_tf = st("tensor", f_tf,
                            waits=[tok_de, tok_fcp.get(j - 2)], inc="cP")

                def f_cf(e, j=j, sl=sl):
                    src = tpsF[sl][:, 0:512].rearrange("p (t c) -> p t c", t=4)
                    dst = ffn_tok[:].rearrange("p (t d) -> p t d", t=4)[
                        :, :, j * 128:(j + 1) * 128]
                    return e.tensor_copy(out=dst, in_=src)
                tok_fcp[j] = st("vector", f_cf, waits=[tok_tf], inc="cV")
            tok_ffn_done = tok_fcp[NHC - 1]

            for tt in range(4):
                qt = half * 4 + tt
                ft = ffn_tok[:, tt * DM:(tt + 1) * DM]
                tok_x1r = st("sync", lambda e, qt=qt: e.dma_start(
                    out=x1r[:], in_=x1_dram[qt * 128:(qt + 1) * 128, :]),
                    waits=[tok_y.get(qt - 1), ("dX1W", 16 * NQT)],
                    inc="dX1R", inc_by=16)

                def f_sf(e, ft=ft):
                    last = None
                    for i in range(8):
                        last = e.bn_stats(out=stat6f[:, 6 * i:6 * i + 6],
                                          in_=ft[:, 512 * i:512 * (i + 1)])
                    return last
                tok_sf8 = st("vector", f_sf, waits=[tok_ffn_done], inc="cV")
                tok_sf = st("vector", lambda e: e.bn_aggr(
                    out=mvf[:], in_=stat6f[:]), waits=[tok_sf8], inc="cV")
                tok_lf = st("scalar", lambda e: e.activation(
                    out=lnvf[:], in_=mvf[:, 1:2], func=AF.Ln, bias=epsT[:]),
                    waits=[tok_sf], inc="cA")
                tok_rf = st("scalar", lambda e: e.activation(
                    out=rstdf[:], in_=lnvf[:], func=AF.Exp, scale=-0.5),
                    waits=[tok_lf], inc="cA")
                tok_nf = st("vector", lambda e, ft=ft: e.tensor_scalar(
                    out=lnF[:], in0=ft, scalar1=mvf[:, 0:1], scalar2=rstdf[:],
                    op0=ALU.subtract, op1=ALU.mult), waits=[tok_rf], inc="cV")
                tok_yt = st("vector", lambda e: e.tensor_tensor(
                    out=yt[:], in0=lnF[:], in1=x1r[:], op=ALU.add),
                    waits=[tok_nf, tok_x1r, ("dY", 16 * qt)], inc="cV")
                tok_y[qt] = st("sync", lambda e, qt=qt: e.dma_start(
                    out=y[qt * 128:(qt + 1) * 128, :], in_=yt[:]),
                    waits=[tok_yt], inc="dY", inc_by=16)

        S.q["sync"].append((None, [("dY", 16 * NQT)], None, 0, S.phase))
        if debug and dbg:
            S.q["sync"].append((None, [("dDBG", S.sem_n["dDBG"])], None, 0,
                                S.phase))

        block = octx.enter_context(nc.Block())
        S.build(block)
        f2ctx.close()
    return nc, dbg


_NC_CACHE = {}


def get_nc(debug=False):
    if debug not in _NC_CACHE:
        _NC_CACHE[debug] = build_nc(debug)
    return _NC_CACHE[debug]


def prep_inputs(L0_attn_hidden_size, L0_attn_Wq_c, L0_attn_bq_c, L0_attn_Wq_up,
                L0_attn_bq_up, L0_attn_Wkv_c, L0_attn_bkv_c, L0_attn_Wkv_up,
                L0_attn_bkv_up, L0_attn_Wo, L0_attn_bo, L0_ffn_gate_w,
                L0_ffn_gate_s, L0_ffn_up_w, L0_ffn_up_s, L0_ffn_down_w,
                L0_ffn_down_s):
    x = np.asarray(L0_attn_hidden_size, np.float32)

    def colsT(v, n):
        return np.ascontiguousarray(np.asarray(v, np.float32).reshape(n, 128).T)

    perm_q = np.concatenate(
        [(np.arange(32)[:, None] * 192 + np.arange(128)[None, :]).ravel(),
         (np.arange(32)[:, None] * 192 + 128 + np.arange(64)[None, :]).ravel()])
    perm_kv = np.concatenate(
        [(np.arange(32)[:, None] * 256 + np.arange(128)[None, :]).ravel(),
         (np.arange(32)[:, None] * 256 + 128 + np.arange(128)[None, :]).ravel()])

    wqc_T = np.asarray(L0_attn_Wq_c, np.float32).T
    wqc_t = np.ascontiguousarray(
        wqc_T.reshape(DM, 4, 384).transpose(1, 0, 2))
    wqup_T = np.asarray(L0_attn_Wq_up, np.float32)[perm_q].T
    wqup_t = np.ascontiguousarray(
        wqup_T.reshape(QL, 12, 512).transpose(1, 0, 2))
    wkvc_t = np.ascontiguousarray(np.asarray(L0_attn_Wkv_c, np.float32).T)
    wkvup_t = np.ascontiguousarray(
        np.asarray(L0_attn_Wkv_up, np.float32)[perm_kv].T)
    wo_t = np.ascontiguousarray(np.asarray(L0_attn_Wo, np.float32).T)

    bqc = colsT(L0_attn_bq_c, 12)
    bqup = colsT(np.asarray(L0_attn_bq_up, np.float32)[perm_q], 48)
    bkvc_pad = np.zeros(640, np.float32)
    bkvc_pad[:576] = np.asarray(L0_attn_bkv_c, np.float32)
    bkvc = colsT(bkvc_pad, 5)
    bkvup_p = np.asarray(L0_attn_bkv_up, np.float32)[perm_kv]
    bkvupk = colsT(bkvup_p[:4096], 32)
    bkvupv = np.ascontiguousarray(bkvup_p[4096:].reshape(1, 4096))
    bo = colsT(L0_attn_bo, 32)
    gs = colsT(L0_ffn_gate_s, 86)
    us = colsT(L0_ffn_up_s, 86)
    ds = colsT(L0_ffn_down_s, 32)

    gw = np.asarray(L0_ffn_gate_w, np.int32)
    uw = np.asarray(L0_ffn_up_w, np.int32)
    dw = np.asarray(L0_ffn_down_w, np.int32)

    def tile_gu(w):
        wt = np.ascontiguousarray(w.T)
        v = wt.reshape(NDC, 128, NFB, 128)
        return np.ascontiguousarray(
            v.transpose(2, 1, 0, 3).reshape(NFB, 128, NDC * 128))
    wg_t = tile_gu(gw)
    wu_t = tile_gu(uw)
    dwt = np.ascontiguousarray(dw.T)
    v = dwt.reshape(NFB, 128, NHC, 128)
    wd_t = np.ascontiguousarray(
        v.transpose(2, 1, 0, 3).reshape(NHC, 128, NFB * 128))

    identb = np.eye(128).astype(ml_dtypes.bfloat16)
    identf = np.eye(128, dtype=np.float32)

    shared = dict(wqc=wqc_t, wqup=wqup_t, wkvc=wkvc_t, wkvup=wkvup_t, wo=wo_t,
                  bqc=bqc, bqup=bqup, bkvc=bkvc, bkvupk=bkvupk, bkvupv=bkvupv,
                  bo=bo, gs=gs, us=us, ds=ds, identb=identb, identf=identf,
                  wg=wg_t, wu=wu_t, wd=wd_t)
    in_maps = []
    for c in range(8):
        b, hf = c // 2, c % 2
        xs = np.concatenate([x[b, hf * 1024:(hf + 1) * 1024],
                             x[b, (1 - hf) * 1024:(2 - hf) * 1024]], axis=0)
        m = dict(shared)
        m["x_seq"] = np.ascontiguousarray(xs)
        in_maps.append(m)
    return in_maps


def kernel(**inputs):
    in_maps = prep_inputs(**inputs)
    nc, _ = get_nc(debug=False)
    res = run_bass_kernel_spmd(nc, in_maps, core_ids=list(range(8)))
    out = np.empty((4, 2048, 4096), np.float32)
    for c in range(8):
        b, hf = c // 2, c % 2
        out[b, hf * 1024:(hf + 1) * 1024] = res.results[c]["y"]
    return out
